# version 09
# baseline (speedup 1.0000x reference)
import numpy as np
import ml_dtypes
import concourse.bass as bass
import concourse.mybir as mybir
from concourse.bass_utils import run_bass_kernel_spmd
from contextlib import ExitStack

F32 = mybir.dt.float32
BF16 = mybir.dt.bfloat16
U8 = mybir.dt.uint8
AF = mybir.ActivationFunctionType
ALU = mybir.AluOpType
AX = mybir.AxisListType
BFNP = ml_dtypes.bfloat16

P = 128
S, D = 4096, 4096
NTB = S // P
TSL = 1024
NTBL = TSL // P
NE = 3277
EPS = 1e-5
NCH = D // P


class Ctr:
    def __init__(self, h):
        self.h = h
        self.v = 0

    def inc(self, instr, n=1):
        self.v += n
        instr.then_inc(self.h, n)

    def wait(self, eng, v=None):
        eng.wait_ge(self.h, max(0, self.v if v is None else v))


def build_nc(upto=7, taps=()):
    nc = bass.Bass("TRN2", target_bir_lowering=False)

    x_d = nc.dram_tensor("x", [S, D], F32, kind="ExternalInput").ap()
    xres_d = nc.dram_tensor("x_res", [TSL, D], F32, kind="ExternalInput").ap()
    wq_d = nc.dram_tensor("wq_sl", [D, 1024], F32, kind="ExternalInput").ap()
    wk_d = nc.dram_tensor("wk_sl", [D, 1024], F32, kind="ExternalInput").ap()
    wv_d = nc.dram_tensor("wv_sl", [D, 1024], F32, kind="ExternalInput").ap()
    wo_d = nc.dram_tensor("woT", [D, D], F32, kind="ExternalInput").ap()
    km_d = [nc.dram_tensor(f"k{i}", [D, D], U8, kind="ExternalInput").ap() for i in range(3)]
    rope_d = nc.dram_tensor("rope", [S, 128], F32, kind="ExternalInput").ap()
    svec_d = nc.dram_tensor("svec", [P, 3], F32, kind="ExternalInput").ap()
    out_d = nc.dram_tensor("out", [TSL, D], F32, kind="ExternalOutput").ap()

    xn_dram = nc.dram_tensor("xn_dram", [NTB, P, D], BF16, kind="Internal").ap()
    attn_myr = [nc.dram_tensor(f"attn_my{r}", [S, 512], BF16, kind="Internal").ap()
                for r in range(2)]
    attn_gr = [nc.dram_tensor(f"attn_g{r}", [4 * S, 512], BF16, kind="Internal").ap()
               for r in range(2)]
    x2_dram = nc.dram_tensor("x2_dram", [TSL, D], F32, kind="Internal").ap()
    wkbf1 = nc.dram_tensor("wkbf1", [D, 512], BF16, kind="Internal").ap()
    wvbf1 = nc.dram_tensor("wvbf1", [D, 512], BF16, kind="Internal").ap()
    wqbf1 = nc.dram_tensor("wqbf1", [D, 512], BF16, kind="Internal").ap()
    wobf = nc.dram_tensor("wobf", [D, D], BF16, kind="Internal").ap()

    tap_outs = {}
    for t in taps:
        shp = {"xn": [NTB * P, D], "kst": [P, 4 * S], "vst": [P, NTB * 512],
               "attn_my": [S, 1024], "attn_g": [4 * S, 1024], "x2": [TSL, D],
               "x2nT": [P, NTBL * D], "h1T": [P, NTBL * D], "h2T": [P, NTBL * D]}[t]
        tap_outs[t] = nc.dram_tensor("tap_" + t, shp, F32, kind="ExternalOutput").ap()

    with ExitStack() as ctx:
        en = ctx.enter_context

        ident = en(nc.sbuf_tensor("ident", [P, P], BF16))
        rs_mix = en(nc.sbuf_tensor("rs_mix", [P, 1], F32))
        svec = en(nc.sbuf_tensor("svec_sb", [P, 3], F32))
        sderiv = en(nc.sbuf_tensor("sderiv", [P, 8], F32))
        st_sum = en(nc.sbuf_tensor("st_sum", [P, 1], F32))
        st_ssq = en(nc.sbuf_tensor("st_ssq", [P, 1], F32))
        st_mean = en(nc.sbuf_tensor("st_mean", [P, 1], F32))
        st_m2 = en(nc.sbuf_tensor("st_m2", [P, 1], F32))
        st_var = en(nc.sbuf_tensor("st_var", [P, 1], F32))
        st_rstd = en(nc.sbuf_tensor("st_rstd", [P, 1], F32))
        st_negmr = en(nc.sbuf_tensor("st_negmr", [P, 1], F32))
        c_eps = en(nc.sbuf_tensor("c_eps", [P, 1], F32))
        c_mhalf = en(nc.sbuf_tensor("c_mhalf", [P, 1], F32))
        c_inv16 = en(nc.sbuf_tensor("c_inv16", [P, 1], F32))

        Sm = {n: Ctr(en(nc.semaphore(n))) for n in [
            "ld_x", "ld_w", "ld_blob", "ld_aux", "st_blob",
            "se_sq", "dv_var", "se_rstd", "dv_negmr", "se_xn",
            "te_tr", "dv_evac", "te_kv", "se_kv", "dv_kbf", "te_ktr", "dv_ktr",
            "te_q", "se_q", "dv_q", "te_qtr", "dv_qfm",
            "te_s", "dv_sevac", "dv_negm", "se_exp", "dv_l",
            "te_pt", "dv_ptevac", "te_pv", "dv_o", "st_attn",
            "cc", "ld_at", "te_at", "dv_at", "ld_wo", "te_o", "dv_x2", "st_x2",
            "ld_x2r", "ld_msk", "dv_wfix", "te_ff", "se_ff", "dv_ff", "st_out",
            "ld_xc", "dv_rs", "st_tap", "te_oh", "te_fh", "dv_fl", "se_fl", "gp_fl", "ld_at2", "cast_w",
        ]}

        with nc.Block() as blk:
            @blk.gpsimd
            def _(g):
                g.memset(ident[:], 0.0)
                g.affine_select(out=ident[:], in_=ident[:], compare_op=ALU.not_equal,
                                fill=1.0, base=0, pattern=[[-1, P]], channel_multiplier=1)
                g.memset(rs_mix[:], 1.0)
                g.affine_select(out=rs_mix[:], in_=rs_mix[:], compare_op=ALU.is_ge,
                                fill=-4.0, base=76, pattern=[[0, 1]], channel_multiplier=-1)
                g.memset(c_eps[:], EPS)
                g.memset(c_mhalf[:], -0.5)
                g.memset(c_inv16[:], 1.0 / 16.0)
                Sm["ld_aux"].inc(g.dma_start(out=svec[:], in_=svec_d), 16)

        with nc.Block() as blk:
            @blk.vector
            def _(v):
                Sm["ld_aux"].wait(v)
                for col, (src, sc) in enumerate([(0, 4.0), (1, 4.0), (2, 2.0),
                                                 (0, -2.0), (1, -2.0), (2, -1.0)]):
                    v.tensor_scalar(out=sderiv[:, col:col + 1], in0=svec[:, src:src + 1],
                                    scalar1=sc, scalar2=None, op0=ALU.mult)

        with ExitStack() as actx:
            aen = actx.enter_context
            rope_sb = aen(nc.sbuf_tensor("rope_sb", [P, NTB, 128], F32))
            Kst = aen(nc.sbuf_tensor("Kst", [P, 4, S], BF16))
            Vst = aen(nc.sbuf_tensor("Vst", [P, NTB, 512], BF16))

            def emit_kv_phase(rnd, first):
                with ExitStack() as pctx:
                    pen = pctx.enter_context
                    wk_sb = pen(nc.sbuf_tensor(f"wk_sb{rnd}", [P, NCH, 512], BF16))
                    wv_sb = pen(nc.sbuf_tensor(f"wv_sb{rnd}", [P, NCH, 512], BF16))
                    xn_dm = pen(nc.sbuf_tensor(f"xn_dm{rnd}", [P, D], BF16))
                    k_tm = pen(nc.sbuf_tensor(f"k_tm{rnd}", [P, 512], F32))
                    k_bf = pen(nc.sbuf_tensor(f"k_bf{rnd}", [P, 512], BF16))
                    k_rsc = pen(nc.sbuf_tensor(f"k_rsc{rnd}", [P, 128], F32))
                    pK = pen(nc.psum_tensor(f"pK{rnd}", [P, 512], F32))
                    pV = pen(nc.psum_tensor(f"pV{rnd}", [P, 512], F32))
                    p_ktr = pen(nc.psum_tensor(f"p_ktr{rnd}", [P, 1024], BF16))
                    if first:
                        x_tile = pen(nc.sbuf_tensor("x_tile", [P, 2, D], F32))
                        xn_tm = pen(nc.sbuf_tensor("xn_tm", [P, D], BF16))
                        p_tr = pen(nc.psum_tensor("p_tr", [P, 2, 1024], BF16))
                    else:
                        xn_blob3 = pen(nc.sbuf_tensor("xn_blob3", [P, 4, D], BF16))

                    ld_x_b = Sm["ld_x"].v
                    ld_blob_b = Sm["ld_blob"].v
                    kv_b = Sm["te_kv"].v
                    sekv_b = Sm["se_kv"].v
                    evac_b = Sm["dv_evac"].v
                    tr_b = Sm["te_tr"].v
                    blob_b = Sm["st_blob"].v
                    ktr_b = Sm["te_ktr"].v
                    dkt_b = Sm["dv_ktr"].v
                    kbf_b = Sm["dv_kbf"].v
                    var_b = Sm["dv_var"].v
                    rstd_b = Sm["se_rstd"].v
                    xn_b = Sm["se_xn"].v

                    with nc.Block() as blk:
                        @blk.gpsimd
                        def _(g):
                            if first:
                                Sm["ld_w"].inc(g.dma_start(
                                    out=wk_sb[:],
                                    in_=wk_d.rearrange("(c p) n -> p c n", p=P)[:, :, 0:512]), 16)
                                Sm["ld_w"].inc(g.dma_start(
                                    out=wv_sb[:],
                                    in_=wv_d.rearrange("(c p) n -> p c n", p=P)[:, :, 0:512]), 16)
                                if upto >= 3:
                                    Sm["cast_w"].inc(g.dma_start(
                                        out=wkbf1[:], in_=wk_d[:, 512:1024]), 16)
                                    Sm["cast_w"].inc(g.dma_start(
                                        out=wvbf1[:], in_=wv_d[:, 512:1024]), 16)
                                if upto >= 4:
                                    Sm["cast_w"].inc(g.dma_start(
                                        out=wqbf1[:], in_=wq_d[:, 512:1024]), 16)
                            else:
                                g.wait_ge(Sm["cast_w"].h, 32)
                                Sm["ld_w"].inc(g.dma_start(
                                    out=wk_sb[:],
                                    in_=wkbf1.rearrange("(c p) n -> p c n", p=P)), 16)
                                Sm["ld_w"].inc(g.dma_start(
                                    out=wv_sb[:],
                                    in_=wvbf1.rearrange("(c p) n -> p c n", p=P)), 16)

                        @blk.sync
                        def _(sync):
                            if first:
                                Sm["ld_aux"].inc(sync.dma_start(
                                    out=rope_sb[:],
                                    in_=rope_d.rearrange("(tb p) f -> p tb f", p=P)), 16)
                                Sm["ld_x"].inc(sync.dma_start(
                                    out=x_tile[:, 0, :], in_=x_d[0:P, :]), 16)
                            else:
                                Sm["ld_blob"].inc(sync.dma_start(
                                    out=xn_blob3[:, 0, :], in_=xn_dram[0]), 16)

                    ld_w_t = Sm["ld_w"].v

                    def kv_rotary_dv(v, tb):
                        v.wait_ge(Sm["se_kv"].h, sekv_b + 2 * tb + 1)
                        cos = rope_sb[:, tb, 0:64]
                        sin = rope_sb[:, tb, 64:128]
                        for h in (0, 1):
                            kr = k_tm[:, 256 * h:256 * h + 64]
                            t1 = k_rsc[:, 0:64]
                            t2 = k_rsc[:, 64:128]
                            v.tensor_mul(t1, kr, cos)
                            Sm["dv_fl"].inc(v.tensor_mul(t2, kr, sin))
                            Sm["dv_fl"].wait(v)
                            ev = lambda a: a.rearrange("p (i two) -> p i two", two=2)[:, :, 0]
                            od = lambda a: a.rearrange("p (i two) -> p i two", two=2)[:, :, 1]
                            v.tensor_sub(ev(kr), ev(t1), od(t2))
                            Sm["dv_fl"].inc(v.tensor_add(od(kr), od(t1), ev(t2)))
                            Sm["dv_fl"].wait(v)
                        v.wait_ge(Sm["te_ktr"].h, max(0, ktr_b + tb))
                        Sm["dv_kbf"].inc(v.tensor_copy(k_bf[:], k_tm[:]))

                    def kv_kst_evac(v, tb):
                        v.wait_ge(Sm["te_ktr"].h, ktr_b + tb + 1)
                        Sm["dv_ktr"].inc(v.tensor_copy(
                            Kst[:, :, P * tb:P * (tb + 1)],
                            p_ktr.rearrange("p (u q) -> p u q", q=P)[:, 0:4, :]))

                    def kv_epi_scalar(s, tb):
                        s.wait_ge(Sm["te_kv"].h, kv_b + tb + 1)
                        s.wait_ge(Sm["dv_kbf"].h, max(0, kbf_b + tb))
                        Sm["se_kv"].inc(s.copy(k_tm[:], pK[:]))
                        Sm["se_kv"].inc(s.activation(
                            out=Vst[:, tb, :], in_=pV[:], func=AF.Copy))

                    def kv_ktr(t, tb):
                        t.wait_ge(Sm["dv_kbf"].h, kbf_b + tb + 1)
                        t.wait_ge(Sm["dv_ktr"].h, max(0, dkt_b + tb))
                        for u in range(4):
                            ti = t.transpose(p_ktr[:, 128 * u:128 * (u + 1)],
                                             k_bf[:, 128 * u:128 * (u + 1)], ident[:])
                            if u == 3:
                                Sm["te_ktr"].inc(ti)

                    with nc.Block() as blk:
                        @blk.sync
                        def _(sync):
                            for tb in range(1, NTB):
                                if first:
                                    sync.wait_ge(Sm["se_xn"].h, max(0, xn_b + tb - 1))
                                    Sm["ld_x"].inc(sync.dma_start(
                                        out=x_tile[:, tb % 2, :],
                                        in_=x_d[P * tb:P * (tb + 1), :]), 16)
                                else:
                                    sync.wait_ge(Sm["te_kv"].h, max(0, kv_b + tb - 3))
                                    Sm["ld_blob"].inc(sync.dma_start(
                                        out=xn_blob3[:, tb % 4, :],
                                        in_=xn_dram[tb]), 16)

                        if first:
                            def ln_chain(v, tb):
                                xt = x_tile[:, tb % 2, :]
                                v.wait_ge(Sm["ld_x"].h, ld_x_b + 16 * (tb + 1))
                                if tb >= 1:
                                    v.wait_ge(Sm["te_tr"].h, tr_b + 8 * tb)
                                    v.wait_ge(Sm["se_xn"].h, xn_b + tb)
                                Sm["dv_fl"].inc(v.reduce_sum(st_sum[:], xt, axis=AX.X))
                                Sm["dv_fl"].inc(v.scalar_tensor_tensor(
                                    out=xn_tm[:], in0=xt, scalar=1.0, in1=xt,
                                    op0=ALU.mult, op1=ALU.mult, accum_out=st_ssq[:]))
                                Sm["dv_fl"].wait(v)
                                Sm["dv_fl"].inc(v.tensor_scalar(
                                    out=st_mean[:], in0=st_sum[:],
                                    scalar1=-1.0 / D, scalar2=None, op0=ALU.mult))
                                Sm["dv_fl"].wait(v)
                                Sm["dv_fl"].inc(v.tensor_mul(st_m2[:], st_mean[:], st_mean[:]))
                                Sm["dv_fl"].wait(v)
                                Sm["dv_var"].inc(v.scalar_tensor_tensor(
                                    out=st_var[:], in0=st_ssq[:], scalar=1.0 / D,
                                    in1=st_m2[:], op0=ALU.mult, op1=ALU.subtract))

                            @blk.vector
                            def _(v):
                                Sm["ld_aux"].wait(v)
                                ln_chain(v, 0)
                                for tb in range(NTB):
                                    for j in range(8):
                                        v.wait_ge(Sm["te_tr"].h, tr_b + 8 * tb + j + 1)
                                        if j == 0:
                                            v.wait_ge(Sm["te_kv"].h, max(0, kv_b + tb))
                                            v.wait_ge(Sm["st_blob"].h, max(0, blob_b + 16 * tb))
                                        Sm["dv_evac"].inc(v.tensor_copy(
                                            xn_dm[:, 512 * j:512 * (j + 1)],
                                            p_tr[:, j % 2, 0:512]))
                                    if tb + 1 < NTB:
                                        ln_chain(v, tb + 1)
                                    kv_rotary_dv(v, tb)
                                    kv_kst_evac(v, tb)

                            @blk.scalar
                            def _(s):
                                for tb in range(NTB):
                                    xt = x_tile[:, tb % 2, :]
                                    s.wait_ge(Sm["dv_var"].h, var_b + tb + 1)
                                    Sm["se_fl"].inc(s.activation(
                                        out=st_rstd[:], in_=st_var[:],
                                        func=AF.Ln, bias=c_eps[:]))
                                    Sm["se_fl"].wait(s)
                                    Sm["se_rstd"].inc(s.activation(
                                        out=st_rstd[:], in_=st_rstd[:], func=AF.Exp,
                                        scale=c_mhalf[:]))
                                    Sm["se_fl"].inc(s.activation(
                                        out=xt, in_=xt, func=AF.Identity, bias=st_mean[:]))
                                    Sm["se_rstd"].wait(s)
                                    Sm["se_fl"].wait(s)
                                    Sm["se_xn"].inc(s.activation(
                                        out=xn_tm[:], in_=xt, func=AF.Identity,
                                        scale=st_rstd[:]))
                                    kv_epi_scalar(s, tb)
                                    s.wait_ge(Sm["dv_evac"].h, evac_b + 8 * (tb + 1))
                                    Sm["st_blob"].inc(s.dma_start(
                                        out=xn_dram[tb], in_=xn_dm[:]), 16)

                            @blk.tensor
                            def _(t):
                                for tb in range(NTB):
                                    t.wait_ge(Sm["se_xn"].h, xn_b + tb + 1)

                                    def tgroup(j):
                                        t.wait_ge(Sm["dv_evac"].h, max(0, evac_b + 8 * tb + j - 1))
                                        for u in range(4):
                                            c = 4 * j + u
                                            ti = t.transpose(
                                                p_tr[:, j % 2, 128 * u:128 * (u + 1)],
                                                xn_tm[:, 128 * c:128 * (c + 1)], ident[:])
                                            if u == 3:
                                                Sm["te_tr"].inc(ti)

                                    def mmgroup(j):
                                        t.wait_ge(Sm["dv_evac"].h, evac_b + 8 * tb + j + 1)
                                        if j == 0:
                                            t.wait_ge(Sm["ld_w"].h, ld_w_t)
                                            t.wait_ge(Sm["se_kv"].h, max(0, sekv_b + 2 * tb))
                                        for u in range(4):
                                            c = 4 * j + u
                                            t.matmul(pK[:], xn_dm[:, 128 * c:128 * (c + 1)],
                                                     wk_sb[:, c, :], start=(c == 0), stop=(c == NCH - 1))
                                            mi = t.matmul(pV[:], xn_dm[:, 128 * c:128 * (c + 1)],
                                                          wv_sb[:, c, :], start=(c == 0), stop=(c == NCH - 1))
                                            if c == NCH - 1:
                                                Sm["te_kv"].inc(mi)

                                    tgroup(0)
                                    tgroup(1)
                                    if tb >= 1:
                                        kv_ktr(t, tb - 1)
                                    for j in range(8):
                                        mmgroup(j)
                                        if j + 2 < 8:
                                            tgroup(j + 2)
                                    if tb == NTB - 1:
                                        kv_ktr(t, tb)
                        else:
                            @blk.tensor
                            def _(t):
                                for tb in range(NTB):
                                    src_dm = xn_blob3[:, tb % 4, :]
                                    t.wait_ge(Sm["ld_blob"].h, ld_blob_b + 16 * (tb + 1))
                                    if tb == 0:
                                        t.wait_ge(Sm["ld_w"].h, ld_w_t)
                                    t.wait_ge(Sm["se_kv"].h, max(0, sekv_b + 2 * tb))
                                    if tb >= 1:
                                        kv_ktr(t, tb - 1)
                                    for c in range(NCH):
                                        t.matmul(pK[:], src_dm[:, 128 * c:128 * (c + 1)],
                                                 wk_sb[:, c, :], start=(c == 0), stop=(c == NCH - 1))
                                        mi = t.matmul(pV[:], src_dm[:, 128 * c:128 * (c + 1)],
                                                      wv_sb[:, c, :], start=(c == 0), stop=(c == NCH - 1))
                                        if c == NCH - 1:
                                            Sm["te_kv"].inc(mi)
                                    if tb == NTB - 1:
                                        kv_ktr(t, tb)

                            @blk.scalar
                            def _(s):
                                for tb in range(NTB):
                                    kv_epi_scalar(s, tb)

                            @blk.vector
                            def _(v):
                                for tb in range(NTB):
                                    kv_rotary_dv(v, tb)
                                    kv_kst_evac(v, tb)

                    if first:
                        with nc.Block() as blk:
                            @blk.scalar
                            def _(s):
                                Sm["st_blob"].wait(s)

            def emit_flash_phase(rnd):
                with ExitStack() as pctx:
                    pen = pctx.enter_context
                    wq_sb = pen(nc.sbuf_tensor(f"wq_sb{rnd}", [P, NCH, 512], BF16))
                    xn_blob = pen(nc.sbuf_tensor(f"xn_blob{rnd}", [P, 3, D], BF16))
                    q_tm = pen(nc.sbuf_tensor(f"q_tm{rnd}", [P, 512], F32))
                    q_bf = pen(nc.sbuf_tensor(f"q_bf{rnd}", [P, 512], BF16))
                    q_fm = pen(nc.sbuf_tensor(f"q_fm{rnd}", [P, 4, P], BF16))
                    q_rsc = pen(nc.sbuf_tensor(f"q_rsc{rnd}", [P, 128], F32))
                    S_row = pen(nc.sbuf_tensor(f"S_row{rnd}", [P, S], F32))
                    P_row2 = pen(nc.sbuf_tensor(f"P_row{rnd}", [P, 2, S], BF16))
                    Pt_sb = pen(nc.sbuf_tensor(f"Pt_sb{rnd}", [P, 2, 512], BF16))
                    dmask = pen(nc.sbuf_tensor(f"dmask{rnd}", [P, 4, 512], BF16))
                    mxp = pen(nc.sbuf_tensor(f"mxp{rnd}", [P, 8], F32))
                    lp = pen(nc.sbuf_tensor(f"lp{rnd}", [P, 8], F32))
                    negm2 = pen(nc.sbuf_tensor(f"negm{rnd}", [P, 2], F32))
                    lsum = pen(nc.sbuf_tensor(f"lsum{rnd}", [P, 1], F32))
                    rtmp7 = pen(nc.sbuf_tensor(f"rtmp7_{rnd}", [P, 1], F32))
                    linv = pen(nc.sbuf_tensor(f"linv{rnd}", [P, 1], F32))
                    astage = pen(nc.sbuf_tensor(f"astage{rnd}", [P, 2, 512], BF16))
                    pQ = pen(nc.psum_tensor(f"pQ{rnd}", [P, 512], F32))
                    pS0 = pen(nc.psum_tensor(f"pS0{rnd}", [P, 512], F32))
                    pS1 = pen(nc.psum_tensor(f"pS1{rnd}", [P, 512], F32))
                    p_qtr = pen(nc.psum_tensor(f"p_qtr{rnd}", [P, 1024], BF16))
                    p_pt0 = pen(nc.psum_tensor(f"p_pt0{rnd}", [P, 1024], BF16))
                    p_pt1 = pen(nc.psum_tensor(f"p_pt1{rnd}", [P, 1024], BF16))
                    pO0 = pen(nc.psum_tensor(f"pO0{rnd}", [P, 512], F32))
                    pO1 = pen(nc.psum_tensor(f"pO1{rnd}", [P, 512], F32))
                    pS = [pS0, pS1]
                    p_pt = [p_pt0, p_pt1]
                    pO = [pO0, pO1]

                    Bq = Sm["te_q"].v
                    Bseq = Sm["se_q"].v
                    Bdvq = Sm["dv_q"].v
                    Bqtr = Sm["te_qtr"].v
                    Bqfm = Sm["dv_qfm"].v
                    Bnegm = Sm["dv_negm"].v
                    Bexp = Sm["se_exp"].v
                    Bl = Sm["dv_l"].v
                    Bo = Sm["dv_o"].v
                    Battn = Sm["st_attn"].v
                    Bblob = Sm["ld_blob"].v
                    NKT = [qb // 4 + 1 for qb in range(NTB)]
                    TES = {}
                    cum = 0
                    for qb in range(NTB):
                        for h in (0, 1):
                            TES[(qb, h)] = cum
                            cum += NKT[qb]
                    TES_END = cum
                    Bs = Sm["te_s"].v
                    Bsev = Sm["dv_sevac"].v
                    Bpt = Sm["te_pt"].v
                    Bptv = Sm["dv_ptevac"].v
                    Bpv = Sm["te_pv"].v
                    linv_h2 = [linv[:], lsum[:]]

                    with nc.Block() as blk:
                        @blk.gpsimd
                        def _(g):
                            for a in range(4):
                                g.memset(dmask[:, a, :], 1.0)
                                g.affine_select(out=dmask[:, a, :], in_=dmask[:, a, :],
                                                compare_op=ALU.is_ge, fill=0.0, base=128 * a,
                                                pattern=[[-1, 512]], channel_multiplier=1)
                            if rnd == 0:
                                Sm["ld_w"].inc(g.dma_start(
                                    out=wq_sb[:],
                                    in_=wq_d.rearrange("(c p) n -> p c n", p=P)[:, :, 0:512]), 16)
                                if upto >= 6:
                                    for c4 in range(4):
                                        Sm["cast_w"].inc(g.dma_start(
                                            out=wobf[:, 1024 * c4:1024 * (c4 + 1)],
                                            in_=wo_d[:, 1024 * c4:1024 * (c4 + 1)]), 16)
                            else:
                                g.wait_ge(Sm["cast_w"].h, 48)
                                Sm["ld_w"].inc(g.dma_start(
                                    out=wq_sb[:],
                                    in_=wqbf1.rearrange("(c p) n -> p c n", p=P)), 16)

                        @blk.sync
                        def _(sync):
                            Sm["ld_blob"].inc(sync.dma_start(
                                out=xn_blob[:, 0, :], in_=xn_dram[0]), 16)

                    ld_w_t = Sm["ld_w"].v

                    with nc.Block() as blk:
                        @blk.sync
                        def _(sync):
                            for qb in range(1, NTB):
                                sync.wait_ge(Sm["te_q"].h, max(0, Bq + qb - 2))
                                Sm["ld_blob"].inc(sync.dma_start(
                                    out=xn_blob[:, qb % 3, :],
                                    in_=xn_dram[qb]), 16)

                        @blk.tensor
                        def _(t):
                            def ptpv(qb, h):
                                nkt = NKT[qb]
                                t.wait_ge(Sm["dv_l"].h, Bl + 2 * qb + h + 1)
                                t.wait_ge(Sm["dv_o"].h, max(0, Bo + 2 * qb + h - 1))
                                t.wait_ge(Sm["dv_ptevac"].h, Bptv + TES[(qb, h)])

                                def ptr_group(kt):
                                    if kt >= 2:
                                        t.wait_ge(Sm["dv_ptevac"].h, Bptv + TES[(qb, h)] + kt - 1)
                                    for u in range(4):
                                        ti = t.transpose(
                                            p_pt[kt % 2][:, 128 * u:128 * (u + 1)],
                                            P_row2[:, h, 512 * kt + 128 * u:512 * kt + 128 * (u + 1)],
                                            ident[:])
                                        if u == 3:
                                            Sm["te_pt"].inc(ti)

                                def pv_group(kt):
                                    t.wait_ge(Sm["dv_ptevac"].h, Bptv + TES[(qb, h)] + kt + 1)
                                    for u in range(4):
                                        mi = t.matmul(
                                            pO[h][:, 0:256],
                                            Pt_sb[:, kt % 2, 128 * u:128 * (u + 1)],
                                            Vst[:, 4 * kt + u, 256 * h:256 * (h + 1)],
                                            start=(kt == 0 and u == 0),
                                            stop=(kt == nkt - 1 and u == 3))
                                    Sm["te_pv"].inc(mi)

                                ptr_group(0)
                                if nkt > 1:
                                    ptr_group(1)
                                for kt in range(nkt):
                                    pv_group(kt)
                                    if kt + 2 < nkt:
                                        ptr_group(kt + 2)

                            for qb in range(NTB):
                                nkt = NKT[qb]
                                t.wait_ge(Sm["ld_blob"].h, Bblob + 16 * (qb + 1))
                                if qb == 0:
                                    t.wait_ge(Sm["ld_w"].h, ld_w_t)
                                t.wait_ge(Sm["se_q"].h, max(0, Bseq + qb))
                                xb = xn_blob[:, qb % 3, :]
                                for c in range(NCH):
                                    mi = t.matmul(pQ[:], xb[:, 128 * c:128 * (c + 1)],
                                                  wq_sb[:, c, :], start=(c == 0), stop=(c == NCH - 1))
                                Sm["te_q"].inc(mi)
                                t.wait_ge(Sm["dv_q"].h, Bdvq + qb + 1)
                                t.wait_ge(Sm["dv_qfm"].h, max(0, Bqfm + qb))
                                for u in range(4):
                                    ti = t.transpose(p_qtr[:, 128 * u:128 * (u + 1)],
                                                     q_bf[:, 128 * u:128 * (u + 1)], ident[:])
                                    if u == 3:
                                        Sm["te_qtr"].inc(ti)
                                t.wait_ge(Sm["dv_qfm"].h, Bqfm + qb + 1)
                                for h in (0, 1):
                                    t.wait_ge(Sm["dv_sevac"].h, Bsev + TES[(qb, h)])
                                    for kt in range(nkt):
                                        if kt >= 2:
                                            t.wait_ge(Sm["dv_sevac"].h, Bsev + TES[(qb, h)] + kt - 1)
                                        bank = pS[kt % 2]
                                        t.matmul(bank[:], q_fm[:, 2 * h, :],
                                                 Kst[:, 2 * h, 512 * kt:512 * (kt + 1)],
                                                 start=True, stop=False)
                                        mi = t.matmul(bank[:], q_fm[:, 2 * h + 1, :],
                                                      Kst[:, 2 * h + 1, 512 * kt:512 * (kt + 1)],
                                                      start=False, stop=True)
                                        Sm["te_s"].inc(mi)
                                ptpv(qb, 0)
                                ptpv(qb, 1)

                        @blk.scalar
                        def _(s):
                            def attn_store(qb):
                                s.wait_ge(Sm["dv_o"].h, Bo + 2 * (qb + 1))
                                Sm["st_attn"].inc(s.dma_start(
                                    out=attn_myr[rnd][P * qb:P * (qb + 1), :],
                                    in_=astage[:, qb % 2, :]), 16)

                            for qb in range(NTB):
                                nkt = NKT[qb]
                                s.wait_ge(Sm["te_q"].h, Bq + qb + 1)
                                s.wait_ge(Sm["dv_q"].h, max(0, Bdvq + qb))
                                Sm["se_q"].inc(s.activation(
                                    out=q_tm[:], in_=pQ[:], func=AF.Identity, scale=c_inv16[:]))
                                if qb >= 1:
                                    attn_store(qb - 1)
                                for h in (0, 1):
                                    s.wait_ge(Sm["dv_negm"].h, Bnegm + 2 * qb + h + 1)
                                    s.wait_ge(Sm["dv_l"].h, max(0, Bl + 2 * qb + h))
                                    if qb >= 1:
                                        prev_done = TES[(qb - 1, h)] + NKT[qb - 1]
                                        s.wait_ge(Sm["te_pt"].h, Bpt + prev_done)
                                    for kt in range(nkt):
                                        ai = s.activation(
                                            out=P_row2[:, h, 512 * kt:512 * (kt + 1)],
                                            in_=S_row[:, 512 * kt:512 * (kt + 1)],
                                            func=AF.Exp, bias=negm2[:, h:h + 1],
                                            accum_out=lp[:, kt:kt + 1])
                                    Sm["se_exp"].inc(ai)
                                for h in (0, 1):
                                    s.wait_ge(Sm["te_pv"].h, Bpv + TES[(qb, h)] + nkt)
                                    s.wait_ge(Sm["dv_l"].h, Bl + 2 * qb + h + 1)
                                    if h == 0:
                                        s.wait_ge(Sm["st_attn"].h, max(0, Battn + 16 * (qb - 1)))
                                    Sm["dv_o"].inc(s.activation(
                                        out=astage[:, qb % 2, 256 * h:256 * (h + 1)],
                                        in_=pO[h][:, 0:256], func=AF.Identity,
                                        scale=linv_h2[h]))
                                if qb == NTB - 1:
                                    attn_store(qb)

                        @blk.vector
                        def _(v):
                            for qb in range(NTB):
                                nkt = NKT[qb]
                                if qb == 0 and rnd == 0:
                                    Sm["ld_aux"].wait(v)
                                v.wait_ge(Sm["se_q"].h, Bseq + qb + 1)
                                cos = rope_sb[:, qb, 0:64]
                                sin = rope_sb[:, qb, 64:128]
                                for h in (0, 1):
                                    qr = q_tm[:, 256 * h:256 * h + 64]
                                    t1 = q_rsc[:, 0:64]
                                    t2 = q_rsc[:, 64:128]
                                    v.tensor_mul(t1, qr, cos)
                                    Sm["dv_fl"].inc(v.tensor_mul(t2, qr, sin))
                                    Sm["dv_fl"].wait(v)
                                    ev = lambda a: a.rearrange("p (i two) -> p i two", two=2)[:, :, 0]
                                    od = lambda a: a.rearrange("p (i two) -> p i two", two=2)[:, :, 1]
                                    v.tensor_sub(ev(qr), ev(t1), od(t2))
                                    Sm["dv_fl"].inc(v.tensor_add(od(qr), od(t1), ev(t2)))
                                    Sm["dv_fl"].wait(v)
                                v.wait_ge(Sm["te_qtr"].h, max(0, Bqtr + qb))
                                Sm["dv_q"].inc(v.tensor_copy(q_bf[:], q_tm[:]))
                                v.wait_ge(Sm["te_qtr"].h, Bqtr + qb + 1)
                                v.wait_ge(Sm["te_s"].h, Bs + TES[(qb, 0)])
                                Sm["dv_qfm"].inc(v.tensor_copy(
                                    q_fm.rearrange("p u q -> p (u q)"),
                                    p_qtr[:, 0:512]))
                                for h in (0, 1):
                                    v.wait_ge(Sm["se_exp"].h, max(0, Bexp + 2 * qb + h))
                                    for kt in range(nkt):
                                        v.wait_ge(Sm["te_s"].h, Bs + TES[(qb, h)] + kt + 1)
                                        Sm["dv_sevac"].inc(v.tensor_scalar(
                                            out=S_row[:, 512 * kt:512 * (kt + 1)],
                                            in0=pS[kt % 2][:], scalar1=0.0, scalar2=None,
                                            op0=ALU.add, op1=ALU.max,
                                            accum_out=mxp[:, kt:kt + 1]))
                                    Sm["dv_sevac"].wait(v)
                                    Sm["dv_negm"].inc(v.reduce_max(
                                        negm2[:, h:h + 1], mxp[:, 0:nkt], axis=AX.X, negate=True))
                                    v.wait_ge(Sm["se_exp"].h, Bexp + 2 * qb + h + 1)
                                    dtile = P_row2[:, h, 512 * (nkt - 1):512 * nkt]
                                    Sm["dv_fl"].inc(v.scalar_tensor_tensor(
                                        out=dtile, in0=dtile, scalar=1.0,
                                        in1=dmask[:, qb % 4, :], op0=ALU.mult, op1=ALU.mult,
                                        accum_out=lp[:, nkt - 1:nkt]))
                                    Sm["dv_fl"].wait(v)
                                    Sm["dv_fl"].inc(v.reduce_sum(rtmp7[:, 0:1], lp[:, 0:nkt], axis=AX.X))
                                    Sm["dv_fl"].wait(v)
                                    v.wait_ge(Sm["dv_o"].h, max(0, Bo + 2 * qb + h - 1))
                                    Sm["dv_l"].inc(v.reciprocal(linv_h2[h], rtmp7[:, 0:1]))
                                for h in (0, 1):
                                    v.wait_ge(Sm["te_pv"].h, Bpv + TES[(qb, h)])
                                    for kt in range(NKT[qb]):
                                        v.wait_ge(Sm["te_pt"].h, Bpt + TES[(qb, h)] + kt + 1)
                                        if kt >= 2:
                                            v.wait_ge(Sm["te_pv"].h, Bpv + TES[(qb, h)] + kt - 1)
                                        Sm["dv_ptevac"].inc(v.tensor_copy(
                                            Pt_sb[:, kt % 2, :],
                                            p_pt[kt % 2][:, 0:512]))

                    with nc.Block() as blk:
                        @blk.scalar
                        def _(s):
                            Sm["st_attn"].wait(s)

            def emit_gather(rnd):
                with nc.Block() as blk:
                    @blk.gpsimd
                    def _(g):
                        Sm["st_attn"].wait(g)
                        Sm["cc"].inc(g.collective_compute(
                            "AllGather", ALU.bypass,
                            replica_groups=[[0, 1, 2, 3], [4, 5, 6, 7]],
                            ins=[attn_myr[rnd]],
                            outs=[attn_gr[rnd]]), 1)

            emit_kv_phase(0, True)
            if "xn" in taps:
                _emit_dram_tap(nc, Sm, tap_outs["xn"],
                               xn_dram.rearrange("tb p d -> (tb p) d"))
            if "kst" in taps:
                _emit_sbuf_tap(nc, Sm, tap_outs["kst"], Kst.rearrange("p u s -> p (u s)"))
            if "vst" in taps:
                _emit_sbuf_tap(nc, Sm, tap_outs["vst"], Vst.rearrange("p tb n -> p (tb n)"))
            if upto >= 2:
                emit_flash_phase(0)
            if upto >= 5:
                emit_gather(0)
            if upto >= 3:
                emit_kv_phase(1, False)
            if upto >= 4:
                emit_flash_phase(1)
            if upto >= 5:
                emit_gather(1)

        if upto >= 6:
            with ExitStack() as pctx:
                pen = pctx.enter_context
                attnT = pen(nc.sbuf_tensor("attnT", [P, NTBL, D], BF16))
                wo_sb = pen(nc.sbuf_tensor("wo_sb", [P, 3, 16, 512], BF16))
                xr_sb = pen(nc.sbuf_tensor("xr_sb", [P, NTBL, 512], F32))
                x2st = pen(nc.sbuf_tensor("x2st", [P, 2, 512], F32))
                wo_b = Sm["ld_wo"].v
                xr_b = Sm["ld_x2r"].v
                x2_b = Sm["st_x2"].v

                def wo_src(step):
                    o, ph = step // 2, step % 2
                    return wobf.rearrange("(c p) n -> p c n", p=P)[
                        :, 16 * ph:16 * (ph + 1), 512 * o:512 * (o + 1)]

                with nc.Block() as blk:
                    @blk.gpsimd
                    def _(g):
                        g.wait_ge(Sm["cast_w"].h, 112)
                        for st_ in range(3):
                            Sm["ld_wo"].inc(g.dma_start(
                                out=wo_sb[:, st_, :, :], in_=wo_src(st_)), 16)

                    @blk.sync
                    def _(sync):
                        Sm["ld_x2r"].inc(sync.dma_start(
                            out=xr_sb[:],
                            in_=xres_d.rearrange("(tb p) n -> p tb n", p=P)[:, :, 0:512]), 16)

                with ExitStack() as sctx:
                    sen = sctx.enter_context
                    attn_tm = sen(nc.sbuf_tensor("attn_tm", [P, NTBL, D], BF16))
                    p_at = sen(nc.psum_tensor("p_at", [P, 2, 1024], BF16))
                    at_b = Sm["ld_at"].v
                    at2_b = Sm["ld_at2"].v
                    with nc.Block() as blk:
                        @blk.gpsimd
                        def _(g):
                            g.wait_ge(Sm["cc"].h, Sm["cc"].v - 1)
                            gm = g.partition_id() % 4
                            big = attn_gr[0].rearrange("(r s) n -> s r n", r=4)
                            for tb_ in range(NTBL):
                                Sm["ld_at"].inc(g.dma_start(
                                    out=attn_tm[:, tb_, :].rearrange("p (r n) -> p r n", r=4)[:, :, 0:512],
                                    in_=big[bass.ds(gm * TSL + P * tb_, P), :, :]), 16)

                        @blk.scalar
                        def _(sc):
                            Sm["cc"].wait(sc)
                            gm2 = sc.partition_id() % 4
                            big2 = attn_gr[1].rearrange("(r s) n -> s r n", r=4)
                            for tb_ in range(NTBL):
                                Sm["ld_at2"].inc(sc.dma_start(
                                    out=attn_tm[:, tb_, :].rearrange("p (r n) -> p r n", r=4)[:, :, 512:1024],
                                    in_=big2[bass.ds(gm2 * TSL + P * tb_, P), :, :]), 16)

                    JL = [0, 2, 4, 6, 1, 3, 5, 7]
                    for tblk in range(NTBL):
                        with nc.Block() as blk:
                            teat_b = Sm["te_at"].v
                            dvat_b = Sm["dv_at"].v

                            @blk.tensor
                            def _(t):
                                for idx, j in enumerate(JL):
                                    if idx == 0:
                                        t.wait_ge(Sm["ld_at"].h, at_b + 16 * (tblk + 1))
                                    if idx == 4:
                                        t.wait_ge(Sm["ld_at2"].h, at2_b + 16 * (tblk + 1))
                                    t.wait_ge(Sm["dv_at"].h, max(0, dvat_b + idx - 1))
                                    for u in range(4):
                                        c = 4 * j + u
                                        ti = t.transpose(
                                            p_at[:, idx % 2, 128 * u:128 * (u + 1)],
                                            attn_tm[:, tblk, 128 * c:128 * (c + 1)], ident[:])
                                        if u == 3:
                                            Sm["te_at"].inc(ti)

                            @blk.vector
                            def _(v):
                                for idx, j in enumerate(JL):
                                    v.wait_ge(Sm["te_at"].h, teat_b + idx + 1)
                                    Sm["dv_at"].inc(v.tensor_copy(
                                        attnT[:, tblk, 512 * j:512 * (j + 1)],
                                        p_at[:, idx % 2, 0:512]))

                with ExitStack() as sctx:
                    sen = sctx.enter_context
                    pF6 = [sen(nc.psum_tensor(f"pF6_{i}", [P, 512], F32)) for i in range(NTBL)]
                    for o in range(8):
                        for ph in range(2):
                            step = 2 * o + ph
                            with nc.Block() as blk:
                                teo_b = Sm["te_o"].v

                                @blk.tensor
                                def _(t):
                                    Sm["ld_wo"].wait(t, wo_b + 16 * (step + 1))
                                    for cl in range(16):
                                        c = 16 * ph + cl
                                        for tblk in range(NTBL):
                                            mi = t.matmul(
                                                pF6[tblk][:],
                                                attnT[:, tblk, 128 * c:128 * (c + 1)],
                                                wo_sb[:, step % 3, cl, :],
                                                start=(c == 0), stop=(c == NCH - 1))
                                            if c == NCH - 1 and tblk < NTBL - 1:
                                                Sm["te_o"].inc(mi)
                                            if cl == 15 and tblk == NTBL - 1:
                                                Sm["te_oh"].inc(mi)

                                if ph == 1:
                                    dvx2_b = Sm["dv_x2"].v

                                    @blk.vector
                                    def _(v):
                                        Sm["ld_x2r"].wait(v, xr_b + 16 * (o + 1))
                                        for tblk in range(NTBL):
                                            if tblk < NTBL - 1:
                                                Sm["te_o"].wait(v, teo_b + tblk + 1)
                                            else:
                                                Sm["te_oh"].wait(v)
                                            Sm["st_x2"].wait(v, x2_b + 16 * (8 * o + tblk - 1))
                                            Sm["dv_x2"].inc(v.tensor_add(
                                                x2st[:, tblk % 2, :], pF6[tblk][:],
                                                xr_sb[:, tblk, :]))

                                    @blk.scalar
                                    def _(sc):
                                        for tblk in range(NTBL):
                                            Sm["dv_x2"].wait(sc, dvx2_b + tblk + 1)
                                            Sm["st_x2"].inc(sc.dma_start(
                                                out=x2_dram[:, 512 * o:512 * (o + 1)]
                                                .rearrange("(tb p) n -> p tb n", p=P)[:, tblk, :],
                                                in_=x2st[:, tblk % 2, :]), 16)

                                    @blk.sync
                                    def _(sync):
                                        if o + 1 < 8:
                                            sync.wait_ge(Sm["dv_x2"].h, dvx2_b + 8)
                                            Sm["ld_x2r"].inc(sync.dma_start(
                                                out=xr_sb[:],
                                                in_=xres_d.rearrange("(tb p) n -> p tb n", p=P)[:, :, 512 * (o + 1):512 * (o + 2)]), 16)

                                @blk.gpsimd
                                def _(g):
                                    if step + 3 < 16:
                                        Sm["te_oh"].wait(g)
                                        Sm["ld_wo"].inc(g.dma_start(
                                            out=wo_sb[:, (step + 3) % 3, :, :],
                                            in_=wo_src(step + 3)), 16)

                    with nc.Block() as blk:
                        @blk.gpsimd
                        def _(g):
                            Sm["st_x2"].wait(g)
            if "x2" in taps:
                _emit_dram_tap(nc, Sm, tap_outs["x2"], x2_dram)

        if upto >= 7:
            with ExitStack() as pctx:
                pen = pctx.enter_context
                bufA = pen(nc.sbuf_tensor("bufA", [P, NTBL, D], BF16))
                bufB = pen(nc.sbuf_tensor("bufB", [P, NTBL, D], BF16))
                rs_prev = pen(nc.sbuf_tensor("rs_prev", [P, NTBL], F32))
                rs_next = pen(nc.sbuf_tensor("rs_next", [P, NTBL], F32))
                bL = pen(nc.sbuf_tensor("bL", [P, NTBL], F32))
                hp = pen(nc.sbuf_tensor("hp", [P, NTBL, 16], F32))
                rtmp = pen(nc.sbuf_tensor("rtmp", [P, 2], F32))

                with ExitStack() as sctx:
                    sen = sctx.enter_context
                    x2row = sen(nc.sbuf_tensor("x2row", [P, 2, D], F32))
                    x2n_tm = sen(nc.sbuf_tensor("x2n_tm", [P, D], BF16))
                    p_tr7 = sen(nc.psum_tensor("p_tr7", [P, 2, 1024], BF16))
                    x2r_b = Sm["ld_x2r"].v
                    var_b = Sm["dv_var"].v
                    rstd_b = Sm["se_rstd"].v
                    xn_b = Sm["se_xn"].v
                    tr_b = Sm["te_tr"].v
                    evac_b = Sm["dv_evac"].v
                    sq_b = Sm["se_sq"].v

                    with nc.Block() as blk:
                        @blk.sync
                        def _(sync):
                            Sm["st_x2"].wait(sync)
                            Sm["ld_x2r"].inc(sync.dma_start(
                                out=x2row[:, 0, :], in_=x2_dram[0:P, :]), 16)

                    with nc.Block() as blk:
                        @blk.sync
                        def _(sync):
                            for tblk in range(1, NTBL):
                                sync.wait_ge(Sm["se_xn"].h, max(0, xn_b + tblk - 1))
                                Sm["ld_x2r"].inc(sync.dma_start(
                                    out=x2row[:, tblk % 2, :],
                                    in_=x2_dram[P * tblk:P * (tblk + 1), :]), 16)

                        def ln2_chain(v, tblk):
                            xt = x2row[:, tblk % 2, :]
                            v.wait_ge(Sm["ld_x2r"].h, x2r_b + 16 * (tblk + 1))
                            if tblk >= 1:
                                v.wait_ge(Sm["te_tr"].h, tr_b + 8 * tblk)
                                v.wait_ge(Sm["se_sq"].h, sq_b + tblk)
                                v.wait_ge(Sm["se_rstd"].h, rstd_b + tblk)
                            Sm["dv_fl"].inc(v.reduce_sum(st_sum[:], xt, axis=AX.X))
                            Sm["dv_fl"].inc(v.scalar_tensor_tensor(
                                out=x2n_tm[:], in0=xt, scalar=1.0, in1=xt,
                                op0=ALU.mult, op1=ALU.mult, accum_out=st_ssq[:]))
                            Sm["dv_fl"].wait(v)
                            Sm["dv_fl"].inc(v.tensor_scalar(
                                out=st_mean[:], in0=st_sum[:],
                                scalar1=-1.0 / D, scalar2=None, op0=ALU.mult))
                            Sm["dv_fl"].wait(v)
                            Sm["dv_fl"].inc(v.tensor_mul(st_m2[:], st_mean[:], st_mean[:]))
                            Sm["dv_fl"].wait(v)
                            Sm["dv_var"].inc(v.scalar_tensor_tensor(
                                out=st_var[:], in0=st_ssq[:], scalar=1.0 / D,
                                in1=st_m2[:], op0=ALU.mult, op1=ALU.subtract))
                            v.wait_ge(Sm["se_rstd"].h, rstd_b + tblk + 1)
                            Sm["dv_fl"].inc(v.tensor_copy(st_negmr[:], st_rstd[:]))
                            Sm["dv_fl"].wait(v)
                            Sm["se_xn"].inc(v.tensor_scalar(
                                out=x2n_tm[:], in0=xt, scalar1=st_mean[:],
                                scalar2=st_negmr[:], op0=ALU.add, op1=ALU.mult))

                        @blk.vector
                        def _(v):
                            ln2_chain(v, 0)
                            for tblk in range(NTBL):
                                for j in range(8):
                                    v.wait_ge(Sm["te_tr"].h, tr_b + 8 * tblk + j + 1)
                                    Sm["dv_evac"].inc(v.tensor_copy(
                                        bufA[:, tblk, 512 * j:512 * (j + 1)],
                                        p_tr7[:, j % 2, 0:512]))
                                if tblk + 1 < NTBL:
                                    ln2_chain(v, tblk + 1)

                        @blk.scalar
                        def _(s):
                            for tblk in range(NTBL):
                                s.wait_ge(Sm["dv_var"].h, var_b + tblk + 1)
                                if tblk >= 1:
                                    s.wait_ge(Sm["se_xn"].h, xn_b + tblk)
                                Sm["se_fl"].inc(s.activation(
                                    out=st_rstd[:], in_=st_var[:], func=AF.Ln, bias=c_eps[:]))
                                Sm["se_fl"].wait(s)
                                Sm["se_rstd"].inc(s.activation(
                                    out=st_rstd[:], in_=st_rstd[:], func=AF.Exp, scale=c_mhalf[:]))
                                s.wait_ge(Sm["se_xn"].h, xn_b + tblk + 1)
                                Sm["se_sq"].inc(s.activation(
                                    out=x2n_tm[:], in_=x2n_tm[:], func=AF.Identity,
                                    accum_out=rs_prev[:, tblk:tblk + 1]))

                        @blk.tensor
                        def _(t):
                            for tblk in range(NTBL):
                                t.wait_ge(Sm["se_xn"].h, xn_b + tblk + 1)
                                for j in range(8):
                                    t.wait_ge(Sm["dv_evac"].h, max(0, evac_b + 8 * tblk + j - 1))
                                    for u in range(4):
                                        c = 4 * j + u
                                        ti = t.transpose(
                                            p_tr7[:, j % 2, 128 * u:128 * (u + 1)],
                                            x2n_tm[:, 128 * c:128 * (c + 1)], ident[:])
                                        if u == 3:
                                            Sm["te_tr"].inc(ti)

                with ExitStack() as sctx:
                    sen = sctx.enter_context
                    wm_sb = sen(nc.sbuf_tensor("wm_sb", [P, 2, 16, 512], BF16))
                    x2c = sen(nc.sbuf_tensor("x2c", [P, NTBL, 512], F32))
                    ostage = sen(nc.sbuf_tensor("ostage", [P, 2, 512], F32))
                    otmp = sen(nc.sbuf_tensor("otmp", [P, 512], F32))

                    def mask_src(L, step):
                        o, ph = step // 2, step % 2
                        return km_d[L].rearrange("(c p) n -> p c n", p=P)[
                            :, 16 * ph:16 * (ph + 1), 512 * o:512 * (o + 1)]

                    def emit_layer(L, src_buf, dst_buf, pFF):
                        msk_b = Sm["ld_msk"].v
                        with nc.Block() as blk:
                            if L < 2:
                                @blk.scalar
                                def _(s):
                                    if L == 1:
                                        Sm["dv_rs"].wait(s)
                                        s.copy(rs_prev[:], rs_next[:])
                                    for tblk in range(NTBL):
                                        s.activation(out=bL[:, tblk:tblk + 1],
                                                     in_=rs_prev[:, tblk:tblk + 1],
                                                     func=AF.Identity,
                                                     scale=sderiv[:, 3 + L:4 + L])
                            else:
                                @blk.vector
                                def _(v):
                                    Sm["dv_rs"].wait(v)
                                    for tblk in range(NTBL):
                                        v.tensor_mul(bL[:, tblk:tblk + 1],
                                                     rs_next[:, tblk:tblk + 1],
                                                     sderiv[:, 5:6])

                            @blk.gpsimd
                            def _(g):
                                for st_ in range(2):
                                    Sm["ld_msk"].inc(g.dma_start(
                                        out=wm_sb[:, st_ % 2, :, :], in_=mask_src(L, st_)), 16)

                            if L == 2:
                                @blk.sync
                                def _(sync):
                                    Sm["ld_xc"].inc(sync.dma_start(
                                        out=x2c[:],
                                        in_=x2_dram.rearrange("(tb p) n -> p tb n", p=P)[:, :, 0:512]), 16)

                        xc_b = Sm["ld_xc"].v - (16 if L == 2 else 0)
                        out_b = Sm["st_out"].v
                        for o in range(8):
                            for ph in range(2):
                                step = 2 * o + ph
                                with nc.Block() as blk:
                                    teff_b = Sm["te_ff"].v
                                    if L >= 1 and ph == 1:
                                        @blk.vector
                                        def _(v):
                                            Sm["ld_msk"].wait(v, msk_b + 16 * (step + 1))
                                            buf = step % 2
                                            v.tensor_scalar(
                                                out=wm_sb[:, buf, 9, :], in0=wm_sb[:, buf, 9, :],
                                                scalar1=rs_mix[:], scalar2=None, op0=ALU.mult)
                                            Sm["dv_wfix"].inc(v.tensor_scalar(
                                                out=wm_sb[:, buf, 10:16, :].rearrange("p c n -> p (c n)"),
                                                in0=wm_sb[:, buf, 10:16, :].rearrange("p c n -> p (c n)"),
                                                scalar1=-4.0, scalar2=None, op0=ALU.mult))

                                    dvwf = Sm["dv_wfix"].v

                                    @blk.tensor
                                    def _(t):
                                        Sm["ld_msk"].wait(t, msk_b + 16 * (step + 1))
                                        if L >= 1 and ph == 1:
                                            Sm["dv_wfix"].wait(t, dvwf)
                                        for cl in range(16):
                                            c = 16 * ph + cl
                                            for tblk in range(NTBL):
                                                mi = t.matmul(
                                                    pFF[tblk][:],
                                                    src_buf[:, tblk, 128 * c:128 * (c + 1)],
                                                    wm_sb[:, step % 2, cl, :],
                                                    start=(c == 0), stop=(c == NCH - 1))
                                                if c == NCH - 1 and tblk < NTBL - 1:
                                                    Sm["te_ff"].inc(mi)
                                                if cl == 15 and tblk == NTBL - 1:
                                                    Sm["te_fh"].inc(mi)

                                    if ph == 1:
                                        if L < 2:
                                            @blk.scalar
                                            def _(s):
                                                for tblk in range(NTBL):
                                                    if tblk < NTBL - 1:
                                                        Sm["te_ff"].wait(s, teff_b + tblk + 1)
                                                    else:
                                                        Sm["te_fh"].wait(s)
                                                    scale_ap = sderiv[:, L:L + 1]
                                                    bias_ap = bL[:, tblk:tblk + 1]
                                                    if o == 6:
                                                        si = s.activation(
                                                            out=dst_buf[:, tblk, 3072:NE],
                                                            in_=pFF[tblk][:, 0:NE - 3072],
                                                            func=AF.Relu, scale=scale_ap, bias=bias_ap,
                                                            accum_out=hp[:, tblk, 6:7])
                                                        si = s.activation(
                                                            out=dst_buf[:, tblk, NE:3584],
                                                            in_=pFF[tblk][:, NE - 3072:512],
                                                            func=AF.Relu, scale=scale_ap, bias=bias_ap,
                                                            accum_out=hp[:, tblk, 7:8])
                                                    else:
                                                        col = o if o < 6 else 8
                                                        si = s.activation(
                                                            out=dst_buf[:, tblk, 512 * o:512 * (o + 1)],
                                                            in_=pFF[tblk][:],
                                                            func=AF.Relu, scale=scale_ap, bias=bias_ap,
                                                            accum_out=hp[:, tblk, col:col + 1])
                                                    Sm["se_ff"].inc(si)
                                        else:
                                            dvff_b = Sm["dv_ff"].v

                                            @blk.vector
                                            def _(v):
                                                Sm["ld_xc"].wait(v, xc_b + 16 * (o + 1))
                                                for tblk in range(NTBL):
                                                    if tblk < NTBL - 1:
                                                        Sm["te_ff"].wait(v, teff_b + tblk + 1)
                                                    else:
                                                        Sm["te_fh"].wait(v)
                                                    Sm["st_out"].wait(v, out_b + 16 * (8 * o + tblk - 1))
                                                    Sm["dv_fl"].inc(v.tensor_scalar(
                                                        out=otmp[:], in0=x2c[:, tblk, :],
                                                        scalar1=bL[:, tblk:tblk + 1],
                                                        scalar2=None, op0=ALU.add))
                                                    Sm["dv_fl"].wait(v)
                                                    Sm["dv_ff"].inc(v.scalar_tensor_tensor(
                                                        out=ostage[:, tblk % 2, :],
                                                        in0=pFF[tblk][:], scalar=sderiv[:, 2:3],
                                                        in1=otmp[:], op0=ALU.mult, op1=ALU.add))

                                            @blk.scalar
                                            def _(sc):
                                                for tblk in range(NTBL):
                                                    Sm["dv_ff"].wait(sc, dvff_b + tblk + 1)
                                                    Sm["st_out"].inc(sc.dma_start(
                                                        out=out_d[:, 512 * o:512 * (o + 1)]
                                                        .rearrange("(tb p) n -> p tb n", p=P)[:, tblk, :],
                                                        in_=ostage[:, tblk % 2, :]), 16)

                                            @blk.sync
                                            def _(sync):
                                                if o + 1 < 8:
                                                    sync.wait_ge(Sm["dv_ff"].h, dvff_b + 8)
                                                    Sm["ld_xc"].inc(sync.dma_start(
                                                        out=x2c[:],
                                                        in_=x2_dram.rearrange("(tb p) n -> p tb n", p=P)[:, :, 512 * (o + 1):512 * (o + 2)]), 16)

                                    @blk.gpsimd
                                    def _(g):
                                        if step + 2 < 16:
                                            Sm["te_fh"].wait(g)
                                            Sm["ld_msk"].inc(g.dma_start(
                                                out=wm_sb[:, step % 2, :, :],
                                                in_=mask_src(L, step + 2)), 16)

                        if L < 2:
                            with nc.Block() as blk:
                                @blk.vector
                                def _(v):
                                    Sm["se_ff"].wait(v)
                                    for tblk in range(NTBL):
                                        v.reduce_sum(rtmp[:, 0:1], hp[:, tblk, 0:7], axis=AX.X)
                                        Sm["dv_fl"].inc(v.reduce_sum(rtmp[:, 1:2], hp[:, tblk, 7:9], axis=AX.X), 2)
                                        Sm["dv_fl"].wait(v)
                                        Sm["dv_rs"].inc(v.scalar_tensor_tensor(
                                            out=rs_next[:, tblk:tblk + 1], in0=rtmp[:, 1:2],
                                            scalar=-4.0, in1=rtmp[:, 0:1],
                                            op0=ALU.mult, op1=ALU.add))

                    _trn = [0]

                    def emit_transpose(src_buf, dst_buf):
                        _trn[0] += 1
                        with ExitStack() as tctx:
                            p_trx = tctx.enter_context(nc.psum_tensor(f"p_trx{_trn[0]}", [P, 2, 1024], BF16))
                            for tblk in range(NTBL):
                                with nc.Block() as blk:
                                    tetr_b = Sm["te_tr"].v
                                    dvev_b = Sm["dv_evac"].v

                                    @blk.tensor
                                    def _(t):
                                        for j in range(8):
                                            Sm["dv_evac"].wait(t, dvev_b + j - 1)
                                            for u in range(4):
                                                c = 4 * j + u
                                                ti = t.transpose(
                                                    p_trx[:, j % 2, 128 * u:128 * (u + 1)],
                                                    src_buf[:, tblk, 128 * c:128 * (c + 1)], ident[:])
                                                if u == 3:
                                                    Sm["te_tr"].inc(ti)

                                    @blk.vector
                                    def _(v):
                                        for j in range(8):
                                            Sm["te_tr"].wait(v, tetr_b + j + 1)
                                            Sm["dv_evac"].inc(v.tensor_copy(
                                                dst_buf[:, tblk, 512 * j:512 * (j + 1)],
                                                p_trx[:, j % 2, 0:512]))

                    def layer_scoped(L, src_buf, dst_buf):
                        with ExitStack() as lctx:
                            pFF = [lctx.enter_context(
                                nc.psum_tensor(f"pFF{L}_{i}", [P, 512], F32))
                                for i in range(NTBL)]
                            emit_layer(L, src_buf, dst_buf, pFF)

                    if "x2nT" in taps:
                        _emit_sbuf_tap(nc, Sm, tap_outs["x2nT"], bufA.rearrange("p tb d -> p (tb d)"))
                    layer_scoped(0, bufA, bufB)
                    emit_transpose(bufB, bufA)
                    if "h1T" in taps:
                        _emit_sbuf_tap(nc, Sm, tap_outs["h1T"], bufA.rearrange("p tb d -> p (tb d)"))
                    layer_scoped(1, bufA, bufB)
                    emit_transpose(bufB, bufA)
                    if "h2T" in taps:
                        _emit_sbuf_tap(nc, Sm, tap_outs["h2T"], bufA.rearrange("p tb d -> p (tb d)"))
                    layer_scoped(2, bufA, None)

                    with nc.Block() as blk:
                        @blk.gpsimd
                        def _(g):
                            Sm["st_out"].wait(g)

    return nc


def _emit_dram_tap(nc, Sm, dst_f32, src_ap):
    rows = src_ap.shape[0]
    step = P
    _TAPN[0] += 1
    with nc.sbuf_tensor(f"tapbuf_{_TAPN[0]}", [P, src_ap.shape[1]], F32) as tb_:
        for r0 in range(0, rows, step):
            with nc.Block() as blk:
                @blk.gpsimd
                def _(g):
                    Sm["st_tap"].inc(g.dma_start(out=tb_[0:min(step, rows - r0), :],
                                                 in_=src_ap[r0:r0 + min(step, rows - r0), :]), 16)
                    Sm["st_tap"].wait(g)
                    Sm["st_tap"].inc(g.dma_start(out=dst_f32[r0:r0 + min(step, rows - r0), :],
                                                 in_=tb_[0:min(step, rows - r0), :]), 16)
                    Sm["st_tap"].wait(g)


_TAPN = [0]


def _emit_sbuf_tap(nc, Sm, dst_f32, src_sbuf_ap):
    cols = src_sbuf_ap.shape[1]
    CH = 4096
    _TAPN[0] += 1
    with nc.sbuf_tensor(f"tapbuf2_{_TAPN[0]}", [P, CH], F32) as tb_:
        for c0 in range(0, cols, CH):
            w = min(CH, cols - c0)
            with nc.Block() as blk:
                @blk.vector
                def _(v):
                    v.tensor_copy(tb_[:, 0:w], src_sbuf_ap[:, c0:c0 + w])
            with nc.Block() as blk:
                @blk.gpsimd
                def _(g):
                    Sm["st_tap"].inc(g.dma_start(out=dst_f32[:, c0:c0 + w], in_=tb_[:, 0:w]), 16)
                    Sm["st_tap"].wait(g)


def _host_prep(inputs):
    x = np.asarray(inputs["L0_attn_hidden_states"], np.float32)
    wq = np.asarray(inputs["L0_attn_wq"], np.float32)
    wk = np.asarray(inputs["L0_attn_wk"], np.float32)
    wv = np.asarray(inputs["L0_attn_wv"], np.float32)
    wo = np.asarray(inputs["L0_attn_wo"], np.float32)
    k0 = np.ascontiguousarray(np.asarray(inputs["L0_ffn_k0"]).view(np.uint8))
    k1 = np.ascontiguousarray(np.asarray(inputs["L0_ffn_k1"]).view(np.uint8))
    k2 = np.ascontiguousarray(np.asarray(inputs["L0_ffn_k2"]).view(np.uint8))
    sv = np.ascontiguousarray(np.tile(np.array(
        [np.float32(inputs["L0_ffn_s0"]), np.float32(inputs["L0_ffn_s1"]),
         np.float32(inputs["L0_ffn_s2"])], dtype=np.float32)[None, :], (P, 1)))
    inv = 1.0 / 10000.0 ** (np.arange(0, 64, 2, dtype=np.float32) / 64.0)
    ang = np.arange(S, dtype=np.float32)[:, None] * inv[None, :]
    cosr = np.repeat(np.cos(ang), 2, axis=1).astype(np.float32)
    sinr = np.repeat(np.sin(ang), 2, axis=1).astype(np.float32)
    rope = np.ascontiguousarray(np.concatenate([cosr, sinr], axis=1))
    woT = np.ascontiguousarray(wo.T)

    in_maps = []
    for c in range(8):
        b, g = c // 4, c % 4
        sl = slice(1024 * g, 1024 * (g + 1))
        in_maps.append({
            "x": np.ascontiguousarray(x[b]),
            "x_res": np.ascontiguousarray(x[b, sl, :]),
            "wq_sl": np.ascontiguousarray(wq[sl, :].T),
            "wk_sl": np.ascontiguousarray(wk[sl, :].T),
            "wv_sl": np.ascontiguousarray(wv[sl, :].T),
            "woT": woT,
            "k0": k0, "k1": k1, "k2": k2,
            "rope": rope, "svec": sv,
        })
    return in_maps


def kernel(**inputs):
    in_maps = _host_prep(inputs)
    nc = build_nc(upto=7)
    res = run_bass_kernel_spmd(nc, in_maps, core_ids=list(range(8)))
    out = np.empty((2, S, D), np.float32)
    for c in range(8):
        b, g = c // 4, c % 4
        out[b, 1024 * g:1024 * (g + 1), :] = res.results[c]["out"]
    return out


# version 10
# speedup vs baseline: 1.0560x; 1.0560x over previous
import numpy as np
import ml_dtypes
import concourse.bass as bass
import concourse.mybir as mybir
from concourse.bass_utils import run_bass_kernel_spmd
from contextlib import ExitStack

F32 = mybir.dt.float32
BF16 = mybir.dt.bfloat16
U8 = mybir.dt.uint8
AF = mybir.ActivationFunctionType
ALU = mybir.AluOpType
AX = mybir.AxisListType
BFNP = ml_dtypes.bfloat16

P = 128
S, D = 4096, 4096
NTB = S // P
TSL = 1024
NTBL = TSL // P
NE = 3277
EPS = 1e-5
NCH = D // P


class Ctr:
    def __init__(self, h):
        self.h = h
        self.v = 0

    def inc(self, instr, n=1):
        self.v += n
        instr.then_inc(self.h, n)

    def wait(self, eng, v=None):
        eng.wait_ge(self.h, max(0, self.v if v is None else v))


def build_nc(upto=7, taps=()):
    nc = bass.Bass("TRN2", target_bir_lowering=False)

    x_d = nc.dram_tensor("x", [S, D], F32, kind="ExternalInput").ap()
    xres_d = nc.dram_tensor("x_res", [TSL, D], F32, kind="ExternalInput").ap()
    wq_d = nc.dram_tensor("wq_sl", [D, 1024], F32, kind="ExternalInput").ap()
    wk_d = nc.dram_tensor("wk_sl", [D, 1024], F32, kind="ExternalInput").ap()
    wv_d = nc.dram_tensor("wv_sl", [D, 1024], F32, kind="ExternalInput").ap()
    wo_d = nc.dram_tensor("woT", [D, D], F32, kind="ExternalInput").ap()
    km_d = [nc.dram_tensor(f"k{i}", [D, D], U8, kind="ExternalInput").ap() for i in range(3)]
    rope_d = nc.dram_tensor("rope", [S, 128], F32, kind="ExternalInput").ap()
    svec_d = nc.dram_tensor("svec", [P, 3], F32, kind="ExternalInput").ap()
    out_d = nc.dram_tensor("out", [TSL, D], F32, kind="ExternalOutput").ap()

    xn_dram = nc.dram_tensor("xn_dram", [NTB, P, D], BF16, kind="Internal").ap()
    attn_myr = [nc.dram_tensor(f"attn_my{r}", [S, 512], BF16, kind="Internal").ap()
                for r in range(2)]
    attn_gr = [nc.dram_tensor(f"attn_g{r}", [4 * S, 512], BF16, kind="Internal").ap()
               for r in range(2)]
    x2_dram = nc.dram_tensor("x2_dram", [TSL, D], F32, kind="Internal").ap()
    wkbf1 = nc.dram_tensor("wkbf1", [D, 512], BF16, kind="Internal").ap()
    wvbf1 = nc.dram_tensor("wvbf1", [D, 512], BF16, kind="Internal").ap()
    wqbf1 = nc.dram_tensor("wqbf1", [D, 512], BF16, kind="Internal").ap()
    wobf = nc.dram_tensor("wobf", [D, D], BF16, kind="Internal").ap()

    tap_outs = {}
    for t in taps:
        shp = {"xn": [NTB * P, D], "kst": [P, 4 * S], "vst": [P, NTB * 512],
               "attn_my": [S, 1024], "attn_g": [4 * S, 1024], "x2": [TSL, D],
               "x2nT": [P, NTBL * D], "h1T": [P, NTBL * D], "h2T": [P, NTBL * D]}[t]
        tap_outs[t] = nc.dram_tensor("tap_" + t, shp, F32, kind="ExternalOutput").ap()

    with ExitStack() as ctx:
        en = ctx.enter_context

        ident = en(nc.sbuf_tensor("ident", [P, P], BF16))
        rs_mix = en(nc.sbuf_tensor("rs_mix", [P, 1], F32))
        svec = en(nc.sbuf_tensor("svec_sb", [P, 3], F32))
        sderiv = en(nc.sbuf_tensor("sderiv", [P, 8], F32))
        st_sum = en(nc.sbuf_tensor("st_sum", [P, 1], F32))
        st_ssq = en(nc.sbuf_tensor("st_ssq", [P, 1], F32))
        st_mean = en(nc.sbuf_tensor("st_mean", [P, 1], F32))
        st_m2 = en(nc.sbuf_tensor("st_m2", [P, 1], F32))
        st_var = en(nc.sbuf_tensor("st_var", [P, 1], F32))
        st_rstd = en(nc.sbuf_tensor("st_rstd", [P, 1], F32))
        st_negmr = en(nc.sbuf_tensor("st_negmr", [P, 1], F32))
        c_eps = en(nc.sbuf_tensor("c_eps", [P, 1], F32))
        c_mhalf = en(nc.sbuf_tensor("c_mhalf", [P, 1], F32))
        c_inv16 = en(nc.sbuf_tensor("c_inv16", [P, 1], F32))

        Sm = {n: Ctr(en(nc.semaphore(n))) for n in [
            "ld_x", "ld_w", "ld_blob", "ld_aux", "st_blob",
            "se_sq", "dv_var", "se_rstd", "dv_negmr", "se_xn",
            "te_tr", "dv_evac", "te_kv", "se_kv", "dv_kbf", "te_ktr", "dv_ktr",
            "te_q", "se_q", "dv_q", "te_qtr", "dv_qfm",
            "te_s", "dv_sevac", "dv_negm", "se_exp", "dv_l",
            "te_pt", "dv_ptevac", "te_pv", "dv_o", "st_attn",
            "cc", "ld_at", "te_at", "dv_at", "ld_wo", "te_o", "dv_x2", "st_x2",
            "ld_x2r", "ld_msk", "dv_wfix", "te_ff", "se_ff", "dv_ff", "st_out",
            "ld_xc", "dv_rs", "st_tap", "te_oh", "te_fh", "dv_fl", "se_fl", "gp_fl", "ld_at2", "cast_w",
        ]}

        with nc.Block() as blk:
            @blk.gpsimd
            def _(g):
                g.memset(ident[:], 0.0)
                g.affine_select(out=ident[:], in_=ident[:], compare_op=ALU.not_equal,
                                fill=1.0, base=0, pattern=[[-1, P]], channel_multiplier=1)
                g.memset(rs_mix[:], 1.0)
                g.affine_select(out=rs_mix[:], in_=rs_mix[:], compare_op=ALU.is_ge,
                                fill=-4.0, base=76, pattern=[[0, 1]], channel_multiplier=-1)
                g.memset(c_eps[:], EPS)
                g.memset(c_mhalf[:], -0.5)
                g.memset(c_inv16[:], 1.0 / 16.0)
                Sm["ld_aux"].inc(g.dma_start(out=svec[:], in_=svec_d), 16)

        with nc.Block() as blk:
            @blk.vector
            def _(v):
                Sm["ld_aux"].wait(v)
                for col, (src, sc) in enumerate([(0, 4.0), (1, 4.0), (2, 2.0),
                                                 (0, -2.0), (1, -2.0), (2, -1.0)]):
                    v.tensor_scalar(out=sderiv[:, col:col + 1], in0=svec[:, src:src + 1],
                                    scalar1=sc, scalar2=None, op0=ALU.mult)

        with ExitStack() as actx:
            aen = actx.enter_context
            rope_sb = aen(nc.sbuf_tensor("rope_sb", [P, NTB, 128], F32))
            Kst = aen(nc.sbuf_tensor("Kst", [P, 4, S], BF16))
            Vst = aen(nc.sbuf_tensor("Vst", [P, NTB, 512], BF16))

            def emit_kv_phase(rnd, first):
                with ExitStack() as pctx:
                    pen = pctx.enter_context
                    wk_sb = pen(nc.sbuf_tensor(f"wk_sb{rnd}", [P, NCH, 512], BF16))
                    wv_sb = pen(nc.sbuf_tensor(f"wv_sb{rnd}", [P, NCH, 512], BF16))
                    xn_dm = pen(nc.sbuf_tensor(f"xn_dm{rnd}", [P, D], BF16))
                    k_tm = pen(nc.sbuf_tensor(f"k_tm{rnd}", [P, 512], F32))
                    k_bf = pen(nc.sbuf_tensor(f"k_bf{rnd}", [P, 512], BF16))
                    k_rsc = pen(nc.sbuf_tensor(f"k_rsc{rnd}", [P, 128], F32))
                    pK = pen(nc.psum_tensor(f"pK{rnd}", [P, 512], F32))
                    pV = pen(nc.psum_tensor(f"pV{rnd}", [P, 512], F32))
                    p_ktr = pen(nc.psum_tensor(f"p_ktr{rnd}", [P, 1024], BF16))
                    if first:
                        x_tile = pen(nc.sbuf_tensor("x_tile", [P, 2, D], F32))
                        xn_tm = pen(nc.sbuf_tensor("xn_tm", [P, D], BF16))
                        p_tr = pen(nc.psum_tensor("p_tr", [P, 2, 1024], BF16))
                    else:
                        xn_blob3 = pen(nc.sbuf_tensor("xn_blob3", [P, 4, D], BF16))

                    ld_x_b = Sm["ld_x"].v
                    ld_blob_b = Sm["ld_blob"].v
                    kv_b = Sm["te_kv"].v
                    sekv_b = Sm["se_kv"].v
                    evac_b = Sm["dv_evac"].v
                    tr_b = Sm["te_tr"].v
                    blob_b = Sm["st_blob"].v
                    ktr_b = Sm["te_ktr"].v
                    dkt_b = Sm["dv_ktr"].v
                    kbf_b = Sm["dv_kbf"].v
                    var_b = Sm["dv_var"].v
                    rstd_b = Sm["se_rstd"].v
                    xn_b = Sm["se_xn"].v

                    def kv_rotary_dv(v, tb):
                        v.wait_ge(Sm["se_kv"].h, sekv_b + 2 * tb + 1)
                        cos = rope_sb[:, tb, 0:64]
                        sin = rope_sb[:, tb, 64:128]
                        for h in (0, 1):
                            kr = k_tm[:, 256 * h:256 * h + 64]
                            t1 = k_rsc[:, 0:64]
                            t2 = k_rsc[:, 64:128]
                            v.tensor_mul(t1, kr, cos)
                            Sm["dv_fl"].inc(v.tensor_mul(t2, kr, sin))
                            Sm["dv_fl"].wait(v)
                            ev = lambda a: a.rearrange("p (i two) -> p i two", two=2)[:, :, 0]
                            od = lambda a: a.rearrange("p (i two) -> p i two", two=2)[:, :, 1]
                            v.tensor_sub(ev(kr), ev(t1), od(t2))
                            Sm["dv_fl"].inc(v.tensor_add(od(kr), od(t1), ev(t2)))
                            Sm["dv_fl"].wait(v)
                        v.wait_ge(Sm["te_ktr"].h, max(0, ktr_b + tb))
                        Sm["dv_kbf"].inc(v.tensor_copy(k_bf[:], k_tm[:]))

                    def kv_kst_evac(v, tb):
                        v.wait_ge(Sm["te_ktr"].h, ktr_b + tb + 1)
                        Sm["dv_ktr"].inc(v.tensor_copy(
                            Kst[:, :, P * tb:P * (tb + 1)],
                            p_ktr.rearrange("p (u q) -> p u q", q=P)[:, 0:4, :]))

                    def kv_epi_scalar(s, tb):
                        s.wait_ge(Sm["te_kv"].h, kv_b + tb + 1)
                        s.wait_ge(Sm["dv_kbf"].h, max(0, kbf_b + tb))
                        Sm["se_kv"].inc(s.copy(k_tm[:], pK[:]))
                        Sm["se_kv"].inc(s.activation(
                            out=Vst[:, tb, :], in_=pV[:], func=AF.Copy))

                    def kv_ktr(t, tb):
                        t.wait_ge(Sm["dv_kbf"].h, kbf_b + tb + 1)
                        t.wait_ge(Sm["dv_ktr"].h, max(0, dkt_b + tb))
                        for u in range(4):
                            ti = t.transpose(p_ktr[:, 128 * u:128 * (u + 1)],
                                             k_bf[:, 128 * u:128 * (u + 1)], ident[:])
                            if u == 3:
                                Sm["te_ktr"].inc(ti)

                    with nc.Block() as blk:
                        @blk.gpsimd
                        def _(g):
                            if first:
                                Sm["ld_w"].inc(g.dma_start(
                                    out=wk_sb[:],
                                    in_=wk_d.rearrange("(c p) n -> p c n", p=P)[:, :, 0:512]), 16)
                                Sm["ld_w"].inc(g.dma_start(
                                    out=wv_sb[:],
                                    in_=wv_d.rearrange("(c p) n -> p c n", p=P)[:, :, 0:512]), 16)
                                if upto >= 3:
                                    Sm["cast_w"].inc(g.dma_start(
                                        out=wkbf1[:], in_=wk_d[:, 512:1024]), 16)
                                    Sm["cast_w"].inc(g.dma_start(
                                        out=wvbf1[:], in_=wv_d[:, 512:1024]), 16)
                                if upto >= 4:
                                    Sm["cast_w"].inc(g.dma_start(
                                        out=wqbf1[:], in_=wq_d[:, 512:1024]), 16)
                            else:
                                g.wait_ge(Sm["cast_w"].h, 32)
                                Sm["ld_w"].inc(g.dma_start(
                                    out=wk_sb[:],
                                    in_=wkbf1.rearrange("(c p) n -> p c n", p=P)), 16)
                                Sm["ld_w"].inc(g.dma_start(
                                    out=wv_sb[:],
                                    in_=wvbf1.rearrange("(c p) n -> p c n", p=P)), 16)

                        ld_w_t = Sm["ld_w"].v

                        @blk.sync
                        def _(sync):
                            if first:
                                Sm["ld_aux"].inc(sync.dma_start(
                                    out=rope_sb[:],
                                    in_=rope_d.rearrange("(tb p) f -> p tb f", p=P)), 16)
                                Sm["ld_x"].inc(sync.dma_start(
                                    out=x_tile[:, 0, :], in_=x_d[0:P, :]), 16)
                            else:
                                Sm["ld_blob"].inc(sync.dma_start(
                                    out=xn_blob3[:, 0, :], in_=xn_dram[0]), 16)
                            for tb in range(1, NTB):
                                if first:
                                    sync.wait_ge(Sm["se_xn"].h, max(0, xn_b + tb - 1))
                                    Sm["ld_x"].inc(sync.dma_start(
                                        out=x_tile[:, tb % 2, :],
                                        in_=x_d[P * tb:P * (tb + 1), :]), 16)
                                else:
                                    sync.wait_ge(Sm["te_kv"].h, max(0, kv_b + tb - 3))
                                    Sm["ld_blob"].inc(sync.dma_start(
                                        out=xn_blob3[:, tb % 4, :],
                                        in_=xn_dram[tb]), 16)

                        if first:
                            def ln_chain(v, tb):
                                xt = x_tile[:, tb % 2, :]
                                v.wait_ge(Sm["ld_x"].h, ld_x_b + 16 * (tb + 1))
                                if tb >= 1:
                                    v.wait_ge(Sm["te_tr"].h, tr_b + 8 * tb)
                                    v.wait_ge(Sm["se_rstd"].h, rstd_b + tb)
                                Sm["dv_fl"].inc(v.reduce_sum(st_sum[:], xt, axis=AX.X))
                                Sm["dv_fl"].inc(v.scalar_tensor_tensor(
                                    out=xn_tm[:], in0=xt, scalar=1.0, in1=xt,
                                    op0=ALU.mult, op1=ALU.mult, accum_out=st_ssq[:]))
                                Sm["dv_fl"].wait(v)
                                Sm["dv_fl"].inc(v.tensor_scalar(
                                    out=st_mean[:], in0=st_sum[:],
                                    scalar1=-1.0 / D, scalar2=None, op0=ALU.mult))
                                Sm["dv_fl"].wait(v)
                                Sm["dv_fl"].inc(v.tensor_mul(st_m2[:], st_mean[:], st_mean[:]))
                                Sm["dv_fl"].wait(v)
                                Sm["dv_var"].inc(v.scalar_tensor_tensor(
                                    out=st_var[:], in0=st_ssq[:], scalar=1.0 / D,
                                    in1=st_m2[:], op0=ALU.mult, op1=ALU.subtract))
                                v.wait_ge(Sm["se_rstd"].h, rstd_b + tb + 1)
                                Sm["dv_fl"].inc(v.tensor_copy(st_negmr[:], st_rstd[:]))
                                Sm["dv_fl"].wait(v)
                                Sm["se_xn"].inc(v.tensor_scalar(
                                    out=xn_tm[:], in0=xt, scalar1=st_mean[:],
                                    scalar2=st_negmr[:], op0=ALU.add, op1=ALU.mult))

                            @blk.vector
                            def _(v):
                                Sm["ld_aux"].wait(v)
                                ln_chain(v, 0)
                                for tb in range(NTB):
                                    for j in range(8):
                                        v.wait_ge(Sm["te_tr"].h, tr_b + 8 * tb + j + 1)
                                        if j == 0:
                                            v.wait_ge(Sm["te_kv"].h, max(0, kv_b + tb))
                                            v.wait_ge(Sm["st_blob"].h, max(0, blob_b + 16 * tb))
                                        Sm["dv_evac"].inc(v.tensor_copy(
                                            xn_dm[:, 512 * j:512 * (j + 1)],
                                            p_tr[:, j % 2, 0:512]))
                                    if tb + 1 < NTB:
                                        ln_chain(v, tb + 1)
                                    kv_rotary_dv(v, tb)
                                    kv_kst_evac(v, tb)

                            @blk.scalar
                            def _(s):
                                for tb in range(NTB):
                                    s.wait_ge(Sm["dv_var"].h, var_b + tb + 1)
                                    if tb >= 1:
                                        s.wait_ge(Sm["se_xn"].h, xn_b + tb)
                                    Sm["se_fl"].inc(s.activation(
                                        out=st_rstd[:], in_=st_var[:],
                                        func=AF.Ln, bias=c_eps[:]))
                                    Sm["se_fl"].wait(s)
                                    Sm["se_rstd"].inc(s.activation(
                                        out=st_rstd[:], in_=st_rstd[:], func=AF.Exp,
                                        scale=c_mhalf[:]))
                                    kv_epi_scalar(s, tb)
                                    s.wait_ge(Sm["dv_evac"].h, evac_b + 8 * (tb + 1))
                                    Sm["st_blob"].inc(s.dma_start(
                                        out=xn_dram[tb], in_=xn_dm[:]), 16)

                            @blk.tensor
                            def _(t):
                                for tb in range(NTB):
                                    t.wait_ge(Sm["se_xn"].h, xn_b + tb + 1)

                                    def tgroup(j):
                                        t.wait_ge(Sm["dv_evac"].h, max(0, evac_b + 8 * tb + j - 1))
                                        for u in range(4):
                                            c = 4 * j + u
                                            ti = t.transpose(
                                                p_tr[:, j % 2, 128 * u:128 * (u + 1)],
                                                xn_tm[:, 128 * c:128 * (c + 1)], ident[:])
                                            if u == 3:
                                                Sm["te_tr"].inc(ti)

                                    def mmgroup(j):
                                        t.wait_ge(Sm["dv_evac"].h, evac_b + 8 * tb + j + 1)
                                        if j == 0:
                                            t.wait_ge(Sm["ld_w"].h, ld_w_t)
                                            t.wait_ge(Sm["se_kv"].h, max(0, sekv_b + 2 * tb))
                                        for u in range(4):
                                            c = 4 * j + u
                                            t.matmul(pK[:], xn_dm[:, 128 * c:128 * (c + 1)],
                                                     wk_sb[:, c, :], start=(c == 0), stop=(c == NCH - 1))
                                            mi = t.matmul(pV[:], xn_dm[:, 128 * c:128 * (c + 1)],
                                                          wv_sb[:, c, :], start=(c == 0), stop=(c == NCH - 1))
                                            if c == NCH - 1:
                                                Sm["te_kv"].inc(mi)

                                    tgroup(0)
                                    tgroup(1)
                                    if tb >= 1:
                                        kv_ktr(t, tb - 1)
                                    for j in range(8):
                                        mmgroup(j)
                                        if j + 2 < 8:
                                            tgroup(j + 2)
                                    if tb == NTB - 1:
                                        kv_ktr(t, tb)
                        else:
                            @blk.tensor
                            def _(t):
                                for tb in range(NTB):
                                    src_dm = xn_blob3[:, tb % 4, :]
                                    t.wait_ge(Sm["ld_blob"].h, ld_blob_b + 16 * (tb + 1))
                                    if tb == 0:
                                        t.wait_ge(Sm["ld_w"].h, ld_w_t)
                                    t.wait_ge(Sm["se_kv"].h, max(0, sekv_b + 2 * tb))
                                    if tb >= 1:
                                        kv_ktr(t, tb - 1)
                                    for c in range(NCH):
                                        t.matmul(pK[:], src_dm[:, 128 * c:128 * (c + 1)],
                                                 wk_sb[:, c, :], start=(c == 0), stop=(c == NCH - 1))
                                        mi = t.matmul(pV[:], src_dm[:, 128 * c:128 * (c + 1)],
                                                      wv_sb[:, c, :], start=(c == 0), stop=(c == NCH - 1))
                                        if c == NCH - 1:
                                            Sm["te_kv"].inc(mi)
                                    if tb == NTB - 1:
                                        kv_ktr(t, tb)

                            @blk.scalar
                            def _(s):
                                for tb in range(NTB):
                                    kv_epi_scalar(s, tb)

                            @blk.vector
                            def _(v):
                                for tb in range(NTB):
                                    kv_rotary_dv(v, tb)
                                    kv_kst_evac(v, tb)

                    if first:
                        with nc.Block() as blk:
                            @blk.scalar
                            def _(s):
                                Sm["st_blob"].wait(s)

            def emit_flash_phase(rnd):
                with ExitStack() as pctx:
                    pen = pctx.enter_context
                    wq_sb = pen(nc.sbuf_tensor(f"wq_sb{rnd}", [P, NCH, 512], BF16))
                    xn_blob = pen(nc.sbuf_tensor(f"xn_blob{rnd}", [P, 3, D], BF16))
                    q_tm = pen(nc.sbuf_tensor(f"q_tm{rnd}", [P, 512], F32))
                    q_bf = pen(nc.sbuf_tensor(f"q_bf{rnd}", [P, 512], BF16))
                    q_fm = pen(nc.sbuf_tensor(f"q_fm{rnd}", [P, 4, P], BF16))
                    q_rsc = pen(nc.sbuf_tensor(f"q_rsc{rnd}", [P, 128], F32))
                    S_row = pen(nc.sbuf_tensor(f"S_row{rnd}", [P, S], F32))
                    P_row2 = pen(nc.sbuf_tensor(f"P_row{rnd}", [P, 2, S], BF16))
                    Pt_sb = pen(nc.sbuf_tensor(f"Pt_sb{rnd}", [P, 2, 512], BF16))
                    dmask = pen(nc.sbuf_tensor(f"dmask{rnd}", [P, 4, 512], BF16))
                    mxp = pen(nc.sbuf_tensor(f"mxp{rnd}", [P, 8], F32))
                    lp = pen(nc.sbuf_tensor(f"lp{rnd}", [P, 8], F32))
                    negm2 = pen(nc.sbuf_tensor(f"negm{rnd}", [P, 2], F32))
                    lsum = pen(nc.sbuf_tensor(f"lsum{rnd}", [P, 1], F32))
                    rtmp7 = pen(nc.sbuf_tensor(f"rtmp7_{rnd}", [P, 1], F32))
                    linv = pen(nc.sbuf_tensor(f"linv{rnd}", [P, 1], F32))
                    astage = pen(nc.sbuf_tensor(f"astage{rnd}", [P, 2, 512], BF16))
                    pQ = pen(nc.psum_tensor(f"pQ{rnd}", [P, 512], F32))
                    pS0 = pen(nc.psum_tensor(f"pS0{rnd}", [P, 512], F32))
                    pS1 = pen(nc.psum_tensor(f"pS1{rnd}", [P, 512], F32))
                    p_qtr = pen(nc.psum_tensor(f"p_qtr{rnd}", [P, 1024], BF16))
                    p_pt0 = pen(nc.psum_tensor(f"p_pt0{rnd}", [P, 1024], BF16))
                    p_pt1 = pen(nc.psum_tensor(f"p_pt1{rnd}", [P, 1024], BF16))
                    pO0 = pen(nc.psum_tensor(f"pO0{rnd}", [P, 512], F32))
                    pO1 = pen(nc.psum_tensor(f"pO1{rnd}", [P, 512], F32))
                    pS = [pS0, pS1]
                    p_pt = [p_pt0, p_pt1]
                    pO = [pO0, pO1]

                    Bq = Sm["te_q"].v
                    Bseq = Sm["se_q"].v
                    Bdvq = Sm["dv_q"].v
                    Bqtr = Sm["te_qtr"].v
                    Bqfm = Sm["dv_qfm"].v
                    Bnegm = Sm["dv_negm"].v
                    Bexp = Sm["se_exp"].v
                    Bl = Sm["dv_l"].v
                    Bo = Sm["dv_o"].v
                    Battn = Sm["st_attn"].v
                    Bblob = Sm["ld_blob"].v
                    NKT = [qb // 4 + 1 for qb in range(NTB)]
                    TES = {}
                    cum = 0
                    for qb in range(NTB):
                        for h in (0, 1):
                            TES[(qb, h)] = cum
                            cum += NKT[qb]
                    TES_END = cum
                    Bs = Sm["te_s"].v
                    Bsev = Sm["dv_sevac"].v
                    Bpt = Sm["te_pt"].v
                    Bptv = Sm["dv_ptevac"].v
                    Bpv = Sm["te_pv"].v
                    linv_h2 = [linv[:], lsum[:]]

                    with nc.Block() as blk:
                        @blk.gpsimd
                        def _(g):
                            for a in range(4):
                                g.memset(dmask[:, a, :], 1.0)
                                ai = g.affine_select(out=dmask[:, a, :], in_=dmask[:, a, :],
                                                     compare_op=ALU.is_ge, fill=0.0, base=128 * a,
                                                     pattern=[[-1, 512]], channel_multiplier=1)
                                if a == 3:
                                    Sm["gp_fl"].inc(ai)
                            if rnd == 0:
                                Sm["ld_w"].inc(g.dma_start(
                                    out=wq_sb[:],
                                    in_=wq_d.rearrange("(c p) n -> p c n", p=P)[:, :, 0:512]), 16)
                                if upto >= 6:
                                    for c4 in range(4):
                                        Sm["cast_w"].inc(g.dma_start(
                                            out=wobf[:, 1024 * c4:1024 * (c4 + 1)],
                                            in_=wo_d[:, 1024 * c4:1024 * (c4 + 1)]), 16)
                            else:
                                g.wait_ge(Sm["cast_w"].h, 48)
                                Sm["ld_w"].inc(g.dma_start(
                                    out=wq_sb[:],
                                    in_=wqbf1.rearrange("(c p) n -> p c n", p=P)), 16)

                        ld_w_t = Sm["ld_w"].v
                        gp_t = Sm["gp_fl"].v

                        @blk.sync
                        def _(sync):
                            Sm["ld_blob"].inc(sync.dma_start(
                                out=xn_blob[:, 0, :], in_=xn_dram[0]), 16)
                            for qb in range(1, NTB):
                                sync.wait_ge(Sm["te_q"].h, max(0, Bq + qb - 2))
                                Sm["ld_blob"].inc(sync.dma_start(
                                    out=xn_blob[:, qb % 3, :],
                                    in_=xn_dram[qb]), 16)

                        @blk.tensor
                        def _(t):
                            def ptpv(qb, h):
                                nkt = NKT[qb]
                                t.wait_ge(Sm["dv_l"].h, Bl + 2 * qb + h + 1)
                                t.wait_ge(Sm["dv_o"].h, max(0, Bo + 2 * qb + h - 1))
                                t.wait_ge(Sm["dv_ptevac"].h, Bptv + TES[(qb, h)])

                                def ptr_group(kt):
                                    if kt >= 2:
                                        t.wait_ge(Sm["dv_ptevac"].h, Bptv + TES[(qb, h)] + kt - 1)
                                    for u in range(4):
                                        ti = t.transpose(
                                            p_pt[kt % 2][:, 128 * u:128 * (u + 1)],
                                            P_row2[:, h, 512 * kt + 128 * u:512 * kt + 128 * (u + 1)],
                                            ident[:])
                                        if u == 3:
                                            Sm["te_pt"].inc(ti)

                                def pv_group(kt):
                                    t.wait_ge(Sm["dv_ptevac"].h, Bptv + TES[(qb, h)] + kt + 1)
                                    for u in range(4):
                                        mi = t.matmul(
                                            pO[h][:, 0:256],
                                            Pt_sb[:, kt % 2, 128 * u:128 * (u + 1)],
                                            Vst[:, 4 * kt + u, 256 * h:256 * (h + 1)],
                                            start=(kt == 0 and u == 0),
                                            stop=(kt == nkt - 1 and u == 3))
                                    Sm["te_pv"].inc(mi)

                                ptr_group(0)
                                if nkt > 1:
                                    ptr_group(1)
                                for kt in range(nkt):
                                    pv_group(kt)
                                    if kt + 2 < nkt:
                                        ptr_group(kt + 2)

                            for qb in range(NTB):
                                nkt = NKT[qb]
                                t.wait_ge(Sm["ld_blob"].h, Bblob + 16 * (qb + 1))
                                if qb == 0:
                                    t.wait_ge(Sm["ld_w"].h, ld_w_t)
                                t.wait_ge(Sm["se_q"].h, max(0, Bseq + qb))
                                xb = xn_blob[:, qb % 3, :]
                                for c in range(NCH):
                                    mi = t.matmul(pQ[:], xb[:, 128 * c:128 * (c + 1)],
                                                  wq_sb[:, c, :], start=(c == 0), stop=(c == NCH - 1))
                                Sm["te_q"].inc(mi)
                                t.wait_ge(Sm["dv_q"].h, Bdvq + qb + 1)
                                t.wait_ge(Sm["dv_qfm"].h, max(0, Bqfm + qb))
                                for u in range(4):
                                    ti = t.transpose(p_qtr[:, 128 * u:128 * (u + 1)],
                                                     q_bf[:, 128 * u:128 * (u + 1)], ident[:])
                                    if u == 3:
                                        Sm["te_qtr"].inc(ti)
                                t.wait_ge(Sm["dv_qfm"].h, Bqfm + qb + 1)
                                for h in (0, 1):
                                    t.wait_ge(Sm["dv_sevac"].h, Bsev + TES[(qb, h)])
                                    for kt in range(nkt):
                                        if kt >= 2:
                                            t.wait_ge(Sm["dv_sevac"].h, Bsev + TES[(qb, h)] + kt - 1)
                                        bank = pS[kt % 2]
                                        t.matmul(bank[:], q_fm[:, 2 * h, :],
                                                 Kst[:, 2 * h, 512 * kt:512 * (kt + 1)],
                                                 start=True, stop=False)
                                        mi = t.matmul(bank[:], q_fm[:, 2 * h + 1, :],
                                                      Kst[:, 2 * h + 1, 512 * kt:512 * (kt + 1)],
                                                      start=False, stop=True)
                                        Sm["te_s"].inc(mi)
                                ptpv(qb, 0)
                                ptpv(qb, 1)

                        @blk.scalar
                        def _(s):
                            def attn_store(qb):
                                s.wait_ge(Sm["dv_o"].h, Bo + 2 * (qb + 1))
                                Sm["st_attn"].inc(s.dma_start(
                                    out=attn_myr[rnd][P * qb:P * (qb + 1), :],
                                    in_=astage[:, qb % 2, :]), 16)

                            for qb in range(NTB):
                                nkt = NKT[qb]
                                s.wait_ge(Sm["te_q"].h, Bq + qb + 1)
                                s.wait_ge(Sm["dv_q"].h, max(0, Bdvq + qb))
                                Sm["se_q"].inc(s.activation(
                                    out=q_tm[:], in_=pQ[:], func=AF.Identity, scale=c_inv16[:]))
                                if qb >= 1:
                                    attn_store(qb - 1)
                                for h in (0, 1):
                                    s.wait_ge(Sm["dv_negm"].h, Bnegm + 2 * qb + h + 1)
                                    s.wait_ge(Sm["dv_l"].h, max(0, Bl + 2 * qb + h))
                                    if qb >= 1:
                                        prev_done = TES[(qb - 1, h)] + NKT[qb - 1]
                                        s.wait_ge(Sm["te_pt"].h, Bpt + prev_done)
                                    for kt in range(nkt):
                                        ai = s.activation(
                                            out=P_row2[:, h, 512 * kt:512 * (kt + 1)],
                                            in_=S_row[:, 512 * kt:512 * (kt + 1)],
                                            func=AF.Exp, bias=negm2[:, h:h + 1],
                                            accum_out=lp[:, kt:kt + 1])
                                    Sm["se_exp"].inc(ai)
                                for h in (0, 1):
                                    s.wait_ge(Sm["te_pv"].h, Bpv + TES[(qb, h)] + nkt)
                                    s.wait_ge(Sm["dv_l"].h, Bl + 2 * qb + h + 1)
                                    if h == 0:
                                        s.wait_ge(Sm["st_attn"].h, max(0, Battn + 16 * (qb - 1)))
                                    Sm["dv_o"].inc(s.activation(
                                        out=astage[:, qb % 2, 256 * h:256 * (h + 1)],
                                        in_=pO[h][:, 0:256], func=AF.Identity,
                                        scale=linv_h2[h]))
                                if qb == NTB - 1:
                                    attn_store(qb)

                        @blk.vector
                        def _(v):
                            for qb in range(NTB):
                                nkt = NKT[qb]
                                if qb == 0 and rnd == 0:
                                    Sm["ld_aux"].wait(v)
                                if qb == 0:
                                    v.wait_ge(Sm["gp_fl"].h, gp_t)
                                v.wait_ge(Sm["se_q"].h, Bseq + qb + 1)
                                cos = rope_sb[:, qb, 0:64]
                                sin = rope_sb[:, qb, 64:128]
                                for h in (0, 1):
                                    qr = q_tm[:, 256 * h:256 * h + 64]
                                    t1 = q_rsc[:, 0:64]
                                    t2 = q_rsc[:, 64:128]
                                    v.tensor_mul(t1, qr, cos)
                                    Sm["dv_fl"].inc(v.tensor_mul(t2, qr, sin))
                                    Sm["dv_fl"].wait(v)
                                    ev = lambda a: a.rearrange("p (i two) -> p i two", two=2)[:, :, 0]
                                    od = lambda a: a.rearrange("p (i two) -> p i two", two=2)[:, :, 1]
                                    v.tensor_sub(ev(qr), ev(t1), od(t2))
                                    Sm["dv_fl"].inc(v.tensor_add(od(qr), od(t1), ev(t2)))
                                    Sm["dv_fl"].wait(v)
                                v.wait_ge(Sm["te_qtr"].h, max(0, Bqtr + qb))
                                Sm["dv_q"].inc(v.tensor_copy(q_bf[:], q_tm[:]))
                                v.wait_ge(Sm["te_qtr"].h, Bqtr + qb + 1)
                                v.wait_ge(Sm["te_s"].h, Bs + TES[(qb, 0)])
                                Sm["dv_qfm"].inc(v.tensor_copy(
                                    q_fm.rearrange("p u q -> p (u q)"),
                                    p_qtr[:, 0:512]))
                                for h in (0, 1):
                                    v.wait_ge(Sm["se_exp"].h, max(0, Bexp + 2 * qb + h))
                                    for kt in range(nkt):
                                        v.wait_ge(Sm["te_s"].h, Bs + TES[(qb, h)] + kt + 1)
                                        Sm["dv_sevac"].inc(v.tensor_scalar(
                                            out=S_row[:, 512 * kt:512 * (kt + 1)],
                                            in0=pS[kt % 2][:], scalar1=0.0, scalar2=None,
                                            op0=ALU.add, op1=ALU.max,
                                            accum_out=mxp[:, kt:kt + 1]))
                                    Sm["dv_sevac"].wait(v)
                                    Sm["dv_negm"].inc(v.reduce_max(
                                        negm2[:, h:h + 1], mxp[:, 0:nkt], axis=AX.X, negate=True))
                                    v.wait_ge(Sm["se_exp"].h, Bexp + 2 * qb + h + 1)
                                    dtile = P_row2[:, h, 512 * (nkt - 1):512 * nkt]
                                    Sm["dv_fl"].inc(v.scalar_tensor_tensor(
                                        out=dtile, in0=dtile, scalar=1.0,
                                        in1=dmask[:, qb % 4, :], op0=ALU.mult, op1=ALU.mult,
                                        accum_out=lp[:, nkt - 1:nkt]))
                                    Sm["dv_fl"].wait(v)
                                    Sm["dv_fl"].inc(v.reduce_sum(rtmp7[:, 0:1], lp[:, 0:nkt], axis=AX.X))
                                    Sm["dv_fl"].wait(v)
                                    v.wait_ge(Sm["dv_o"].h, max(0, Bo + 2 * qb + h - 1))
                                    Sm["dv_l"].inc(v.reciprocal(linv_h2[h], rtmp7[:, 0:1]))
                                for h in (0, 1):
                                    v.wait_ge(Sm["te_pv"].h, Bpv + TES[(qb, h)])
                                    for kt in range(NKT[qb]):
                                        v.wait_ge(Sm["te_pt"].h, Bpt + TES[(qb, h)] + kt + 1)
                                        if kt >= 2:
                                            v.wait_ge(Sm["te_pv"].h, Bpv + TES[(qb, h)] + kt - 1)
                                        Sm["dv_ptevac"].inc(v.tensor_copy(
                                            Pt_sb[:, kt % 2, :],
                                            p_pt[kt % 2][:, 0:512]))

                    with nc.Block() as blk:
                        @blk.scalar
                        def _(s):
                            Sm["st_attn"].wait(s)

            def emit_gather(rnd):
                with nc.Block() as blk:
                    @blk.gpsimd
                    def _(g):
                        Sm["st_attn"].wait(g)
                        Sm["cc"].inc(g.collective_compute(
                            "AllGather", ALU.bypass,
                            replica_groups=[[0, 1, 2, 3], [4, 5, 6, 7]],
                            ins=[attn_myr[rnd]],
                            outs=[attn_gr[rnd]]), 1)

            emit_kv_phase(0, True)
            if "xn" in taps:
                _emit_dram_tap(nc, Sm, tap_outs["xn"],
                               xn_dram.rearrange("tb p d -> (tb p) d"))
            if "kst" in taps:
                _emit_sbuf_tap(nc, Sm, tap_outs["kst"], Kst.rearrange("p u s -> p (u s)"))
            if "vst" in taps:
                _emit_sbuf_tap(nc, Sm, tap_outs["vst"], Vst.rearrange("p tb n -> p (tb n)"))
            if upto >= 2:
                emit_flash_phase(0)
            if upto >= 5:
                emit_gather(0)
            if upto >= 3:
                emit_kv_phase(1, False)
            if upto >= 4:
                emit_flash_phase(1)
            if upto >= 5:
                emit_gather(1)

        if upto >= 6:
            with ExitStack() as pctx:
                pen = pctx.enter_context
                attnT = pen(nc.sbuf_tensor("attnT", [P, NTBL, D], BF16))
                wo_sb = pen(nc.sbuf_tensor("wo_sb", [P, 3, 16, 512], BF16))
                xr_sb = pen(nc.sbuf_tensor("xr_sb", [P, NTBL, 512], F32))
                x2st = pen(nc.sbuf_tensor("x2st", [P, 2, 512], F32))
                wo_b = Sm["ld_wo"].v
                xr_b = Sm["ld_x2r"].v
                x2_b = Sm["st_x2"].v

                def wo_src(step):
                    o, ph = step // 2, step % 2
                    return wobf.rearrange("(c p) n -> p c n", p=P)[
                        :, 16 * ph:16 * (ph + 1), 512 * o:512 * (o + 1)]

                with nc.Block() as blk:
                    @blk.gpsimd
                    def _(g):
                        g.wait_ge(Sm["cast_w"].h, 112)
                        for st_ in range(3):
                            Sm["ld_wo"].inc(g.dma_start(
                                out=wo_sb[:, st_, :, :], in_=wo_src(st_)), 16)

                    @blk.sync
                    def _(sync):
                        Sm["ld_x2r"].inc(sync.dma_start(
                            out=xr_sb[:],
                            in_=xres_d.rearrange("(tb p) n -> p tb n", p=P)[:, :, 0:512]), 16)

                with ExitStack() as sctx:
                    sen = sctx.enter_context
                    attn_tm = sen(nc.sbuf_tensor("attn_tm", [P, NTBL, D], BF16))
                    p_at = sen(nc.psum_tensor("p_at", [P, 2, 1024], BF16))
                    at_b = Sm["ld_at"].v
                    at2_b = Sm["ld_at2"].v
                    with nc.Block() as blk:
                        @blk.gpsimd
                        def _(g):
                            g.wait_ge(Sm["cc"].h, Sm["cc"].v - 1)
                            gm = g.partition_id() % 4
                            big = attn_gr[0].rearrange("(r s) n -> s r n", r=4)
                            for tb_ in range(NTBL):
                                Sm["ld_at"].inc(g.dma_start(
                                    out=attn_tm[:, tb_, :].rearrange("p (r n) -> p r n", r=4)[:, :, 0:512],
                                    in_=big[bass.ds(gm * TSL + P * tb_, P), :, :]), 16)

                        @blk.scalar
                        def _(sc):
                            Sm["cc"].wait(sc)
                            gm2 = sc.partition_id() % 4
                            big2 = attn_gr[1].rearrange("(r s) n -> s r n", r=4)
                            for tb_ in range(NTBL):
                                Sm["ld_at2"].inc(sc.dma_start(
                                    out=attn_tm[:, tb_, :].rearrange("p (r n) -> p r n", r=4)[:, :, 512:1024],
                                    in_=big2[bass.ds(gm2 * TSL + P * tb_, P), :, :]), 16)

                    JL = [0, 2, 4, 6, 1, 3, 5, 7]
                    for tblk in range(NTBL):
                        with nc.Block() as blk:
                            teat_b = Sm["te_at"].v
                            dvat_b = Sm["dv_at"].v

                            @blk.tensor
                            def _(t):
                                for idx, j in enumerate(JL):
                                    if idx == 0:
                                        t.wait_ge(Sm["ld_at"].h, at_b + 16 * (tblk + 1))
                                    if idx == 4:
                                        t.wait_ge(Sm["ld_at2"].h, at2_b + 16 * (tblk + 1))
                                    t.wait_ge(Sm["dv_at"].h, max(0, dvat_b + idx - 1))
                                    for u in range(4):
                                        c = 4 * j + u
                                        ti = t.transpose(
                                            p_at[:, idx % 2, 128 * u:128 * (u + 1)],
                                            attn_tm[:, tblk, 128 * c:128 * (c + 1)], ident[:])
                                        if u == 3:
                                            Sm["te_at"].inc(ti)

                            @blk.vector
                            def _(v):
                                for idx, j in enumerate(JL):
                                    v.wait_ge(Sm["te_at"].h, teat_b + idx + 1)
                                    Sm["dv_at"].inc(v.tensor_copy(
                                        attnT[:, tblk, 512 * j:512 * (j + 1)],
                                        p_at[:, idx % 2, 0:512]))

                with ExitStack() as sctx:
                    sen = sctx.enter_context
                    pF6 = [sen(nc.psum_tensor(f"pF6_{i}", [P, 512], F32)) for i in range(NTBL)]
                    for o in range(8):
                        for ph in range(2):
                            step = 2 * o + ph
                            with nc.Block() as blk:
                                teo_b = Sm["te_o"].v

                                @blk.tensor
                                def _(t):
                                    Sm["ld_wo"].wait(t, wo_b + 16 * (step + 1))
                                    for cl in range(16):
                                        c = 16 * ph + cl
                                        for tblk in range(NTBL):
                                            mi = t.matmul(
                                                pF6[tblk][:],
                                                attnT[:, tblk, 128 * c:128 * (c + 1)],
                                                wo_sb[:, step % 3, cl, :],
                                                start=(c == 0), stop=(c == NCH - 1))
                                            if c == NCH - 1 and tblk < NTBL - 1:
                                                Sm["te_o"].inc(mi)
                                            if cl == 15 and tblk == NTBL - 1:
                                                Sm["te_oh"].inc(mi)

                                if ph == 1:
                                    dvx2_b = Sm["dv_x2"].v

                                    @blk.vector
                                    def _(v):
                                        Sm["ld_x2r"].wait(v, xr_b + 16 * (o + 1))
                                        for tblk in range(NTBL):
                                            if tblk < NTBL - 1:
                                                Sm["te_o"].wait(v, teo_b + tblk + 1)
                                            else:
                                                Sm["te_oh"].wait(v)
                                            Sm["st_x2"].wait(v, x2_b + 16 * (8 * o + tblk - 1))
                                            Sm["dv_x2"].inc(v.tensor_add(
                                                x2st[:, tblk % 2, :], pF6[tblk][:],
                                                xr_sb[:, tblk, :]))

                                    @blk.scalar
                                    def _(sc):
                                        for tblk in range(NTBL):
                                            Sm["dv_x2"].wait(sc, dvx2_b + tblk + 1)
                                            Sm["st_x2"].inc(sc.dma_start(
                                                out=x2_dram[:, 512 * o:512 * (o + 1)]
                                                .rearrange("(tb p) n -> p tb n", p=P)[:, tblk, :],
                                                in_=x2st[:, tblk % 2, :]), 16)

                                    @blk.sync
                                    def _(sync):
                                        if o + 1 < 8:
                                            sync.wait_ge(Sm["dv_x2"].h, dvx2_b + 8)
                                            Sm["ld_x2r"].inc(sync.dma_start(
                                                out=xr_sb[:],
                                                in_=xres_d.rearrange("(tb p) n -> p tb n", p=P)[:, :, 512 * (o + 1):512 * (o + 2)]), 16)

                                @blk.gpsimd
                                def _(g):
                                    if step + 3 < 16:
                                        Sm["te_oh"].wait(g)
                                        Sm["ld_wo"].inc(g.dma_start(
                                            out=wo_sb[:, (step + 3) % 3, :, :],
                                            in_=wo_src(step + 3)), 16)

                    with nc.Block() as blk:
                        @blk.gpsimd
                        def _(g):
                            Sm["st_x2"].wait(g)
            if "x2" in taps:
                _emit_dram_tap(nc, Sm, tap_outs["x2"], x2_dram)

        if upto >= 7:
            with ExitStack() as pctx:
                pen = pctx.enter_context
                bufA = pen(nc.sbuf_tensor("bufA", [P, NTBL, D], BF16))
                bufB = pen(nc.sbuf_tensor("bufB", [P, NTBL, D], BF16))
                rs_prev = pen(nc.sbuf_tensor("rs_prev", [P, NTBL], F32))
                rs_next = pen(nc.sbuf_tensor("rs_next", [P, NTBL], F32))
                bL = pen(nc.sbuf_tensor("bL", [P, NTBL], F32))
                hp = pen(nc.sbuf_tensor("hp", [P, NTBL, 16], F32))
                rtmp = pen(nc.sbuf_tensor("rtmp", [P, 2], F32))

                with ExitStack() as sctx:
                    sen = sctx.enter_context
                    x2row = sen(nc.sbuf_tensor("x2row", [P, 2, D], F32))
                    x2n_tm = sen(nc.sbuf_tensor("x2n_tm", [P, D], BF16))
                    p_tr7 = sen(nc.psum_tensor("p_tr7", [P, 2, 1024], BF16))
                    x2r_b = Sm["ld_x2r"].v
                    var_b = Sm["dv_var"].v
                    rstd_b = Sm["se_rstd"].v
                    xn_b = Sm["se_xn"].v
                    tr_b = Sm["te_tr"].v
                    evac_b = Sm["dv_evac"].v
                    sq_b = Sm["se_sq"].v

                    with nc.Block() as blk:
                        @blk.sync
                        def _(sync):
                            Sm["st_x2"].wait(sync)
                            Sm["ld_x2r"].inc(sync.dma_start(
                                out=x2row[:, 0, :], in_=x2_dram[0:P, :]), 16)

                    with nc.Block() as blk:
                        @blk.sync
                        def _(sync):
                            for tblk in range(1, NTBL):
                                sync.wait_ge(Sm["se_xn"].h, max(0, xn_b + tblk - 1))
                                Sm["ld_x2r"].inc(sync.dma_start(
                                    out=x2row[:, tblk % 2, :],
                                    in_=x2_dram[P * tblk:P * (tblk + 1), :]), 16)

                        def ln2_chain(v, tblk):
                            xt = x2row[:, tblk % 2, :]
                            v.wait_ge(Sm["ld_x2r"].h, x2r_b + 16 * (tblk + 1))
                            if tblk >= 1:
                                v.wait_ge(Sm["te_tr"].h, tr_b + 8 * tblk)
                                v.wait_ge(Sm["se_sq"].h, sq_b + tblk)
                                v.wait_ge(Sm["se_rstd"].h, rstd_b + tblk)
                            Sm["dv_fl"].inc(v.reduce_sum(st_sum[:], xt, axis=AX.X))
                            Sm["dv_fl"].inc(v.scalar_tensor_tensor(
                                out=x2n_tm[:], in0=xt, scalar=1.0, in1=xt,
                                op0=ALU.mult, op1=ALU.mult, accum_out=st_ssq[:]))
                            Sm["dv_fl"].wait(v)
                            Sm["dv_fl"].inc(v.tensor_scalar(
                                out=st_mean[:], in0=st_sum[:],
                                scalar1=-1.0 / D, scalar2=None, op0=ALU.mult))
                            Sm["dv_fl"].wait(v)
                            Sm["dv_fl"].inc(v.tensor_mul(st_m2[:], st_mean[:], st_mean[:]))
                            Sm["dv_fl"].wait(v)
                            Sm["dv_var"].inc(v.scalar_tensor_tensor(
                                out=st_var[:], in0=st_ssq[:], scalar=1.0 / D,
                                in1=st_m2[:], op0=ALU.mult, op1=ALU.subtract))
                            v.wait_ge(Sm["se_rstd"].h, rstd_b + tblk + 1)
                            Sm["dv_fl"].inc(v.tensor_copy(st_negmr[:], st_rstd[:]))
                            Sm["dv_fl"].wait(v)
                            Sm["se_xn"].inc(v.tensor_scalar(
                                out=x2n_tm[:], in0=xt, scalar1=st_mean[:],
                                scalar2=st_negmr[:], op0=ALU.add, op1=ALU.mult))

                        @blk.vector
                        def _(v):
                            ln2_chain(v, 0)
                            for tblk in range(NTBL):
                                for j in range(8):
                                    v.wait_ge(Sm["te_tr"].h, tr_b + 8 * tblk + j + 1)
                                    Sm["dv_evac"].inc(v.tensor_copy(
                                        bufA[:, tblk, 512 * j:512 * (j + 1)],
                                        p_tr7[:, j % 2, 0:512]))
                                if tblk + 1 < NTBL:
                                    ln2_chain(v, tblk + 1)

                        @blk.scalar
                        def _(s):
                            for tblk in range(NTBL):
                                s.wait_ge(Sm["dv_var"].h, var_b + tblk + 1)
                                if tblk >= 1:
                                    s.wait_ge(Sm["se_xn"].h, xn_b + tblk)
                                Sm["se_fl"].inc(s.activation(
                                    out=st_rstd[:], in_=st_var[:], func=AF.Ln, bias=c_eps[:]))
                                Sm["se_fl"].wait(s)
                                Sm["se_rstd"].inc(s.activation(
                                    out=st_rstd[:], in_=st_rstd[:], func=AF.Exp, scale=c_mhalf[:]))
                                s.wait_ge(Sm["se_xn"].h, xn_b + tblk + 1)
                                Sm["se_sq"].inc(s.activation(
                                    out=x2n_tm[:], in_=x2n_tm[:], func=AF.Identity,
                                    accum_out=rs_prev[:, tblk:tblk + 1]))

                        @blk.tensor
                        def _(t):
                            for tblk in range(NTBL):
                                t.wait_ge(Sm["se_xn"].h, xn_b + tblk + 1)
                                for j in range(8):
                                    t.wait_ge(Sm["dv_evac"].h, max(0, evac_b + 8 * tblk + j - 1))
                                    for u in range(4):
                                        c = 4 * j + u
                                        ti = t.transpose(
                                            p_tr7[:, j % 2, 128 * u:128 * (u + 1)],
                                            x2n_tm[:, 128 * c:128 * (c + 1)], ident[:])
                                        if u == 3:
                                            Sm["te_tr"].inc(ti)

                with ExitStack() as sctx:
                    sen = sctx.enter_context
                    wm_sb = sen(nc.sbuf_tensor("wm_sb", [P, 2, 16, 512], BF16))
                    x2c = sen(nc.sbuf_tensor("x2c", [P, NTBL, 512], F32))
                    ostage = sen(nc.sbuf_tensor("ostage", [P, 2, 512], F32))
                    otmp = sen(nc.sbuf_tensor("otmp", [P, 512], F32))

                    def mask_src(L, step):
                        o, ph = step // 2, step % 2
                        return km_d[L].rearrange("(c p) n -> p c n", p=P)[
                            :, 16 * ph:16 * (ph + 1), 512 * o:512 * (o + 1)]

                    def emit_layer(L, src_buf, dst_buf, pFF):
                        msk_b = Sm["ld_msk"].v
                        with nc.Block() as blk:
                            if L < 2:
                                @blk.scalar
                                def _(s):
                                    if L == 1:
                                        Sm["dv_rs"].wait(s)
                                        s.copy(rs_prev[:], rs_next[:])
                                    for tblk in range(NTBL):
                                        s.activation(out=bL[:, tblk:tblk + 1],
                                                     in_=rs_prev[:, tblk:tblk + 1],
                                                     func=AF.Identity,
                                                     scale=sderiv[:, 3 + L:4 + L])
                            else:
                                @blk.vector
                                def _(v):
                                    Sm["dv_rs"].wait(v)
                                    for tblk in range(NTBL):
                                        v.tensor_mul(bL[:, tblk:tblk + 1],
                                                     rs_next[:, tblk:tblk + 1],
                                                     sderiv[:, 5:6])

                            @blk.gpsimd
                            def _(g):
                                for st_ in range(2):
                                    Sm["ld_msk"].inc(g.dma_start(
                                        out=wm_sb[:, st_ % 2, :, :], in_=mask_src(L, st_)), 16)

                            if L == 2:
                                @blk.sync
                                def _(sync):
                                    Sm["ld_xc"].inc(sync.dma_start(
                                        out=x2c[:],
                                        in_=x2_dram.rearrange("(tb p) n -> p tb n", p=P)[:, :, 0:512]), 16)

                        xc_b = Sm["ld_xc"].v - (16 if L == 2 else 0)
                        out_b = Sm["st_out"].v
                        for o in range(8):
                            for ph in range(2):
                                step = 2 * o + ph
                                with nc.Block() as blk:
                                    teff_b = Sm["te_ff"].v
                                    if L >= 1 and ph == 1:
                                        @blk.vector
                                        def _(v):
                                            Sm["ld_msk"].wait(v, msk_b + 16 * (step + 1))
                                            buf = step % 2
                                            v.tensor_scalar(
                                                out=wm_sb[:, buf, 9, :], in0=wm_sb[:, buf, 9, :],
                                                scalar1=rs_mix[:], scalar2=None, op0=ALU.mult)
                                            Sm["dv_wfix"].inc(v.tensor_scalar(
                                                out=wm_sb[:, buf, 10:16, :].rearrange("p c n -> p (c n)"),
                                                in0=wm_sb[:, buf, 10:16, :].rearrange("p c n -> p (c n)"),
                                                scalar1=-4.0, scalar2=None, op0=ALU.mult))

                                    dvwf = Sm["dv_wfix"].v

                                    @blk.tensor
                                    def _(t):
                                        Sm["ld_msk"].wait(t, msk_b + 16 * (step + 1))
                                        if L >= 1 and ph == 1:
                                            Sm["dv_wfix"].wait(t, dvwf)
                                        for cl in range(16):
                                            c = 16 * ph + cl
                                            for tblk in range(NTBL):
                                                mi = t.matmul(
                                                    pFF[tblk][:],
                                                    src_buf[:, tblk, 128 * c:128 * (c + 1)],
                                                    wm_sb[:, step % 2, cl, :],
                                                    start=(c == 0), stop=(c == NCH - 1))
                                                if c == NCH - 1 and tblk < NTBL - 1:
                                                    Sm["te_ff"].inc(mi)
                                                if cl == 15 and tblk == NTBL - 1:
                                                    Sm["te_fh"].inc(mi)

                                    if ph == 1:
                                        if L < 2:
                                            @blk.scalar
                                            def _(s):
                                                for tblk in range(NTBL):
                                                    if tblk < NTBL - 1:
                                                        Sm["te_ff"].wait(s, teff_b + tblk + 1)
                                                    else:
                                                        Sm["te_fh"].wait(s)
                                                    scale_ap = sderiv[:, L:L + 1]
                                                    bias_ap = bL[:, tblk:tblk + 1]
                                                    if o == 6:
                                                        si = s.activation(
                                                            out=dst_buf[:, tblk, 3072:NE],
                                                            in_=pFF[tblk][:, 0:NE - 3072],
                                                            func=AF.Relu, scale=scale_ap, bias=bias_ap,
                                                            accum_out=hp[:, tblk, 6:7])
                                                        si = s.activation(
                                                            out=dst_buf[:, tblk, NE:3584],
                                                            in_=pFF[tblk][:, NE - 3072:512],
                                                            func=AF.Relu, scale=scale_ap, bias=bias_ap,
                                                            accum_out=hp[:, tblk, 7:8])
                                                    else:
                                                        col = o if o < 6 else 8
                                                        si = s.activation(
                                                            out=dst_buf[:, tblk, 512 * o:512 * (o + 1)],
                                                            in_=pFF[tblk][:],
                                                            func=AF.Relu, scale=scale_ap, bias=bias_ap,
                                                            accum_out=hp[:, tblk, col:col + 1])
                                                    Sm["se_ff"].inc(si)
                                        else:
                                            dvff_b = Sm["dv_ff"].v

                                            @blk.vector
                                            def _(v):
                                                Sm["ld_xc"].wait(v, xc_b + 16 * (o + 1))
                                                for tblk in range(NTBL):
                                                    if tblk < NTBL - 1:
                                                        Sm["te_ff"].wait(v, teff_b + tblk + 1)
                                                    else:
                                                        Sm["te_fh"].wait(v)
                                                    Sm["st_out"].wait(v, out_b + 16 * (8 * o + tblk - 1))
                                                    Sm["dv_fl"].inc(v.tensor_scalar(
                                                        out=otmp[:], in0=x2c[:, tblk, :],
                                                        scalar1=bL[:, tblk:tblk + 1],
                                                        scalar2=None, op0=ALU.add))
                                                    Sm["dv_fl"].wait(v)
                                                    Sm["dv_ff"].inc(v.scalar_tensor_tensor(
                                                        out=ostage[:, tblk % 2, :],
                                                        in0=pFF[tblk][:], scalar=sderiv[:, 2:3],
                                                        in1=otmp[:], op0=ALU.mult, op1=ALU.add))

                                            @blk.scalar
                                            def _(sc):
                                                for tblk in range(NTBL):
                                                    Sm["dv_ff"].wait(sc, dvff_b + tblk + 1)
                                                    Sm["st_out"].inc(sc.dma_start(
                                                        out=out_d[:, 512 * o:512 * (o + 1)]
                                                        .rearrange("(tb p) n -> p tb n", p=P)[:, tblk, :],
                                                        in_=ostage[:, tblk % 2, :]), 16)

                                            @blk.sync
                                            def _(sync):
                                                if o + 1 < 8:
                                                    sync.wait_ge(Sm["dv_ff"].h, dvff_b + 8)
                                                    Sm["ld_xc"].inc(sync.dma_start(
                                                        out=x2c[:],
                                                        in_=x2_dram.rearrange("(tb p) n -> p tb n", p=P)[:, :, 512 * (o + 1):512 * (o + 2)]), 16)

                                    @blk.gpsimd
                                    def _(g):
                                        if step + 2 < 16:
                                            Sm["te_fh"].wait(g)
                                            Sm["ld_msk"].inc(g.dma_start(
                                                out=wm_sb[:, step % 2, :, :],
                                                in_=mask_src(L, step + 2)), 16)

                        if L < 2:
                            with nc.Block() as blk:
                                @blk.vector
                                def _(v):
                                    Sm["se_ff"].wait(v)
                                    for tblk in range(NTBL):
                                        v.reduce_sum(rtmp[:, 0:1], hp[:, tblk, 0:7], axis=AX.X)
                                        Sm["dv_fl"].inc(v.reduce_sum(rtmp[:, 1:2], hp[:, tblk, 7:9], axis=AX.X), 2)
                                        Sm["dv_fl"].wait(v)
                                        Sm["dv_rs"].inc(v.scalar_tensor_tensor(
                                            out=rs_next[:, tblk:tblk + 1], in0=rtmp[:, 1:2],
                                            scalar=-4.0, in1=rtmp[:, 0:1],
                                            op0=ALU.mult, op1=ALU.add))

                    _trn = [0]

                    def emit_transpose(src_buf, dst_buf):
                        _trn[0] += 1
                        with ExitStack() as tctx:
                            p_trx = tctx.enter_context(nc.psum_tensor(f"p_trx{_trn[0]}", [P, 2, 1024], BF16))
                            for tblk in range(NTBL):
                                with nc.Block() as blk:
                                    tetr_b = Sm["te_tr"].v
                                    dvev_b = Sm["dv_evac"].v

                                    @blk.tensor
                                    def _(t):
                                        for j in range(8):
                                            Sm["dv_evac"].wait(t, dvev_b + j - 1)
                                            for u in range(4):
                                                c = 4 * j + u
                                                ti = t.transpose(
                                                    p_trx[:, j % 2, 128 * u:128 * (u + 1)],
                                                    src_buf[:, tblk, 128 * c:128 * (c + 1)], ident[:])
                                                if u == 3:
                                                    Sm["te_tr"].inc(ti)

                                    @blk.vector
                                    def _(v):
                                        for j in range(8):
                                            Sm["te_tr"].wait(v, tetr_b + j + 1)
                                            Sm["dv_evac"].inc(v.tensor_copy(
                                                dst_buf[:, tblk, 512 * j:512 * (j + 1)],
                                                p_trx[:, j % 2, 0:512]))

                    def layer_scoped(L, src_buf, dst_buf):
                        with ExitStack() as lctx:
                            pFF = [lctx.enter_context(
                                nc.psum_tensor(f"pFF{L}_{i}", [P, 512], F32))
                                for i in range(NTBL)]
                            emit_layer(L, src_buf, dst_buf, pFF)

                    if "x2nT" in taps:
                        _emit_sbuf_tap(nc, Sm, tap_outs["x2nT"], bufA.rearrange("p tb d -> p (tb d)"))
                    layer_scoped(0, bufA, bufB)
                    emit_transpose(bufB, bufA)
                    if "h1T" in taps:
                        _emit_sbuf_tap(nc, Sm, tap_outs["h1T"], bufA.rearrange("p tb d -> p (tb d)"))
                    layer_scoped(1, bufA, bufB)
                    emit_transpose(bufB, bufA)
                    if "h2T" in taps:
                        _emit_sbuf_tap(nc, Sm, tap_outs["h2T"], bufA.rearrange("p tb d -> p (tb d)"))
                    layer_scoped(2, bufA, None)

                    with nc.Block() as blk:
                        @blk.gpsimd
                        def _(g):
                            Sm["st_out"].wait(g)

    return nc


def _emit_dram_tap(nc, Sm, dst_f32, src_ap):
    rows = src_ap.shape[0]
    step = P
    _TAPN[0] += 1
    with nc.sbuf_tensor(f"tapbuf_{_TAPN[0]}", [P, src_ap.shape[1]], F32) as tb_:
        for r0 in range(0, rows, step):
            with nc.Block() as blk:
                @blk.gpsimd
                def _(g):
                    Sm["st_tap"].inc(g.dma_start(out=tb_[0:min(step, rows - r0), :],
                                                 in_=src_ap[r0:r0 + min(step, rows - r0), :]), 16)
                    Sm["st_tap"].wait(g)
                    Sm["st_tap"].inc(g.dma_start(out=dst_f32[r0:r0 + min(step, rows - r0), :],
                                                 in_=tb_[0:min(step, rows - r0), :]), 16)
                    Sm["st_tap"].wait(g)


_TAPN = [0]


def _emit_sbuf_tap(nc, Sm, dst_f32, src_sbuf_ap):
    cols = src_sbuf_ap.shape[1]
    CH = 4096
    _TAPN[0] += 1
    with nc.sbuf_tensor(f"tapbuf2_{_TAPN[0]}", [P, CH], F32) as tb_:
        for c0 in range(0, cols, CH):
            w = min(CH, cols - c0)
            with nc.Block() as blk:
                @blk.vector
                def _(v):
                    v.tensor_copy(tb_[:, 0:w], src_sbuf_ap[:, c0:c0 + w])
            with nc.Block() as blk:
                @blk.gpsimd
                def _(g):
                    Sm["st_tap"].inc(g.dma_start(out=dst_f32[:, c0:c0 + w], in_=tb_[:, 0:w]), 16)
                    Sm["st_tap"].wait(g)


def _host_prep(inputs):
    x = np.asarray(inputs["L0_attn_hidden_states"], np.float32)
    wq = np.asarray(inputs["L0_attn_wq"], np.float32)
    wk = np.asarray(inputs["L0_attn_wk"], np.float32)
    wv = np.asarray(inputs["L0_attn_wv"], np.float32)
    wo = np.asarray(inputs["L0_attn_wo"], np.float32)
    k0 = np.ascontiguousarray(np.asarray(inputs["L0_ffn_k0"]).view(np.uint8))
    k1 = np.ascontiguousarray(np.asarray(inputs["L0_ffn_k1"]).view(np.uint8))
    k2 = np.ascontiguousarray(np.asarray(inputs["L0_ffn_k2"]).view(np.uint8))
    sv = np.ascontiguousarray(np.tile(np.array(
        [np.float32(inputs["L0_ffn_s0"]), np.float32(inputs["L0_ffn_s1"]),
         np.float32(inputs["L0_ffn_s2"])], dtype=np.float32)[None, :], (P, 1)))
    inv = 1.0 / 10000.0 ** (np.arange(0, 64, 2, dtype=np.float32) / 64.0)
    ang = np.arange(S, dtype=np.float32)[:, None] * inv[None, :]
    cosr = np.repeat(np.cos(ang), 2, axis=1).astype(np.float32)
    sinr = np.repeat(np.sin(ang), 2, axis=1).astype(np.float32)
    rope = np.ascontiguousarray(np.concatenate([cosr, sinr], axis=1))
    woT = np.ascontiguousarray(wo.T)

    in_maps = []
    for c in range(8):
        b, g = c // 4, c % 4
        sl = slice(1024 * g, 1024 * (g + 1))
        in_maps.append({
            "x": np.ascontiguousarray(x[b]),
            "x_res": np.ascontiguousarray(x[b, sl, :]),
            "wq_sl": np.ascontiguousarray(wq[sl, :].T),
            "wk_sl": np.ascontiguousarray(wk[sl, :].T),
            "wv_sl": np.ascontiguousarray(wv[sl, :].T),
            "woT": woT,
            "k0": k0, "k1": k1, "k2": k2,
            "rope": rope, "svec": sv,
        })
    return in_maps


def kernel(**inputs):
    in_maps = _host_prep(inputs)
    nc = build_nc(upto=7)
    res = run_bass_kernel_spmd(nc, in_maps, core_ids=list(range(8)))
    out = np.empty((2, S, D), np.float32)
    for c in range(8):
        b, g = c // 4, c % 4
        out[b, 1024 * g:1024 * (g + 1), :] = res.results[c]["out"]
    return out


# version 12
# speedup vs baseline: 1.1352x; 1.0750x over previous
import numpy as np
import ml_dtypes
import concourse.bass as bass
import concourse.mybir as mybir
from concourse.bass_utils import run_bass_kernel_spmd
from contextlib import ExitStack

F32 = mybir.dt.float32
BF16 = mybir.dt.bfloat16
U8 = mybir.dt.uint8
AF = mybir.ActivationFunctionType
ALU = mybir.AluOpType
AX = mybir.AxisListType
BFNP = ml_dtypes.bfloat16

P = 128
S, D = 4096, 4096
NTB = S // P
TSL = 1024
NTBL = TSL // P
NE = 3277
EPS = 1e-5
NCH = D // P


class Ctr:
    def __init__(self, h):
        self.h = h
        self.v = 0

    def inc(self, instr, n=1):
        self.v += n
        instr.then_inc(self.h, n)

    def wait(self, eng, v=None):
        eng.wait_ge(self.h, max(0, self.v if v is None else v))


def build_nc(upto=7, taps=()):
    nc = bass.Bass("TRN2", target_bir_lowering=False)

    x_d = nc.dram_tensor("x", [S, D], F32, kind="ExternalInput").ap()
    xres_d = nc.dram_tensor("x_res", [TSL, D], F32, kind="ExternalInput").ap()
    wq_d = nc.dram_tensor("wq_sl", [D, 1024], F32, kind="ExternalInput").ap()
    wk_d = nc.dram_tensor("wk_sl", [D, 1024], F32, kind="ExternalInput").ap()
    wv_d = nc.dram_tensor("wv_sl", [D, 1024], F32, kind="ExternalInput").ap()
    wo_d = nc.dram_tensor("woT", [D, D], F32, kind="ExternalInput").ap()
    km_d = [nc.dram_tensor(f"k{i}", [D, D], U8, kind="ExternalInput").ap() for i in range(3)]
    rope_d = nc.dram_tensor("rope", [S, 128], F32, kind="ExternalInput").ap()
    svec_d = nc.dram_tensor("svec", [P, 3], F32, kind="ExternalInput").ap()
    out_d = nc.dram_tensor("out", [TSL, D], F32, kind="ExternalOutput").ap()

    xn_dram = nc.dram_tensor("xn_dram", [NTB, P, D], BF16, kind="Internal").ap()
    attn_myr = [nc.dram_tensor(f"attn_my{r}", [S, 512], BF16, kind="Internal").ap()
                for r in range(2)]
    attn_gr = [nc.dram_tensor(f"attn_g{r}", [4 * S, 512], BF16, kind="Internal").ap()
               for r in range(2)]
    x2_dram = nc.dram_tensor("x2_dram", [TSL, D], F32, kind="Internal").ap()
    wkbf1 = nc.dram_tensor("wkbf1", [D, 512], BF16, kind="Internal").ap()
    wvbf1 = nc.dram_tensor("wvbf1", [D, 512], BF16, kind="Internal").ap()
    wqbf1 = nc.dram_tensor("wqbf1", [D, 512], BF16, kind="Internal").ap()
    wobf = nc.dram_tensor("wobf", [D, D], BF16, kind="Internal").ap()

    tap_outs = {}
    for t in taps:
        shp = {"xn": [NTB * P, D], "kst": [P, 4 * S], "vst": [P, NTB * 512],
               "attn_my": [S, 1024], "attn_g": [4 * S, 1024], "x2": [TSL, D],
               "x2nT": [P, NTBL * D], "h1T": [P, NTBL * D], "h2T": [P, NTBL * D]}[t]
        tap_outs[t] = nc.dram_tensor("tap_" + t, shp, F32, kind="ExternalOutput").ap()

    with ExitStack() as ctx:
        en = ctx.enter_context

        ident = en(nc.sbuf_tensor("ident", [P, P], BF16))
        rs_mix = en(nc.sbuf_tensor("rs_mix", [P, 1], F32))
        svec = en(nc.sbuf_tensor("svec_sb", [P, 3], F32))
        sderiv = en(nc.sbuf_tensor("sderiv", [P, 8], F32))
        st_sum = en(nc.sbuf_tensor("st_sum", [P, 1], F32))
        st_ssq = en(nc.sbuf_tensor("st_ssq", [P, 1], F32))
        st_mean = en(nc.sbuf_tensor("st_mean", [P, 1], F32))
        st_m2 = en(nc.sbuf_tensor("st_m2", [P, 1], F32))
        st_var = en(nc.sbuf_tensor("st_var", [P, 1], F32))
        st_rstd = en(nc.sbuf_tensor("st_rstd", [P, 1], F32))
        st_negmr = en(nc.sbuf_tensor("st_negmr", [P, 1], F32))
        c_eps = en(nc.sbuf_tensor("c_eps", [P, 1], F32))
        c_mhalf = en(nc.sbuf_tensor("c_mhalf", [P, 1], F32))
        c_inv16 = en(nc.sbuf_tensor("c_inv16", [P, 1], F32))

        Sm = {n: Ctr(en(nc.semaphore(n))) for n in [
            "ld_x", "ld_w", "ld_blob", "ld_aux", "st_blob",
            "se_sq", "dv_var", "se_rstd", "dv_negmr", "se_xn",
            "te_tr", "dv_evac", "te_kv", "se_kv", "dv_kbf", "te_ktr", "dv_ktr",
            "te_q", "se_q", "dv_q", "te_qtr", "dv_qfm",
            "te_s", "dv_sevac", "dv_negm", "se_exp", "dv_l",
            "te_pt", "dv_ptevac", "te_pv", "dv_o", "st_attn",
            "cc", "ld_at", "te_at", "dv_at", "ld_wo", "te_o", "dv_x2", "st_x2",
            "ld_x2r", "ld_msk", "dv_wfix", "te_ff", "se_ff", "dv_ff", "st_out",
            "ld_xc", "dv_rs", "st_tap", "te_oh", "te_fh", "dv_fl", "se_fl", "gp_fl", "ld_at2", "cast_w",
        ]}

        with nc.Block() as blk:
            @blk.gpsimd
            def _(g):
                g.memset(ident[:], 0.0)
                g.affine_select(out=ident[:], in_=ident[:], compare_op=ALU.not_equal,
                                fill=1.0, base=0, pattern=[[-1, P]], channel_multiplier=1)
                g.memset(rs_mix[:], 1.0)
                g.affine_select(out=rs_mix[:], in_=rs_mix[:], compare_op=ALU.is_ge,
                                fill=-4.0, base=76, pattern=[[0, 1]], channel_multiplier=-1)
                g.memset(c_eps[:], EPS)
                g.memset(c_mhalf[:], -0.5)
                g.memset(c_inv16[:], 1.0 / 16.0)
                Sm["ld_aux"].inc(g.dma_start(out=svec[:], in_=svec_d), 16)

        with nc.Block() as blk:
            @blk.vector
            def _(v):
                Sm["ld_aux"].wait(v)
                for col, (src, sc) in enumerate([(0, 4.0), (1, 4.0), (2, 2.0),
                                                 (0, -2.0), (1, -2.0), (2, -1.0)]):
                    v.tensor_scalar(out=sderiv[:, col:col + 1], in0=svec[:, src:src + 1],
                                    scalar1=sc, scalar2=None, op0=ALU.mult)

        with ExitStack() as actx:
            aen = actx.enter_context
            rope_sb = aen(nc.sbuf_tensor("rope_sb", [P, NTB, 128], F32))
            Kst = aen(nc.sbuf_tensor("Kst", [P, 4, S], BF16))
            Vst = aen(nc.sbuf_tensor("Vst", [P, NTB, 512], BF16))

            def emit_kv_phase(rnd, first):
                with ExitStack() as pctx:
                    pen = pctx.enter_context
                    wk_sb = pen(nc.sbuf_tensor(f"wk_sb{rnd}", [P, NCH, 512], BF16))
                    wv_sb = pen(nc.sbuf_tensor(f"wv_sb{rnd}", [P, NCH, 512], BF16))
                    xn_dm = pen(nc.sbuf_tensor(f"xn_dm{rnd}", [P, D], BF16))
                    k_tm = pen(nc.sbuf_tensor(f"k_tm{rnd}", [P, 512], F32))
                    k_bf = pen(nc.sbuf_tensor(f"k_bf{rnd}", [P, 512], BF16))
                    k_rsc = pen(nc.sbuf_tensor(f"k_rsc{rnd}", [P, 128], F32))
                    pK = pen(nc.psum_tensor(f"pK{rnd}", [P, 512], F32))
                    pV = pen(nc.psum_tensor(f"pV{rnd}", [P, 512], F32))
                    p_ktr = pen(nc.psum_tensor(f"p_ktr{rnd}", [P, 1024], BF16))
                    if first:
                        x_tile = pen(nc.sbuf_tensor("x_tile", [P, 2, D], F32))
                        xn_tm = pen(nc.sbuf_tensor("xn_tm", [P, D], BF16))
                        p_tr = pen(nc.psum_tensor("p_tr", [P, 2, 1024], BF16))
                    else:
                        xn_blob3 = pen(nc.sbuf_tensor("xn_blob3", [P, 4, D], BF16))

                    ld_x_b = Sm["ld_x"].v
                    ld_blob_b = Sm["ld_blob"].v
                    kv_b = Sm["te_kv"].v
                    sekv_b = Sm["se_kv"].v
                    evac_b = Sm["dv_evac"].v
                    tr_b = Sm["te_tr"].v
                    blob_b = Sm["st_blob"].v
                    ktr_b = Sm["te_ktr"].v
                    dkt_b = Sm["dv_ktr"].v
                    kbf_b = Sm["dv_kbf"].v
                    var_b = Sm["dv_var"].v
                    rstd_b = Sm["se_rstd"].v
                    xn_b = Sm["se_xn"].v

                    def kv_rotary_dv(v, tb):
                        v.wait_ge(Sm["se_kv"].h, sekv_b + 2 * tb + 1)
                        cos = rope_sb[:, tb, 0:64]
                        sin = rope_sb[:, tb, 64:128]
                        for h in (0, 1):
                            kr = k_tm[:, 256 * h:256 * h + 64]
                            t1 = k_rsc[:, 0:64]
                            t2 = k_rsc[:, 64:128]
                            v.tensor_mul(t1, kr, cos)
                            Sm["dv_fl"].inc(v.tensor_mul(t2, kr, sin))
                            Sm["dv_fl"].wait(v)
                            ev = lambda a: a.rearrange("p (i two) -> p i two", two=2)[:, :, 0]
                            od = lambda a: a.rearrange("p (i two) -> p i two", two=2)[:, :, 1]
                            v.tensor_sub(ev(kr), ev(t1), od(t2))
                            Sm["dv_fl"].inc(v.tensor_add(od(kr), od(t1), ev(t2)))
                            Sm["dv_fl"].wait(v)
                        v.wait_ge(Sm["te_ktr"].h, max(0, ktr_b + tb))
                        Sm["dv_kbf"].inc(v.tensor_copy(k_bf[:], k_tm[:]))

                    def kv_kst_evac(v, tb):
                        v.wait_ge(Sm["te_ktr"].h, ktr_b + tb + 1)
                        Sm["dv_ktr"].inc(v.tensor_copy(
                            Kst[:, :, P * tb:P * (tb + 1)],
                            p_ktr.rearrange("p (u q) -> p u q", q=P)[:, 0:4, :]))

                    def kv_epi_scalar(s, tb):
                        s.wait_ge(Sm["te_kv"].h, kv_b + tb + 1)
                        s.wait_ge(Sm["dv_kbf"].h, max(0, kbf_b + tb))
                        Sm["se_kv"].inc(s.copy(k_tm[:], pK[:]))
                        Sm["se_kv"].inc(s.activation(
                            out=Vst[:, tb, :], in_=pV[:], func=AF.Copy))

                    def kv_ktr(t, tb):
                        t.wait_ge(Sm["dv_kbf"].h, kbf_b + tb + 1)
                        t.wait_ge(Sm["dv_ktr"].h, max(0, dkt_b + tb))
                        for u in range(4):
                            ti = t.transpose(p_ktr[:, 128 * u:128 * (u + 1)],
                                             k_bf[:, 128 * u:128 * (u + 1)], ident[:])
                            if u == 3:
                                Sm["te_ktr"].inc(ti)

                    with nc.Block() as blk:
                        @blk.gpsimd
                        def _(g):
                            if first:
                                Sm["ld_w"].inc(g.dma_start(
                                    out=wk_sb[:],
                                    in_=wk_d.rearrange("(c p) n -> p c n", p=P)[:, :, 0:512]), 16)
                                Sm["ld_w"].inc(g.dma_start(
                                    out=wv_sb[:],
                                    in_=wv_d.rearrange("(c p) n -> p c n", p=P)[:, :, 0:512]), 16)
                                if upto >= 3:
                                    Sm["cast_w"].inc(g.dma_start(
                                        out=wkbf1[:], in_=wk_d[:, 512:1024]), 16)
                                    Sm["cast_w"].inc(g.dma_start(
                                        out=wvbf1[:], in_=wv_d[:, 512:1024]), 16)
                                if upto >= 4:
                                    Sm["cast_w"].inc(g.dma_start(
                                        out=wqbf1[:], in_=wq_d[:, 512:1024]), 16)
                            else:
                                g.wait_ge(Sm["cast_w"].h, 32)
                                Sm["ld_w"].inc(g.dma_start(
                                    out=wk_sb[:],
                                    in_=wkbf1.rearrange("(c p) n -> p c n", p=P)), 16)
                                Sm["ld_w"].inc(g.dma_start(
                                    out=wv_sb[:],
                                    in_=wvbf1.rearrange("(c p) n -> p c n", p=P)), 16)

                        ld_w_t = Sm["ld_w"].v

                        @blk.sync
                        def _(sync):
                            if first:
                                Sm["ld_aux"].inc(sync.dma_start(
                                    out=rope_sb[:],
                                    in_=rope_d.rearrange("(tb p) f -> p tb f", p=P)), 16)
                                Sm["ld_x"].inc(sync.dma_start(
                                    out=x_tile[:, 0, :], in_=x_d[0:P, :]), 16)
                            else:
                                Sm["ld_blob"].inc(sync.dma_start(
                                    out=xn_blob3[:, 0, :], in_=xn_dram[0]), 16)
                            for tb in range(1, NTB):
                                if first:
                                    sync.wait_ge(Sm["se_xn"].h, max(0, xn_b + tb - 1))
                                    Sm["ld_x"].inc(sync.dma_start(
                                        out=x_tile[:, tb % 2, :],
                                        in_=x_d[P * tb:P * (tb + 1), :]), 16)
                                else:
                                    sync.wait_ge(Sm["te_kv"].h, max(0, kv_b + tb - 3))
                                    Sm["ld_blob"].inc(sync.dma_start(
                                        out=xn_blob3[:, tb % 4, :],
                                        in_=xn_dram[tb]), 16)

                        if first:
                            def ln_chain(v, tb):
                                xt = x_tile[:, tb % 2, :]
                                v.wait_ge(Sm["ld_x"].h, ld_x_b + 16 * (tb + 1))
                                if tb >= 1:
                                    v.wait_ge(Sm["te_tr"].h, tr_b + 8 * tb)
                                    v.wait_ge(Sm["se_rstd"].h, rstd_b + tb)
                                Sm["dv_fl"].inc(v.reduce_sum(st_sum[:], xt, axis=AX.X))
                                Sm["dv_fl"].inc(v.scalar_tensor_tensor(
                                    out=xn_tm[:], in0=xt, scalar=1.0, in1=xt,
                                    op0=ALU.mult, op1=ALU.mult, accum_out=st_ssq[:]))
                                Sm["dv_fl"].wait(v)
                                Sm["dv_fl"].inc(v.tensor_scalar(
                                    out=st_mean[:], in0=st_sum[:],
                                    scalar1=-1.0 / D, scalar2=None, op0=ALU.mult))
                                Sm["dv_fl"].wait(v)
                                Sm["dv_fl"].inc(v.tensor_mul(st_m2[:], st_mean[:], st_mean[:]))
                                Sm["dv_fl"].wait(v)
                                Sm["dv_var"].inc(v.scalar_tensor_tensor(
                                    out=st_var[:], in0=st_ssq[:], scalar=1.0 / D,
                                    in1=st_m2[:], op0=ALU.mult, op1=ALU.subtract))
                                v.wait_ge(Sm["se_rstd"].h, rstd_b + tb + 1)
                                Sm["dv_fl"].inc(v.tensor_copy(st_negmr[:], st_rstd[:]))
                                Sm["dv_fl"].wait(v)
                                Sm["se_xn"].inc(v.tensor_scalar(
                                    out=xn_tm[:], in0=xt, scalar1=st_mean[:],
                                    scalar2=st_negmr[:], op0=ALU.add, op1=ALU.mult))

                            @blk.vector
                            def _(v):
                                Sm["ld_aux"].wait(v)
                                ln_chain(v, 0)
                                for tb in range(NTB):
                                    for j in range(8):
                                        v.wait_ge(Sm["te_tr"].h, tr_b + 8 * tb + j + 1)
                                        if j == 0:
                                            v.wait_ge(Sm["te_kv"].h, max(0, kv_b + tb))
                                            v.wait_ge(Sm["st_blob"].h, max(0, blob_b + 16 * tb))
                                        Sm["dv_evac"].inc(v.tensor_copy(
                                            xn_dm[:, 512 * j:512 * (j + 1)],
                                            p_tr[:, j % 2, 0:512]))
                                    if tb + 1 < NTB:
                                        ln_chain(v, tb + 1)
                                    kv_rotary_dv(v, tb)
                                    kv_kst_evac(v, tb)

                            @blk.scalar
                            def _(s):
                                for tb in range(NTB):
                                    s.wait_ge(Sm["dv_var"].h, var_b + tb + 1)
                                    if tb >= 1:
                                        s.wait_ge(Sm["se_xn"].h, xn_b + tb)
                                    Sm["se_fl"].inc(s.activation(
                                        out=st_rstd[:], in_=st_var[:],
                                        func=AF.Ln, bias=c_eps[:]))
                                    Sm["se_fl"].wait(s)
                                    Sm["se_rstd"].inc(s.activation(
                                        out=st_rstd[:], in_=st_rstd[:], func=AF.Exp,
                                        scale=c_mhalf[:]))
                                    kv_epi_scalar(s, tb)
                                    s.wait_ge(Sm["dv_evac"].h, evac_b + 8 * (tb + 1))
                                    Sm["st_blob"].inc(s.dma_start(
                                        out=xn_dram[tb], in_=xn_dm[:]), 16)

                            @blk.tensor
                            def _(t):
                                for tb in range(NTB):
                                    t.wait_ge(Sm["se_xn"].h, xn_b + tb + 1)

                                    def tgroup(j):
                                        t.wait_ge(Sm["dv_evac"].h, max(0, evac_b + 8 * tb + j - 1))
                                        for u in range(4):
                                            c = 4 * j + u
                                            ti = t.transpose(
                                                p_tr[:, j % 2, 128 * u:128 * (u + 1)],
                                                xn_tm[:, 128 * c:128 * (c + 1)], ident[:])
                                            if u == 3:
                                                Sm["te_tr"].inc(ti)

                                    def mmgroup(j):
                                        t.wait_ge(Sm["dv_evac"].h, evac_b + 8 * tb + j + 1)
                                        if j == 0:
                                            t.wait_ge(Sm["ld_w"].h, ld_w_t)
                                            t.wait_ge(Sm["se_kv"].h, max(0, sekv_b + 2 * tb))
                                        for u in range(4):
                                            c = 4 * j + u
                                            t.matmul(pK[:], xn_dm[:, 128 * c:128 * (c + 1)],
                                                     wk_sb[:, c, :], start=(c == 0), stop=(c == NCH - 1))
                                            mi = t.matmul(pV[:], xn_dm[:, 128 * c:128 * (c + 1)],
                                                          wv_sb[:, c, :], start=(c == 0), stop=(c == NCH - 1))
                                            if c == NCH - 1:
                                                Sm["te_kv"].inc(mi)

                                    tgroup(0)
                                    tgroup(1)
                                    if tb >= 1:
                                        kv_ktr(t, tb - 1)
                                    for j in range(8):
                                        mmgroup(j)
                                        if j + 2 < 8:
                                            tgroup(j + 2)
                                    if tb == NTB - 1:
                                        kv_ktr(t, tb)
                        else:
                            @blk.tensor
                            def _(t):
                                for tb in range(NTB):
                                    src_dm = xn_blob3[:, tb % 4, :]
                                    t.wait_ge(Sm["ld_blob"].h, ld_blob_b + 16 * (tb + 1))
                                    if tb == 0:
                                        t.wait_ge(Sm["ld_w"].h, ld_w_t)
                                    t.wait_ge(Sm["se_kv"].h, max(0, sekv_b + 2 * tb))
                                    if tb >= 1:
                                        kv_ktr(t, tb - 1)
                                    for c in range(NCH):
                                        t.matmul(pK[:], src_dm[:, 128 * c:128 * (c + 1)],
                                                 wk_sb[:, c, :], start=(c == 0), stop=(c == NCH - 1))
                                        mi = t.matmul(pV[:], src_dm[:, 128 * c:128 * (c + 1)],
                                                      wv_sb[:, c, :], start=(c == 0), stop=(c == NCH - 1))
                                        if c == NCH - 1:
                                            Sm["te_kv"].inc(mi)
                                    if tb == NTB - 1:
                                        kv_ktr(t, tb)

                            @blk.scalar
                            def _(s):
                                for tb in range(NTB):
                                    kv_epi_scalar(s, tb)

                            @blk.vector
                            def _(v):
                                for tb in range(NTB):
                                    kv_rotary_dv(v, tb)
                                    kv_kst_evac(v, tb)

                    if first:
                        with nc.Block() as blk:
                            @blk.scalar
                            def _(s):
                                Sm["st_blob"].wait(s)

            def emit_flash_phase(rnd):
                with ExitStack() as pctx:
                    pen = pctx.enter_context
                    wq_sb = pen(nc.sbuf_tensor(f"wq_sb{rnd}", [P, NCH, 512], BF16))
                    xn_blob = pen(nc.sbuf_tensor(f"xn_blob{rnd}", [P, 3, D], BF16))
                    q_tm = pen(nc.sbuf_tensor(f"q_tm{rnd}", [P, 512], F32))
                    q_bf = pen(nc.sbuf_tensor(f"q_bf{rnd}", [P, 512], BF16))
                    q_fm = pen(nc.sbuf_tensor(f"q_fm{rnd}", [P, 4, P], BF16))
                    q_rsc = pen(nc.sbuf_tensor(f"q_rsc{rnd}", [P, 128], F32))
                    S_row = pen(nc.sbuf_tensor(f"S_row{rnd}", [P, S], F32))
                    P_row2 = pen(nc.sbuf_tensor(f"P_row{rnd}", [P, 2, S], BF16))
                    Pt_sb = pen(nc.sbuf_tensor(f"Pt_sb{rnd}", [P, 2, 512], BF16))
                    dmask = pen(nc.sbuf_tensor(f"dmask{rnd}", [P, 4, 512], BF16))
                    mxp = pen(nc.sbuf_tensor(f"mxp{rnd}", [P, 8], F32))
                    lp = pen(nc.sbuf_tensor(f"lp{rnd}", [P, 8], F32))
                    negm2 = pen(nc.sbuf_tensor(f"negm{rnd}", [P, 2], F32))
                    lsum = pen(nc.sbuf_tensor(f"lsum{rnd}", [P, 1], F32))
                    rtmp7 = pen(nc.sbuf_tensor(f"rtmp7_{rnd}", [P, 1], F32))
                    linv = pen(nc.sbuf_tensor(f"linv{rnd}", [P, 1], F32))
                    astage = pen(nc.sbuf_tensor(f"astage{rnd}", [P, 2, 512], BF16))
                    pQ = pen(nc.psum_tensor(f"pQ{rnd}", [P, 512], F32))
                    pS0 = pen(nc.psum_tensor(f"pS0{rnd}", [P, 512], F32))
                    pS1 = pen(nc.psum_tensor(f"pS1{rnd}", [P, 512], F32))
                    p_qtr = pen(nc.psum_tensor(f"p_qtr{rnd}", [P, 1024], BF16))
                    p_pt0 = pen(nc.psum_tensor(f"p_pt0{rnd}", [P, 1024], BF16))
                    p_pt1 = pen(nc.psum_tensor(f"p_pt1{rnd}", [P, 1024], BF16))
                    pO0 = pen(nc.psum_tensor(f"pO0{rnd}", [P, 512], F32))
                    pO1 = pen(nc.psum_tensor(f"pO1{rnd}", [P, 512], F32))
                    pS = [pS0, pS1]
                    p_pt = [p_pt0, p_pt1]
                    pO = [pO0, pO1]

                    Bq = Sm["te_q"].v
                    Bseq = Sm["se_q"].v
                    Bdvq = Sm["dv_q"].v
                    Bqtr = Sm["te_qtr"].v
                    Bqfm = Sm["dv_qfm"].v
                    Bnegm = Sm["dv_negm"].v
                    Bexp = Sm["se_exp"].v
                    Bl = Sm["dv_l"].v
                    Bo = Sm["dv_o"].v
                    Battn = Sm["st_attn"].v
                    Bblob = Sm["ld_blob"].v
                    NKT = [qb // 4 + 1 for qb in range(NTB)]
                    TES = {}
                    cum = 0
                    for qb in range(NTB):
                        for h in (0, 1):
                            TES[(qb, h)] = cum
                            cum += NKT[qb]
                    TES_END = cum
                    Bs = Sm["te_s"].v
                    Bsev = Sm["dv_sevac"].v
                    Bpt = Sm["te_pt"].v
                    Bptv = Sm["dv_ptevac"].v
                    Bpv = Sm["te_pv"].v
                    linv_h2 = [linv[:], lsum[:]]

                    with nc.Block() as blk:
                        @blk.gpsimd
                        def _(g):
                            for a in range(4):
                                g.memset(dmask[:, a, :], 1.0)
                                ai = g.affine_select(out=dmask[:, a, :], in_=dmask[:, a, :],
                                                     compare_op=ALU.is_ge, fill=0.0, base=128 * a,
                                                     pattern=[[-1, 512]], channel_multiplier=1)
                                if a == 3:
                                    Sm["gp_fl"].inc(ai)
                            if rnd == 0:
                                Sm["ld_w"].inc(g.dma_start(
                                    out=wq_sb[:],
                                    in_=wq_d.rearrange("(c p) n -> p c n", p=P)[:, :, 0:512]), 16)
                                if upto >= 6:
                                    for c4 in range(4):
                                        Sm["cast_w"].inc(g.dma_start(
                                            out=wobf[:, 1024 * c4:1024 * (c4 + 1)],
                                            in_=wo_d[:, 1024 * c4:1024 * (c4 + 1)]), 16)
                            else:
                                g.wait_ge(Sm["cast_w"].h, 48)
                                Sm["ld_w"].inc(g.dma_start(
                                    out=wq_sb[:],
                                    in_=wqbf1.rearrange("(c p) n -> p c n", p=P)), 16)

                        ld_w_t = Sm["ld_w"].v
                        gp_t = Sm["gp_fl"].v

                        @blk.sync
                        def _(sync):
                            Sm["ld_blob"].inc(sync.dma_start(
                                out=xn_blob[:, 0, :], in_=xn_dram[0]), 16)
                            for qb in range(1, NTB):
                                sync.wait_ge(Sm["te_q"].h, max(0, Bq + qb - 2))
                                Sm["ld_blob"].inc(sync.dma_start(
                                    out=xn_blob[:, qb % 3, :],
                                    in_=xn_dram[qb]), 16)

                        @blk.tensor
                        def _(t):
                            def ptpv(qb, h):
                                nkt = NKT[qb]
                                t.wait_ge(Sm["dv_l"].h, Bl + 2 * qb + h + 1)
                                t.wait_ge(Sm["dv_o"].h, max(0, Bo + 2 * qb + h - 1))
                                t.wait_ge(Sm["dv_ptevac"].h, Bptv + TES[(qb, h)])

                                def ptr_group(kt):
                                    if kt >= 2:
                                        t.wait_ge(Sm["dv_ptevac"].h, Bptv + TES[(qb, h)] + kt - 1)
                                    for u in range(4):
                                        ti = t.transpose(
                                            p_pt[kt % 2][:, 128 * u:128 * (u + 1)],
                                            P_row2[:, h, 512 * kt + 128 * u:512 * kt + 128 * (u + 1)],
                                            ident[:])
                                        if u == 3:
                                            Sm["te_pt"].inc(ti)

                                def pv_group(kt):
                                    t.wait_ge(Sm["dv_ptevac"].h, Bptv + TES[(qb, h)] + kt + 1)
                                    for u in range(4):
                                        mi = t.matmul(
                                            pO[h][:, 0:256],
                                            Pt_sb[:, kt % 2, 128 * u:128 * (u + 1)],
                                            Vst[:, 4 * kt + u, 256 * h:256 * (h + 1)],
                                            start=(kt == 0 and u == 0),
                                            stop=(kt == nkt - 1 and u == 3))
                                    Sm["te_pv"].inc(mi)

                                ptr_group(0)
                                if nkt > 1:
                                    ptr_group(1)
                                for kt in range(nkt):
                                    pv_group(kt)
                                    if kt + 2 < nkt:
                                        ptr_group(kt + 2)

                            for qb in range(NTB):
                                nkt = NKT[qb]
                                t.wait_ge(Sm["ld_blob"].h, Bblob + 16 * (qb + 1))
                                if qb == 0:
                                    t.wait_ge(Sm["ld_w"].h, ld_w_t)
                                t.wait_ge(Sm["se_q"].h, max(0, Bseq + qb))
                                xb = xn_blob[:, qb % 3, :]
                                for c in range(NCH):
                                    mi = t.matmul(pQ[:], xb[:, 128 * c:128 * (c + 1)],
                                                  wq_sb[:, c, :], start=(c == 0), stop=(c == NCH - 1))
                                Sm["te_q"].inc(mi)
                                t.wait_ge(Sm["dv_q"].h, Bdvq + qb + 1)
                                t.wait_ge(Sm["dv_qfm"].h, max(0, Bqfm + qb))
                                for u in range(4):
                                    ti = t.transpose(p_qtr[:, 128 * u:128 * (u + 1)],
                                                     q_bf[:, 128 * u:128 * (u + 1)], ident[:])
                                    if u == 3:
                                        Sm["te_qtr"].inc(ti)
                                t.wait_ge(Sm["dv_qfm"].h, Bqfm + qb + 1)
                                for h in (0, 1):
                                    t.wait_ge(Sm["dv_sevac"].h, Bsev + TES[(qb, h)])
                                    for kt in range(nkt):
                                        if kt >= 2:
                                            t.wait_ge(Sm["dv_sevac"].h, Bsev + TES[(qb, h)] + kt - 1)
                                        bank = pS[kt % 2]
                                        t.matmul(bank[:], q_fm[:, 2 * h, :],
                                                 Kst[:, 2 * h, 512 * kt:512 * (kt + 1)],
                                                 start=True, stop=False)
                                        mi = t.matmul(bank[:], q_fm[:, 2 * h + 1, :],
                                                      Kst[:, 2 * h + 1, 512 * kt:512 * (kt + 1)],
                                                      start=False, stop=True)
                                        Sm["te_s"].inc(mi)
                                ptpv(qb, 0)
                                ptpv(qb, 1)

                        @blk.scalar
                        def _(s):
                            def attn_store(qb):
                                s.wait_ge(Sm["dv_o"].h, Bo + 2 * (qb + 1))
                                Sm["st_attn"].inc(s.dma_start(
                                    out=attn_myr[rnd][P * qb:P * (qb + 1), :],
                                    in_=astage[:, qb % 2, :]), 16)

                            for qb in range(NTB):
                                nkt = NKT[qb]
                                s.wait_ge(Sm["te_q"].h, Bq + qb + 1)
                                s.wait_ge(Sm["dv_q"].h, max(0, Bdvq + qb))
                                Sm["se_q"].inc(s.activation(
                                    out=q_tm[:], in_=pQ[:], func=AF.Identity, scale=c_inv16[:]))
                                if qb >= 1:
                                    attn_store(qb - 1)
                                for h in (0, 1):
                                    s.wait_ge(Sm["dv_negm"].h, Bnegm + 2 * qb + h + 1)
                                    s.wait_ge(Sm["dv_l"].h, max(0, Bl + 2 * qb + h))
                                    if qb >= 1:
                                        prev_done = TES[(qb - 1, h)] + NKT[qb - 1]
                                        s.wait_ge(Sm["te_pt"].h, Bpt + prev_done)
                                    for kt in range(nkt):
                                        ai = s.activation(
                                            out=P_row2[:, h, 512 * kt:512 * (kt + 1)],
                                            in_=S_row[:, 512 * kt:512 * (kt + 1)],
                                            func=AF.Exp, bias=negm2[:, h:h + 1],
                                            accum_out=lp[:, kt:kt + 1])
                                    Sm["se_exp"].inc(ai)
                                for h in (0, 1):
                                    s.wait_ge(Sm["te_pv"].h, Bpv + TES[(qb, h)] + nkt)
                                    s.wait_ge(Sm["dv_l"].h, Bl + 2 * qb + h + 1)
                                    if h == 0:
                                        s.wait_ge(Sm["st_attn"].h, max(0, Battn + 16 * (qb - 1)))
                                    Sm["dv_o"].inc(s.activation(
                                        out=astage[:, qb % 2, 256 * h:256 * (h + 1)],
                                        in_=pO[h][:, 0:256], func=AF.Identity,
                                        scale=linv_h2[h]))
                                if qb == NTB - 1:
                                    attn_store(qb)

                        @blk.vector
                        def _(v):
                            for qb in range(NTB):
                                nkt = NKT[qb]
                                if qb == 0 and rnd == 0:
                                    Sm["ld_aux"].wait(v)
                                if qb == 0:
                                    v.wait_ge(Sm["gp_fl"].h, gp_t)
                                v.wait_ge(Sm["se_q"].h, Bseq + qb + 1)
                                cos = rope_sb[:, qb, 0:64]
                                sin = rope_sb[:, qb, 64:128]
                                for h in (0, 1):
                                    qr = q_tm[:, 256 * h:256 * h + 64]
                                    t1 = q_rsc[:, 0:64]
                                    t2 = q_rsc[:, 64:128]
                                    v.tensor_mul(t1, qr, cos)
                                    Sm["dv_fl"].inc(v.tensor_mul(t2, qr, sin))
                                    Sm["dv_fl"].wait(v)
                                    ev = lambda a: a.rearrange("p (i two) -> p i two", two=2)[:, :, 0]
                                    od = lambda a: a.rearrange("p (i two) -> p i two", two=2)[:, :, 1]
                                    v.tensor_sub(ev(qr), ev(t1), od(t2))
                                    Sm["dv_fl"].inc(v.tensor_add(od(qr), od(t1), ev(t2)))
                                    Sm["dv_fl"].wait(v)
                                v.wait_ge(Sm["te_qtr"].h, max(0, Bqtr + qb))
                                Sm["dv_q"].inc(v.tensor_copy(q_bf[:], q_tm[:]))
                                v.wait_ge(Sm["te_qtr"].h, Bqtr + qb + 1)
                                v.wait_ge(Sm["te_s"].h, Bs + TES[(qb, 0)])
                                Sm["dv_qfm"].inc(v.tensor_copy(
                                    q_fm.rearrange("p u q -> p (u q)"),
                                    p_qtr[:, 0:512]))
                                for h in (0, 1):
                                    v.wait_ge(Sm["se_exp"].h, max(0, Bexp + 2 * qb + h))
                                    for kt in range(nkt):
                                        v.wait_ge(Sm["te_s"].h, Bs + TES[(qb, h)] + kt + 1)
                                        Sm["dv_sevac"].inc(v.tensor_scalar(
                                            out=S_row[:, 512 * kt:512 * (kt + 1)],
                                            in0=pS[kt % 2][:], scalar1=0.0, scalar2=None,
                                            op0=ALU.add, op1=ALU.max,
                                            accum_out=mxp[:, kt:kt + 1]))
                                    Sm["dv_sevac"].wait(v)
                                    Sm["dv_negm"].inc(v.reduce_max(
                                        negm2[:, h:h + 1], mxp[:, 0:nkt], axis=AX.X, negate=True))
                                    v.wait_ge(Sm["se_exp"].h, Bexp + 2 * qb + h + 1)
                                    dtile = P_row2[:, h, 512 * (nkt - 1):512 * nkt]
                                    Sm["dv_fl"].inc(v.scalar_tensor_tensor(
                                        out=dtile, in0=dtile, scalar=1.0,
                                        in1=dmask[:, qb % 4, :], op0=ALU.mult, op1=ALU.mult,
                                        accum_out=lp[:, nkt - 1:nkt]))
                                    Sm["dv_fl"].wait(v)
                                    Sm["dv_fl"].inc(v.reduce_sum(rtmp7[:, 0:1], lp[:, 0:nkt], axis=AX.X))
                                    Sm["dv_fl"].wait(v)
                                    v.wait_ge(Sm["dv_o"].h, max(0, Bo + 2 * qb + h - 1))
                                    Sm["dv_l"].inc(v.reciprocal(linv_h2[h], rtmp7[:, 0:1]))
                                for h in (0, 1):
                                    v.wait_ge(Sm["te_pv"].h, Bpv + TES[(qb, h)])
                                    for kt in range(NKT[qb]):
                                        v.wait_ge(Sm["te_pt"].h, Bpt + TES[(qb, h)] + kt + 1)
                                        if kt >= 2:
                                            v.wait_ge(Sm["te_pv"].h, Bpv + TES[(qb, h)] + kt - 1)
                                        Sm["dv_ptevac"].inc(v.tensor_copy(
                                            Pt_sb[:, kt % 2, :],
                                            p_pt[kt % 2][:, 0:512]))

                    with nc.Block() as blk:
                        @blk.scalar
                        def _(s):
                            Sm["st_attn"].wait(s)

            def emit_gather(rnd):
                with nc.Block() as blk:
                    @blk.gpsimd
                    def _(g):
                        Sm["st_attn"].wait(g)
                        Sm["cc"].inc(g.collective_compute(
                            "AllGather", ALU.bypass,
                            replica_groups=[[0, 1, 2, 3], [4, 5, 6, 7]],
                            ins=[attn_myr[rnd]],
                            outs=[attn_gr[rnd]]), 1)

            emit_kv_phase(0, True)
            if "xn" in taps:
                _emit_dram_tap(nc, Sm, tap_outs["xn"],
                               xn_dram.rearrange("tb p d -> (tb p) d"))
            if "kst" in taps:
                _emit_sbuf_tap(nc, Sm, tap_outs["kst"], Kst.rearrange("p u s -> p (u s)"))
            if "vst" in taps:
                _emit_sbuf_tap(nc, Sm, tap_outs["vst"], Vst.rearrange("p tb n -> p (tb n)"))
            if upto >= 2:
                emit_flash_phase(0)
            if upto >= 5:
                emit_gather(0)
            if upto >= 3:
                emit_kv_phase(1, False)
            if upto >= 4:
                emit_flash_phase(1)
            if upto >= 5:
                emit_gather(1)

        if upto >= 6:
            with ExitStack() as pctx:
                pen = pctx.enter_context
                attnT = pen(nc.sbuf_tensor("attnT", [P, NTBL, D], BF16))
                wo_sb = pen(nc.sbuf_tensor("wo_sb", [P, 3, 16, 512], BF16))
                xr_sb = pen(nc.sbuf_tensor("xr_sb", [P, NTBL, 512], F32))
                x2st = pen(nc.sbuf_tensor("x2st", [P, 2, 512], F32))
                wo_b = Sm["ld_wo"].v
                xr_b = Sm["ld_x2r"].v
                x2_b = Sm["st_x2"].v

                def wo_src(step):
                    o, ph = step // 2, step % 2
                    return wobf.rearrange("(c p) n -> p c n", p=P)[
                        :, 16 * ph:16 * (ph + 1), 512 * o:512 * (o + 1)]

                with ExitStack() as sctx:
                    sen = sctx.enter_context
                    attn_tm = sen(nc.sbuf_tensor("attn_tm", [P, NTBL, D], BF16))
                    p_at = sen(nc.psum_tensor("p_at", [P, 2, 1024], BF16))
                    at_b = Sm["ld_at"].v
                    at2_b = Sm["ld_at2"].v
                    JL = [0, 2, 4, 6, 1, 3, 5, 7]
                    with nc.Block() as blk:
                        Tat = Sm["te_at"].v
                        Dat = Sm["dv_at"].v

                        @blk.gpsimd
                        def _(g):
                            g.wait_ge(Sm["cc"].h, Sm["cc"].v - 1)
                            gm = g.partition_id() % 4
                            big = attn_gr[0].rearrange("(r s) n -> s r n", r=4)
                            for tb_ in range(NTBL):
                                Sm["ld_at"].inc(g.dma_start(
                                    out=attn_tm[:, tb_, :].rearrange("p (r n) -> p r n", r=4)[:, :, 0:512],
                                    in_=big[bass.ds(gm * TSL + P * tb_, P), :, :]), 16)

                        @blk.scalar
                        def _(sc):
                            Sm["cc"].wait(sc)
                            gm2 = sc.partition_id() % 4
                            big2 = attn_gr[1].rearrange("(r s) n -> s r n", r=4)
                            for tb_ in range(NTBL):
                                Sm["ld_at2"].inc(sc.dma_start(
                                    out=attn_tm[:, tb_, :].rearrange("p (r n) -> p r n", r=4)[:, :, 512:1024],
                                    in_=big2[bass.ds(gm2 * TSL + P * tb_, P), :, :]), 16)

                        @blk.sync
                        def _(sync):
                            sync.wait_ge(Sm["cast_w"].h, 112)
                            for st_ in range(3):
                                Sm["ld_wo"].inc(sync.dma_start(
                                    out=wo_sb[:, st_, :, :], in_=wo_src(st_)), 16)
                            Sm["ld_x2r"].inc(sync.dma_start(
                                out=xr_sb[:],
                                in_=xres_d.rearrange("(tb p) n -> p tb n", p=P)[:, :, 0:512]), 16)

                        @blk.tensor
                        def _(t):
                            for tblk in range(NTBL):
                                for idx, j in enumerate(JL):
                                    if idx == 0:
                                        t.wait_ge(Sm["ld_at"].h, at_b + 16 * (tblk + 1))
                                    if idx == 4:
                                        t.wait_ge(Sm["ld_at2"].h, at2_b + 16 * (tblk + 1))
                                    t.wait_ge(Sm["dv_at"].h, max(0, Dat + 8 * tblk + idx - 1))
                                    for u in range(4):
                                        c = 4 * j + u
                                        ti = t.transpose(
                                            p_at[:, idx % 2, 128 * u:128 * (u + 1)],
                                            attn_tm[:, tblk, 128 * c:128 * (c + 1)], ident[:])
                                        if u == 3:
                                            Sm["te_at"].inc(ti)

                        @blk.vector
                        def _(v):
                            for tblk in range(NTBL):
                                for idx, j in enumerate(JL):
                                    v.wait_ge(Sm["te_at"].h, Tat + 8 * tblk + idx + 1)
                                    Sm["dv_at"].inc(v.tensor_copy(
                                        attnT[:, tblk, 512 * j:512 * (j + 1)],
                                        p_at[:, idx % 2, 0:512]))

                with ExitStack() as sctx:
                    sen = sctx.enter_context
                    pF6 = [sen(nc.psum_tensor(f"pF6_{i}", [P, 512], F32)) for i in range(NTBL)]
                    with nc.Block() as blk:
                        To = Sm["te_o"].v
                        Toh = Sm["te_oh"].v
                        Dx2 = Sm["dv_x2"].v

                        @blk.tensor
                        def _(t):
                            for step in range(16):
                                o, ph = step // 2, step % 2
                                t.wait_ge(Sm["ld_wo"].h, wo_b + 16 * (step + 1))
                                for cl in range(16):
                                    c = 16 * ph + cl
                                    for tblk in range(NTBL):
                                        if c == 0:
                                            t.wait_ge(Sm["dv_x2"].h,
                                                      max(0, Dx2 + 8 * (o - 1) + tblk + 1))
                                        mi = t.matmul(
                                            pF6[tblk][:],
                                            attnT[:, tblk, 128 * c:128 * (c + 1)],
                                            wo_sb[:, step % 3, cl, :],
                                            start=(c == 0), stop=(c == NCH - 1))
                                        if c == NCH - 1 and tblk < NTBL - 1:
                                            Sm["te_o"].inc(mi)
                                        if cl == 15 and tblk == NTBL - 1:
                                            Sm["te_oh"].inc(mi)

                        @blk.vector
                        def _(v):
                            for o in range(8):
                                v.wait_ge(Sm["ld_x2r"].h, xr_b + 16 * (o + 1))
                                for tblk in range(NTBL):
                                    if tblk < NTBL - 1:
                                        v.wait_ge(Sm["te_o"].h, To + 7 * o + tblk + 1)
                                    else:
                                        v.wait_ge(Sm["te_oh"].h, Toh + 2 * o + 2)
                                    v.wait_ge(Sm["st_x2"].h, max(0, x2_b + 16 * (8 * o + tblk - 1)))
                                    Sm["dv_x2"].inc(v.tensor_add(
                                        x2st[:, tblk % 2, :], pF6[tblk][:],
                                        xr_sb[:, tblk, :]))

                        @blk.scalar
                        def _(sc):
                            for o in range(8):
                                for tblk in range(NTBL):
                                    sc.wait_ge(Sm["dv_x2"].h, Dx2 + 8 * o + tblk + 1)
                                    Sm["st_x2"].inc(sc.dma_start(
                                        out=x2_dram[:, 512 * o:512 * (o + 1)]
                                        .rearrange("(tb p) n -> p tb n", p=P)[:, tblk, :],
                                        in_=x2st[:, tblk % 2, :]), 16)

                        @blk.sync
                        def _(sync):
                            for st_ in range(3, 16):
                                sync.wait_ge(Sm["te_oh"].h, Toh + st_ - 2)
                                Sm["ld_wo"].inc(sync.dma_start(
                                    out=wo_sb[:, st_ % 3, :, :], in_=wo_src(st_)), 16)
                                if st_ % 2 == 1:
                                    o_next = (st_ - 1) // 2
                                    sync.wait_ge(Sm["dv_x2"].h, Dx2 + 8 * o_next)
                                    Sm["ld_x2r"].inc(sync.dma_start(
                                        out=xr_sb[:],
                                        in_=xres_d.rearrange("(tb p) n -> p tb n", p=P)[:, :, 512 * o_next:512 * (o_next + 1)]), 16)

                    with nc.Block() as blk:
                        @blk.gpsimd
                        def _(g):
                            Sm["st_x2"].wait(g)
            if "x2" in taps:
                _emit_dram_tap(nc, Sm, tap_outs["x2"], x2_dram)

        if upto >= 7:
            with ExitStack() as pctx:
                pen = pctx.enter_context
                bufA = pen(nc.sbuf_tensor("bufA", [P, NTBL, D], BF16))
                bufB = pen(nc.sbuf_tensor("bufB", [P, NTBL, D], BF16))
                rs_prev = pen(nc.sbuf_tensor("rs_prev", [P, NTBL], F32))
                rs_next = pen(nc.sbuf_tensor("rs_next", [P, NTBL], F32))
                bL = pen(nc.sbuf_tensor("bL", [P, NTBL], F32))
                hp = pen(nc.sbuf_tensor("hp", [P, NTBL, 16], F32))
                rtmp = pen(nc.sbuf_tensor("rtmp", [P, 2], F32))

                with ExitStack() as sctx:
                    sen = sctx.enter_context
                    x2row = sen(nc.sbuf_tensor("x2row", [P, 2, D], F32))
                    x2n_tm = sen(nc.sbuf_tensor("x2n_tm", [P, D], BF16))
                    p_tr7 = sen(nc.psum_tensor("p_tr7", [P, 2, 1024], BF16))
                    x2r_b = Sm["ld_x2r"].v
                    var_b = Sm["dv_var"].v
                    rstd_b = Sm["se_rstd"].v
                    xn_b = Sm["se_xn"].v
                    tr_b = Sm["te_tr"].v
                    evac_b = Sm["dv_evac"].v
                    sq_b = Sm["se_sq"].v

                    with nc.Block() as blk:
                        @blk.sync
                        def _(sync):
                            Sm["st_x2"].wait(sync)
                            Sm["ld_x2r"].inc(sync.dma_start(
                                out=x2row[:, 0, :], in_=x2_dram[0:P, :]), 16)

                    with nc.Block() as blk:
                        @blk.sync
                        def _(sync):
                            for tblk in range(1, NTBL):
                                sync.wait_ge(Sm["se_xn"].h, max(0, xn_b + tblk - 1))
                                Sm["ld_x2r"].inc(sync.dma_start(
                                    out=x2row[:, tblk % 2, :],
                                    in_=x2_dram[P * tblk:P * (tblk + 1), :]), 16)

                        def ln2_chain(v, tblk):
                            xt = x2row[:, tblk % 2, :]
                            v.wait_ge(Sm["ld_x2r"].h, x2r_b + 16 * (tblk + 1))
                            if tblk >= 1:
                                v.wait_ge(Sm["te_tr"].h, tr_b + 8 * tblk)
                                v.wait_ge(Sm["se_sq"].h, sq_b + tblk)
                                v.wait_ge(Sm["se_rstd"].h, rstd_b + tblk)
                            Sm["dv_fl"].inc(v.reduce_sum(st_sum[:], xt, axis=AX.X))
                            Sm["dv_fl"].inc(v.scalar_tensor_tensor(
                                out=x2n_tm[:], in0=xt, scalar=1.0, in1=xt,
                                op0=ALU.mult, op1=ALU.mult, accum_out=st_ssq[:]))
                            Sm["dv_fl"].wait(v)
                            Sm["dv_fl"].inc(v.tensor_scalar(
                                out=st_mean[:], in0=st_sum[:],
                                scalar1=-1.0 / D, scalar2=None, op0=ALU.mult))
                            Sm["dv_fl"].wait(v)
                            Sm["dv_fl"].inc(v.tensor_mul(st_m2[:], st_mean[:], st_mean[:]))
                            Sm["dv_fl"].wait(v)
                            Sm["dv_var"].inc(v.scalar_tensor_tensor(
                                out=st_var[:], in0=st_ssq[:], scalar=1.0 / D,
                                in1=st_m2[:], op0=ALU.mult, op1=ALU.subtract))
                            v.wait_ge(Sm["se_rstd"].h, rstd_b + tblk + 1)
                            Sm["dv_fl"].inc(v.tensor_copy(st_negmr[:], st_rstd[:]))
                            Sm["dv_fl"].wait(v)
                            Sm["se_xn"].inc(v.tensor_scalar(
                                out=x2n_tm[:], in0=xt, scalar1=st_mean[:],
                                scalar2=st_negmr[:], op0=ALU.add, op1=ALU.mult))

                        @blk.vector
                        def _(v):
                            ln2_chain(v, 0)
                            for tblk in range(NTBL):
                                for j in range(8):
                                    v.wait_ge(Sm["te_tr"].h, tr_b + 8 * tblk + j + 1)
                                    Sm["dv_evac"].inc(v.tensor_copy(
                                        bufA[:, tblk, 512 * j:512 * (j + 1)],
                                        p_tr7[:, j % 2, 0:512]))
                                if tblk + 1 < NTBL:
                                    ln2_chain(v, tblk + 1)

                        @blk.scalar
                        def _(s):
                            for tblk in range(NTBL):
                                s.wait_ge(Sm["dv_var"].h, var_b + tblk + 1)
                                if tblk >= 1:
                                    s.wait_ge(Sm["se_xn"].h, xn_b + tblk)
                                Sm["se_fl"].inc(s.activation(
                                    out=st_rstd[:], in_=st_var[:], func=AF.Ln, bias=c_eps[:]))
                                Sm["se_fl"].wait(s)
                                Sm["se_rstd"].inc(s.activation(
                                    out=st_rstd[:], in_=st_rstd[:], func=AF.Exp, scale=c_mhalf[:]))
                                s.wait_ge(Sm["se_xn"].h, xn_b + tblk + 1)
                                Sm["se_sq"].inc(s.activation(
                                    out=x2n_tm[:], in_=x2n_tm[:], func=AF.Identity,
                                    accum_out=rs_prev[:, tblk:tblk + 1]))

                        @blk.tensor
                        def _(t):
                            for tblk in range(NTBL):
                                t.wait_ge(Sm["se_xn"].h, xn_b + tblk + 1)
                                for j in range(8):
                                    t.wait_ge(Sm["dv_evac"].h, max(0, evac_b + 8 * tblk + j - 1))
                                    for u in range(4):
                                        c = 4 * j + u
                                        ti = t.transpose(
                                            p_tr7[:, j % 2, 128 * u:128 * (u + 1)],
                                            x2n_tm[:, 128 * c:128 * (c + 1)], ident[:])
                                        if u == 3:
                                            Sm["te_tr"].inc(ti)

                with ExitStack() as sctx:
                    sen = sctx.enter_context
                    wm_sb = sen(nc.sbuf_tensor("wm_sb", [P, 2, 16, 512], BF16))
                    x2c = sen(nc.sbuf_tensor("x2c", [P, NTBL, 512], F32))
                    ostage = sen(nc.sbuf_tensor("ostage", [P, 2, 512], F32))
                    otmp = sen(nc.sbuf_tensor("otmp", [P, 512], F32))

                    def mask_src(L, step):
                        o, ph = step // 2, step % 2
                        return km_d[L].rearrange("(c p) n -> p c n", p=P)[
                            :, 16 * ph:16 * (ph + 1), 512 * o:512 * (o + 1)]

                    def emit_layer(L, src_buf, dst_buf, pFF):
                        msk_b = Sm["ld_msk"].v
                        xc_b = Sm["ld_xc"].v
                        out_b = Sm["st_out"].v
                        with nc.Block() as blk:
                            Tff = Sm["te_ff"].v
                            Tfh = Sm["te_fh"].v
                            Wfx = Sm["dv_wfix"].v
                            Sff = Sm["se_ff"].v
                            Dff = Sm["dv_ff"].v

                            @blk.gpsimd
                            def _(g):
                                for st_ in range(16):
                                    if st_ >= 2:
                                        g.wait_ge(Sm["te_fh"].h, Tfh + st_ - 1)
                                    Sm["ld_msk"].inc(g.dma_start(
                                        out=wm_sb[:, st_ % 2, :, :], in_=mask_src(L, st_)), 16)

                            @blk.tensor
                            def _(t):
                                for step in range(16):
                                    o, ph = step // 2, step % 2
                                    t.wait_ge(Sm["ld_msk"].h, msk_b + 16 * (step + 1))
                                    if L >= 1 and ph == 1:
                                        t.wait_ge(Sm["dv_wfix"].h, Wfx + (step + 1) // 2)
                                    for cl in range(16):
                                        c = 16 * ph + cl
                                        for tblk in range(NTBL):
                                            if c == 0:
                                                if L < 2:
                                                    t.wait_ge(Sm["se_ff"].h,
                                                              max(0, Sff + 8 * (o - 1) + tblk + 1))
                                                else:
                                                    t.wait_ge(Sm["dv_ff"].h,
                                                              max(0, Dff + 8 * (o - 1) + tblk + 1))
                                            mi = t.matmul(
                                                pFF[tblk][:],
                                                src_buf[:, tblk, 128 * c:128 * (c + 1)],
                                                wm_sb[:, step % 2, cl, :],
                                                start=(c == 0), stop=(c == NCH - 1))
                                            if c == NCH - 1 and tblk < NTBL - 1:
                                                Sm["te_ff"].inc(mi)
                                            if cl == 15 and tblk == NTBL - 1:
                                                Sm["te_fh"].inc(mi)

                            if L >= 1:
                                @blk.vector
                                def _(v):
                                    if L == 2:
                                        Sm["dv_rs"].wait(v)
                                        for tblk in range(NTBL):
                                            v.tensor_mul(bL[:, tblk:tblk + 1],
                                                         rs_next[:, tblk:tblk + 1],
                                                         sderiv[:, 5:6])
                                    for o in range(8):
                                        step = 2 * o + 1
                                        v.wait_ge(Sm["ld_msk"].h, msk_b + 16 * (step + 1))
                                        buf = step % 2
                                        v.tensor_scalar(
                                            out=wm_sb[:, buf, 9, :], in0=wm_sb[:, buf, 9, :],
                                            scalar1=rs_mix[:], scalar2=None, op0=ALU.mult)
                                        Sm["dv_wfix"].inc(v.tensor_scalar(
                                            out=wm_sb[:, buf, 10:16, :].rearrange("p c n -> p (c n)"),
                                            in0=wm_sb[:, buf, 10:16, :].rearrange("p c n -> p (c n)"),
                                            scalar1=-4.0, scalar2=None, op0=ALU.mult))
                                        if L == 2:
                                            v.wait_ge(Sm["ld_xc"].h, xc_b + 16 * (o + 1))
                                            for tblk in range(NTBL):
                                                if tblk < NTBL - 1:
                                                    v.wait_ge(Sm["te_ff"].h, Tff + 7 * o + tblk + 1)
                                                else:
                                                    v.wait_ge(Sm["te_fh"].h, Tfh + 2 * o + 2)
                                                v.wait_ge(Sm["st_out"].h, max(0, out_b + 16 * (8 * o + tblk - 1)))
                                                Sm["dv_fl"].inc(v.tensor_scalar(
                                                    out=otmp[:], in0=x2c[:, tblk, :],
                                                    scalar1=bL[:, tblk:tblk + 1],
                                                    scalar2=None, op0=ALU.add))
                                                Sm["dv_fl"].wait(v)
                                                Sm["dv_ff"].inc(v.scalar_tensor_tensor(
                                                    out=ostage[:, tblk % 2, :],
                                                    in0=pFF[tblk][:], scalar=sderiv[:, 2:3],
                                                    in1=otmp[:], op0=ALU.mult, op1=ALU.add))

                            if L < 2:
                                @blk.scalar
                                def _(s):
                                    if L == 1:
                                        Sm["dv_rs"].wait(s)
                                        s.copy(rs_prev[:], rs_next[:])
                                    for tblk in range(NTBL):
                                        s.activation(out=bL[:, tblk:tblk + 1],
                                                     in_=rs_prev[:, tblk:tblk + 1],
                                                     func=AF.Identity,
                                                     scale=sderiv[:, 3 + L:4 + L])
                                    for o in range(8):
                                        for tblk in range(NTBL):
                                            if tblk < NTBL - 1:
                                                s.wait_ge(Sm["te_ff"].h, Tff + 7 * o + tblk + 1)
                                            else:
                                                s.wait_ge(Sm["te_fh"].h, Tfh + 2 * o + 2)
                                            scale_ap = sderiv[:, L:L + 1]
                                            bias_ap = bL[:, tblk:tblk + 1]
                                            if o == 6:
                                                si = s.activation(
                                                    out=dst_buf[:, tblk, 3072:NE],
                                                    in_=pFF[tblk][:, 0:NE - 3072],
                                                    func=AF.Relu, scale=scale_ap, bias=bias_ap,
                                                    accum_out=hp[:, tblk, 6:7])
                                                si = s.activation(
                                                    out=dst_buf[:, tblk, NE:3584],
                                                    in_=pFF[tblk][:, NE - 3072:512],
                                                    func=AF.Relu, scale=scale_ap, bias=bias_ap,
                                                    accum_out=hp[:, tblk, 7:8])
                                            else:
                                                col = o if o < 6 else 8
                                                si = s.activation(
                                                    out=dst_buf[:, tblk, 512 * o:512 * (o + 1)],
                                                    in_=pFF[tblk][:],
                                                    func=AF.Relu, scale=scale_ap, bias=bias_ap,
                                                    accum_out=hp[:, tblk, col:col + 1])
                                            Sm["se_ff"].inc(si)
                            else:
                                @blk.scalar
                                def _(sc):
                                    for o in range(8):
                                        for tblk in range(NTBL):
                                            sc.wait_ge(Sm["dv_ff"].h, Dff + 8 * o + tblk + 1)
                                            Sm["st_out"].inc(sc.dma_start(
                                                out=out_d[:, 512 * o:512 * (o + 1)]
                                                .rearrange("(tb p) n -> p tb n", p=P)[:, tblk, :],
                                                in_=ostage[:, tblk % 2, :]), 16)

                                @blk.sync
                                def _(sync):
                                    Sm["ld_xc"].inc(sync.dma_start(
                                        out=x2c[:],
                                        in_=x2_dram.rearrange("(tb p) n -> p tb n", p=P)[:, :, 0:512]), 16)
                                    for o in range(7):
                                        sync.wait_ge(Sm["dv_ff"].h, Dff + 8 * (o + 1))
                                        Sm["ld_xc"].inc(sync.dma_start(
                                            out=x2c[:],
                                            in_=x2_dram.rearrange("(tb p) n -> p tb n", p=P)[:, :, 512 * (o + 1):512 * (o + 2)]), 16)

                        if L < 2:
                            with nc.Block() as blk:
                                @blk.vector
                                def _(v):
                                    Sm["se_ff"].wait(v)
                                    for tblk in range(NTBL):
                                        v.reduce_sum(rtmp[:, 0:1], hp[:, tblk, 0:7], axis=AX.X)
                                        Sm["dv_fl"].inc(v.reduce_sum(rtmp[:, 1:2], hp[:, tblk, 7:9], axis=AX.X), 2)
                                        Sm["dv_fl"].wait(v)
                                        Sm["dv_rs"].inc(v.scalar_tensor_tensor(
                                            out=rs_next[:, tblk:tblk + 1], in0=rtmp[:, 1:2],
                                            scalar=-4.0, in1=rtmp[:, 0:1],
                                            op0=ALU.mult, op1=ALU.add))

                    _trn = [0]

                    def emit_transpose(src_buf, dst_buf):
                        _trn[0] += 1
                        with ExitStack() as tctx:
                            p_trx = tctx.enter_context(nc.psum_tensor(f"p_trx{_trn[0]}", [P, 2, 1024], BF16))
                            with nc.Block() as blk:
                                tetr_b = Sm["te_tr"].v
                                dvev_b = Sm["dv_evac"].v

                                @blk.tensor
                                def _(t):
                                    for tblk in range(NTBL):
                                        for j in range(8):
                                            t.wait_ge(Sm["dv_evac"].h,
                                                      max(0, dvev_b + 8 * tblk + j - 1))
                                            for u in range(4):
                                                c = 4 * j + u
                                                ti = t.transpose(
                                                    p_trx[:, j % 2, 128 * u:128 * (u + 1)],
                                                    src_buf[:, tblk, 128 * c:128 * (c + 1)], ident[:])
                                                if u == 3:
                                                    Sm["te_tr"].inc(ti)

                                @blk.vector
                                def _(v):
                                    for tblk in range(NTBL):
                                        for j in range(8):
                                            v.wait_ge(Sm["te_tr"].h, tetr_b + 8 * tblk + j + 1)
                                            Sm["dv_evac"].inc(v.tensor_copy(
                                                dst_buf[:, tblk, 512 * j:512 * (j + 1)],
                                                p_trx[:, j % 2, 0:512]))

                    def layer_scoped(L, src_buf, dst_buf):
                        with ExitStack() as lctx:
                            pFF = [lctx.enter_context(
                                nc.psum_tensor(f"pFF{L}_{i}", [P, 512], F32))
                                for i in range(NTBL)]
                            emit_layer(L, src_buf, dst_buf, pFF)

                    if "x2nT" in taps:
                        _emit_sbuf_tap(nc, Sm, tap_outs["x2nT"], bufA.rearrange("p tb d -> p (tb d)"))
                    layer_scoped(0, bufA, bufB)
                    emit_transpose(bufB, bufA)
                    if "h1T" in taps:
                        _emit_sbuf_tap(nc, Sm, tap_outs["h1T"], bufA.rearrange("p tb d -> p (tb d)"))
                    layer_scoped(1, bufA, bufB)
                    emit_transpose(bufB, bufA)
                    if "h2T" in taps:
                        _emit_sbuf_tap(nc, Sm, tap_outs["h2T"], bufA.rearrange("p tb d -> p (tb d)"))
                    layer_scoped(2, bufA, None)

                    with nc.Block() as blk:
                        @blk.gpsimd
                        def _(g):
                            Sm["st_out"].wait(g)

    return nc


def _emit_dram_tap(nc, Sm, dst_f32, src_ap):
    rows = src_ap.shape[0]
    step = P
    _TAPN[0] += 1
    with nc.sbuf_tensor(f"tapbuf_{_TAPN[0]}", [P, src_ap.shape[1]], F32) as tb_:
        for r0 in range(0, rows, step):
            with nc.Block() as blk:
                @blk.gpsimd
                def _(g):
                    Sm["st_tap"].inc(g.dma_start(out=tb_[0:min(step, rows - r0), :],
                                                 in_=src_ap[r0:r0 + min(step, rows - r0), :]), 16)
                    Sm["st_tap"].wait(g)
                    Sm["st_tap"].inc(g.dma_start(out=dst_f32[r0:r0 + min(step, rows - r0), :],
                                                 in_=tb_[0:min(step, rows - r0), :]), 16)
                    Sm["st_tap"].wait(g)


_TAPN = [0]


def _emit_sbuf_tap(nc, Sm, dst_f32, src_sbuf_ap):
    cols = src_sbuf_ap.shape[1]
    CH = 4096
    _TAPN[0] += 1
    with nc.sbuf_tensor(f"tapbuf2_{_TAPN[0]}", [P, CH], F32) as tb_:
        for c0 in range(0, cols, CH):
            w = min(CH, cols - c0)
            with nc.Block() as blk:
                @blk.vector
                def _(v):
                    v.tensor_copy(tb_[:, 0:w], src_sbuf_ap[:, c0:c0 + w])
            with nc.Block() as blk:
                @blk.gpsimd
                def _(g):
                    Sm["st_tap"].inc(g.dma_start(out=dst_f32[:, c0:c0 + w], in_=tb_[:, 0:w]), 16)
                    Sm["st_tap"].wait(g)


def _host_prep(inputs):
    x = np.asarray(inputs["L0_attn_hidden_states"], np.float32)
    wq = np.asarray(inputs["L0_attn_wq"], np.float32)
    wk = np.asarray(inputs["L0_attn_wk"], np.float32)
    wv = np.asarray(inputs["L0_attn_wv"], np.float32)
    wo = np.asarray(inputs["L0_attn_wo"], np.float32)
    k0 = np.ascontiguousarray(np.asarray(inputs["L0_ffn_k0"]).view(np.uint8))
    k1 = np.ascontiguousarray(np.asarray(inputs["L0_ffn_k1"]).view(np.uint8))
    k2 = np.ascontiguousarray(np.asarray(inputs["L0_ffn_k2"]).view(np.uint8))
    sv = np.ascontiguousarray(np.tile(np.array(
        [np.float32(inputs["L0_ffn_s0"]), np.float32(inputs["L0_ffn_s1"]),
         np.float32(inputs["L0_ffn_s2"])], dtype=np.float32)[None, :], (P, 1)))
    inv = 1.0 / 10000.0 ** (np.arange(0, 64, 2, dtype=np.float32) / 64.0)
    ang = np.arange(S, dtype=np.float32)[:, None] * inv[None, :]
    cosr = np.repeat(np.cos(ang), 2, axis=1).astype(np.float32)
    sinr = np.repeat(np.sin(ang), 2, axis=1).astype(np.float32)
    rope = np.ascontiguousarray(np.concatenate([cosr, sinr], axis=1))
    woT = np.ascontiguousarray(wo.T)

    in_maps = []
    for c in range(8):
        b, g = c // 4, c % 4
        sl = slice(1024 * g, 1024 * (g + 1))
        in_maps.append({
            "x": np.ascontiguousarray(x[b]),
            "x_res": np.ascontiguousarray(x[b, sl, :]),
            "wq_sl": np.ascontiguousarray(wq[sl, :].T),
            "wk_sl": np.ascontiguousarray(wk[sl, :].T),
            "wv_sl": np.ascontiguousarray(wv[sl, :].T),
            "woT": woT,
            "k0": k0, "k1": k1, "k2": k2,
            "rope": rope, "svec": sv,
        })
    return in_maps


def kernel(**inputs):
    in_maps = _host_prep(inputs)
    nc = build_nc(upto=7)
    res = run_bass_kernel_spmd(nc, in_maps, core_ids=list(range(8)))
    out = np.empty((2, S, D), np.float32)
    for c in range(8):
        b, g = c // 4, c % 4
        out[b, 1024 * g:1024 * (g + 1), :] = res.results[c]["out"]
    return out


# version 13
# speedup vs baseline: 1.1447x; 1.0084x over previous
import numpy as np
import ml_dtypes
import concourse.bass as bass
import concourse.mybir as mybir
from concourse.bass_utils import run_bass_kernel_spmd
from contextlib import ExitStack

F32 = mybir.dt.float32
BF16 = mybir.dt.bfloat16
U8 = mybir.dt.uint8
AF = mybir.ActivationFunctionType
ALU = mybir.AluOpType
AX = mybir.AxisListType
BFNP = ml_dtypes.bfloat16

P = 128
S, D = 4096, 4096
NTB = S // P
TSL = 1024
NTBL = TSL // P
NE = 3277
EPS = 1e-5
NCH = D // P


class Ctr:
    def __init__(self, h):
        self.h = h
        self.v = 0

    def inc(self, instr, n=1):
        self.v += n
        instr.then_inc(self.h, n)

    def wait(self, eng, v=None):
        eng.wait_ge(self.h, max(0, self.v if v is None else v))


def build_nc(upto=7, taps=()):
    nc = bass.Bass("TRN2", target_bir_lowering=False)

    x_d = nc.dram_tensor("x", [S, D], F32, kind="ExternalInput").ap()
    xres_d = nc.dram_tensor("x_res", [TSL, D], F32, kind="ExternalInput").ap()
    wq_d = nc.dram_tensor("wq_sl", [D, 1024], F32, kind="ExternalInput").ap()
    wk_d = nc.dram_tensor("wk_sl", [D, 1024], F32, kind="ExternalInput").ap()
    wv_d = nc.dram_tensor("wv_sl", [D, 1024], F32, kind="ExternalInput").ap()
    wo_d = nc.dram_tensor("woT", [D, D], F32, kind="ExternalInput").ap()
    km_d = [nc.dram_tensor(f"k{i}", [D, D], U8, kind="ExternalInput").ap() for i in range(3)]
    rope_d = nc.dram_tensor("rope", [S, 128], F32, kind="ExternalInput").ap()
    svec_d = nc.dram_tensor("svec", [P, 3], F32, kind="ExternalInput").ap()
    out_d = nc.dram_tensor("out", [TSL, D], F32, kind="ExternalOutput").ap()

    xn_dram = nc.dram_tensor("xn_dram", [NTB, P, D], BF16, kind="Internal").ap()
    attn_myr = [nc.dram_tensor(f"attn_my{r}", [S, 512], BF16, kind="Internal").ap()
                for r in range(2)]
    attn_gr = [nc.dram_tensor(f"attn_g{r}", [4 * S, 512], BF16, kind="Internal").ap()
               for r in range(2)]
    x2_dram = nc.dram_tensor("x2_dram", [TSL, D], F32, kind="Internal").ap()
    wkbf1 = nc.dram_tensor("wkbf1", [D, 512], BF16, kind="Internal").ap()
    wvbf1 = nc.dram_tensor("wvbf1", [D, 512], BF16, kind="Internal").ap()
    wqbf1 = nc.dram_tensor("wqbf1", [D, 512], BF16, kind="Internal").ap()
    wobf = nc.dram_tensor("wobf", [D, D], BF16, kind="Internal").ap()

    tap_outs = {}
    for t in taps:
        shp = {"xn": [NTB * P, D], "kst": [P, 4 * S], "vst": [P, NTB * 512],
               "attn_my": [S, 1024], "attn_g": [4 * S, 1024], "x2": [TSL, D],
               "x2nT": [P, NTBL * D], "h1T": [P, NTBL * D], "h2T": [P, NTBL * D]}[t]
        tap_outs[t] = nc.dram_tensor("tap_" + t, shp, F32, kind="ExternalOutput").ap()

    with ExitStack() as ctx:
        en = ctx.enter_context

        ident = en(nc.sbuf_tensor("ident", [P, P], BF16))
        rs_mix = en(nc.sbuf_tensor("rs_mix", [P, 1], F32))
        svec = en(nc.sbuf_tensor("svec_sb", [P, 3], F32))
        sderiv = en(nc.sbuf_tensor("sderiv", [P, 8], F32))
        st_sum = en(nc.sbuf_tensor("st_sum", [P, 1], F32))
        st_ssq = en(nc.sbuf_tensor("st_ssq", [P, 1], F32))
        st_mean = en(nc.sbuf_tensor("st_mean", [P, 1], F32))
        st_m2 = en(nc.sbuf_tensor("st_m2", [P, 1], F32))
        st_var = en(nc.sbuf_tensor("st_var", [P, 1], F32))
        st_rstd = en(nc.sbuf_tensor("st_rstd", [P, 1], F32))
        st_negmr = en(nc.sbuf_tensor("st_negmr", [P, 1], F32))
        c_eps = en(nc.sbuf_tensor("c_eps", [P, 1], F32))
        c_mhalf = en(nc.sbuf_tensor("c_mhalf", [P, 1], F32))
        c_inv16 = en(nc.sbuf_tensor("c_inv16", [P, 1], F32))

        Sm = {n: Ctr(en(nc.semaphore(n))) for n in [
            "ld_x", "ld_w", "ld_blob", "ld_aux", "st_blob",
            "se_sq", "dv_var", "se_rstd", "dv_negmr", "se_xn",
            "te_tr", "dv_evac", "te_kv", "se_kv", "dv_kbf", "te_ktr", "dv_ktr",
            "te_q", "se_q", "dv_q", "te_qtr", "dv_qfm",
            "te_s", "dv_sevac", "dv_negm", "se_exp", "dv_l",
            "te_pt", "dv_ptevac", "te_pv", "dv_o", "st_attn",
            "cc", "ld_at", "te_at", "dv_at", "ld_wo", "te_o", "dv_x2", "st_x2",
            "ld_x2r", "ld_msk", "dv_wfix", "te_ff", "se_ff", "dv_ff", "st_out",
            "ld_xc", "dv_rs", "st_tap", "te_oh", "te_fh", "dv_fl", "se_fl", "gp_fl", "ld_at2", "cast_w", "ld_blob2",
        ]}

        with nc.Block() as blk:
            @blk.gpsimd
            def _(g):
                g.memset(ident[:], 0.0)
                g.affine_select(out=ident[:], in_=ident[:], compare_op=ALU.not_equal,
                                fill=1.0, base=0, pattern=[[-1, P]], channel_multiplier=1)
                g.memset(rs_mix[:], 1.0)
                g.affine_select(out=rs_mix[:], in_=rs_mix[:], compare_op=ALU.is_ge,
                                fill=-4.0, base=76, pattern=[[0, 1]], channel_multiplier=-1)
                g.memset(c_eps[:], EPS)
                g.memset(c_mhalf[:], -0.5)
                g.memset(c_inv16[:], 1.0 / 16.0)
                Sm["ld_aux"].inc(g.dma_start(out=svec[:], in_=svec_d), 16)

        with nc.Block() as blk:
            @blk.vector
            def _(v):
                Sm["ld_aux"].wait(v)
                for col, (src, sc) in enumerate([(0, 4.0), (1, 4.0), (2, 2.0),
                                                 (0, -2.0), (1, -2.0), (2, -1.0)]):
                    v.tensor_scalar(out=sderiv[:, col:col + 1], in0=svec[:, src:src + 1],
                                    scalar1=sc, scalar2=None, op0=ALU.mult)

        with ExitStack() as actx:
            aen = actx.enter_context
            rope_sb = aen(nc.sbuf_tensor("rope_sb", [P, NTB, 128], F32))
            Kst = aen(nc.sbuf_tensor("Kst", [P, 4, S], BF16))
            Vst = aen(nc.sbuf_tensor("Vst", [P, NTB, 512], BF16))

            def emit_kv_phase(rnd, first):
                with ExitStack() as pctx:
                    pen = pctx.enter_context
                    wk_sb = pen(nc.sbuf_tensor(f"wk_sb{rnd}", [P, NCH, 512], BF16))
                    wv_sb = pen(nc.sbuf_tensor(f"wv_sb{rnd}", [P, NCH, 512], BF16))
                    xn_dm = pen(nc.sbuf_tensor(f"xn_dm{rnd}", [P, D], BF16))
                    k_tm = pen(nc.sbuf_tensor(f"k_tm{rnd}", [P, 512], F32))
                    k_bf = pen(nc.sbuf_tensor(f"k_bf{rnd}", [P, 512], BF16))
                    k_rsc = pen(nc.sbuf_tensor(f"k_rsc{rnd}", [P, 128], F32))
                    pK = pen(nc.psum_tensor(f"pK{rnd}", [P, 512], F32))
                    pV = pen(nc.psum_tensor(f"pV{rnd}", [P, 512], F32))
                    p_ktr = pen(nc.psum_tensor(f"p_ktr{rnd}", [P, 1024], BF16))
                    if first:
                        x_tile = pen(nc.sbuf_tensor("x_tile", [P, 2, D], F32))
                        xn_tm = pen(nc.sbuf_tensor("xn_tm", [P, D], BF16))
                        p_tr = pen(nc.psum_tensor("p_tr", [P, 2, 1024], BF16))
                    else:
                        xn_blob3 = pen(nc.sbuf_tensor("xn_blob3", [P, 4, D], BF16))

                    ld_x_b = Sm["ld_x"].v
                    ld_blob_b = Sm["ld_blob"].v
                    ld_blob2_b = Sm["ld_blob2"].v
                    kv_b = Sm["te_kv"].v
                    sekv_b = Sm["se_kv"].v
                    evac_b = Sm["dv_evac"].v
                    tr_b = Sm["te_tr"].v
                    blob_b = Sm["st_blob"].v
                    ktr_b = Sm["te_ktr"].v
                    dkt_b = Sm["dv_ktr"].v
                    kbf_b = Sm["dv_kbf"].v
                    var_b = Sm["dv_var"].v
                    rstd_b = Sm["se_rstd"].v
                    xn_b = Sm["se_xn"].v

                    def kv_rotary_dv(v, tb):
                        v.wait_ge(Sm["se_kv"].h, sekv_b + 2 * tb + 1)
                        cos = rope_sb[:, tb, 0:64]
                        sin = rope_sb[:, tb, 64:128]
                        for h in (0, 1):
                            kr = k_tm[:, 256 * h:256 * h + 64]
                            t1 = k_rsc[:, 0:64]
                            t2 = k_rsc[:, 64:128]
                            v.tensor_mul(t1, kr, cos)
                            Sm["dv_fl"].inc(v.tensor_mul(t2, kr, sin))
                            Sm["dv_fl"].wait(v)
                            ev = lambda a: a.rearrange("p (i two) -> p i two", two=2)[:, :, 0]
                            od = lambda a: a.rearrange("p (i two) -> p i two", two=2)[:, :, 1]
                            v.tensor_sub(ev(kr), ev(t1), od(t2))
                            Sm["dv_fl"].inc(v.tensor_add(od(kr), od(t1), ev(t2)))
                            Sm["dv_fl"].wait(v)
                        v.wait_ge(Sm["te_ktr"].h, max(0, ktr_b + tb))
                        Sm["dv_kbf"].inc(v.tensor_copy(k_bf[:], k_tm[:]))

                    def kv_kst_evac(s, tb):
                        s.wait_ge(Sm["te_ktr"].h, ktr_b + tb + 1)
                        Sm["dv_ktr"].inc(s.copy(
                            Kst[:, :, P * tb:P * (tb + 1)],
                            p_ktr.rearrange("p (u q) -> p u q", q=P)[:, 0:4, :]))

                    def kv_epi_scalar(s, tb):
                        if tb >= 1:
                            kv_kst_evac(s, tb - 1)
                        s.wait_ge(Sm["te_kv"].h, kv_b + tb + 1)
                        s.wait_ge(Sm["dv_kbf"].h, max(0, kbf_b + tb))
                        Sm["se_kv"].inc(s.copy(k_tm[:], pK[:]))
                        Sm["se_kv"].inc(s.activation(
                            out=Vst[:, tb, :], in_=pV[:], func=AF.Copy))

                    def kv_ktr(t, tb):
                        t.wait_ge(Sm["dv_kbf"].h, kbf_b + tb + 1)
                        t.wait_ge(Sm["dv_ktr"].h, max(0, dkt_b + tb))
                        for u in range(4):
                            ti = t.transpose(p_ktr[:, 128 * u:128 * (u + 1)],
                                             k_bf[:, 128 * u:128 * (u + 1)], ident[:])
                            if u == 3:
                                Sm["te_ktr"].inc(ti)

                    with nc.Block() as blk:
                        @blk.gpsimd
                        def _(g):
                            if first:
                                Sm["ld_w"].inc(g.dma_start(
                                    out=wk_sb[:],
                                    in_=wk_d.rearrange("(c p) n -> p c n", p=P)[:, :, 0:512]), 16)
                                Sm["ld_w"].inc(g.dma_start(
                                    out=wv_sb[:],
                                    in_=wv_d.rearrange("(c p) n -> p c n", p=P)[:, :, 0:512]), 16)
                                if upto >= 3:
                                    Sm["cast_w"].inc(g.dma_start(
                                        out=wkbf1[:], in_=wk_d[:, 512:1024]), 16)
                                    Sm["cast_w"].inc(g.dma_start(
                                        out=wvbf1[:], in_=wv_d[:, 512:1024]), 16)
                                if upto >= 4:
                                    Sm["cast_w"].inc(g.dma_start(
                                        out=wqbf1[:], in_=wq_d[:, 512:1024]), 16)
                            else:
                                g.wait_ge(Sm["cast_w"].h, 32)
                                Sm["ld_w"].inc(g.dma_start(
                                    out=wk_sb[:],
                                    in_=wkbf1.rearrange("(c p) n -> p c n", p=P)), 16)
                                Sm["ld_w"].inc(g.dma_start(
                                    out=wv_sb[:],
                                    in_=wvbf1.rearrange("(c p) n -> p c n", p=P)), 16)

                        ld_w_t = Sm["ld_w"].v

                        @blk.sync
                        def _(sync):
                            if first:
                                Sm["ld_aux"].inc(sync.dma_start(
                                    out=rope_sb[:],
                                    in_=rope_d.rearrange("(tb p) f -> p tb f", p=P)), 16)
                                Sm["ld_x"].inc(sync.dma_start(
                                    out=x_tile[:, 0, :], in_=x_d[0:P, :]), 16)
                            else:
                                Sm["ld_blob"].inc(sync.dma_start(
                                    out=xn_blob3[:, 0, :], in_=xn_dram[0]), 16)
                                Sm["ld_blob2"].inc(sync.dma_start(
                                    out=xn_blob3[:, 1, :], in_=xn_dram[1]), 16)
                            for tb in range(1, NTB):
                                if first:
                                    sync.wait_ge(Sm["se_xn"].h, max(0, xn_b + tb - 1))
                                    Sm["ld_x"].inc(sync.dma_start(
                                        out=x_tile[:, tb % 2, :],
                                        in_=x_d[P * tb:P * (tb + 1), :]), 16)
                                elif tb % 2 == 0:
                                    sync.wait_ge(Sm["te_kv"].h, max(0, kv_b + tb - 3))
                                    Sm["ld_blob"].inc(sync.dma_start(
                                        out=xn_blob3[:, tb % 4, :],
                                        in_=xn_dram[tb]), 16)

                        if first:
                            def ln_chain(v, tb):
                                xt = x_tile[:, tb % 2, :]
                                v.wait_ge(Sm["ld_x"].h, ld_x_b + 16 * (tb + 1))
                                if tb >= 1:
                                    v.wait_ge(Sm["te_tr"].h, tr_b + 8 * tb)
                                    v.wait_ge(Sm["se_rstd"].h, rstd_b + tb)
                                Sm["dv_fl"].inc(v.reduce_sum(st_sum[:], xt, axis=AX.X))
                                Sm["dv_fl"].inc(v.scalar_tensor_tensor(
                                    out=xn_tm[:], in0=xt, scalar=1.0, in1=xt,
                                    op0=ALU.mult, op1=ALU.mult, accum_out=st_ssq[:]))
                                Sm["dv_fl"].wait(v)
                                Sm["dv_fl"].inc(v.tensor_scalar(
                                    out=st_mean[:], in0=st_sum[:],
                                    scalar1=-1.0 / D, scalar2=None, op0=ALU.mult))
                                Sm["dv_fl"].inc(v.scalar_tensor_tensor(
                                    out=st_m2[:], in0=st_sum[:], scalar=1.0 / (D * D),
                                    in1=st_sum[:], op0=ALU.mult, op1=ALU.mult))
                                Sm["dv_fl"].wait(v)
                                Sm["dv_var"].inc(v.scalar_tensor_tensor(
                                    out=st_var[:], in0=st_ssq[:], scalar=1.0 / D,
                                    in1=st_m2[:], op0=ALU.mult, op1=ALU.subtract))
                                v.wait_ge(Sm["se_rstd"].h, rstd_b + tb + 1)
                                Sm["se_xn"].inc(v.tensor_scalar(
                                    out=xn_tm[:], in0=xt, scalar1=st_mean[:],
                                    scalar2=st_rstd[:], op0=ALU.add, op1=ALU.mult))

                            @blk.vector
                            def _(v):
                                Sm["ld_aux"].wait(v)
                                ln_chain(v, 0)
                                for tb in range(NTB):
                                    for j in range(8):
                                        v.wait_ge(Sm["te_tr"].h, tr_b + 8 * tb + j + 1)
                                        if j == 0:
                                            v.wait_ge(Sm["te_kv"].h, max(0, kv_b + tb))
                                            v.wait_ge(Sm["st_blob"].h, max(0, blob_b + 16 * tb))
                                        Sm["dv_evac"].inc(v.tensor_copy(
                                            xn_dm[:, 512 * j:512 * (j + 1)],
                                            p_tr[:, j % 2, 0:512]))
                                    if tb + 1 < NTB:
                                        ln_chain(v, tb + 1)
                                    kv_rotary_dv(v, tb)

                            @blk.scalar
                            def _(s):
                                for tb in range(NTB):
                                    s.wait_ge(Sm["dv_var"].h, var_b + tb + 1)
                                    if tb >= 1:
                                        s.wait_ge(Sm["se_xn"].h, xn_b + tb)
                                    Sm["se_fl"].inc(s.activation(
                                        out=st_rstd[:], in_=st_var[:],
                                        func=AF.Ln, bias=c_eps[:]))
                                    Sm["se_fl"].wait(s)
                                    Sm["se_rstd"].inc(s.activation(
                                        out=st_rstd[:], in_=st_rstd[:], func=AF.Exp,
                                        scale=c_mhalf[:]))
                                    kv_epi_scalar(s, tb)
                                    s.wait_ge(Sm["dv_evac"].h, evac_b + 8 * (tb + 1))
                                    Sm["st_blob"].inc(s.dma_start(
                                        out=xn_dram[tb], in_=xn_dm[:]), 16)
                                kv_kst_evac(s, NTB - 1)

                            @blk.tensor
                            def _(t):
                                for tb in range(NTB):
                                    t.wait_ge(Sm["se_xn"].h, xn_b + tb + 1)

                                    def tgroup(j):
                                        t.wait_ge(Sm["dv_evac"].h, max(0, evac_b + 8 * tb + j - 1))
                                        for u in range(4):
                                            c = 4 * j + u
                                            ti = t.transpose(
                                                p_tr[:, j % 2, 128 * u:128 * (u + 1)],
                                                xn_tm[:, 128 * c:128 * (c + 1)], ident[:])
                                            if u == 3:
                                                Sm["te_tr"].inc(ti)

                                    def mmgroup(j):
                                        t.wait_ge(Sm["dv_evac"].h, evac_b + 8 * tb + j + 1)
                                        if j == 0:
                                            t.wait_ge(Sm["ld_w"].h, ld_w_t)
                                            t.wait_ge(Sm["se_kv"].h, max(0, sekv_b + 2 * tb))
                                        for u in range(4):
                                            c = 4 * j + u
                                            t.matmul(pK[:], xn_dm[:, 128 * c:128 * (c + 1)],
                                                     wk_sb[:, c, :], start=(c == 0), stop=(c == NCH - 1))
                                            mi = t.matmul(pV[:], xn_dm[:, 128 * c:128 * (c + 1)],
                                                          wv_sb[:, c, :], start=(c == 0), stop=(c == NCH - 1))
                                            if c == NCH - 1:
                                                Sm["te_kv"].inc(mi)

                                    tgroup(0)
                                    tgroup(1)
                                    if tb >= 1:
                                        kv_ktr(t, tb - 1)
                                    for j in range(8):
                                        mmgroup(j)
                                        if j + 2 < 8:
                                            tgroup(j + 2)
                                    if tb == NTB - 1:
                                        kv_ktr(t, tb)
                        else:
                            @blk.tensor
                            def _(t):
                                for tb in range(NTB):
                                    src_dm = xn_blob3[:, tb % 4, :]
                                    t.wait_ge(Sm["ld_blob"].h, ld_blob_b + 16 * (tb // 2 + 1))
                                    t.wait_ge(Sm["ld_blob2"].h, ld_blob2_b + 16 * ((tb + 1) // 2))
                                    if tb == 0:
                                        t.wait_ge(Sm["ld_w"].h, ld_w_t)
                                    t.wait_ge(Sm["se_kv"].h, max(0, sekv_b + 2 * tb))
                                    if tb >= 1:
                                        kv_ktr(t, tb - 1)
                                    for c in range(NCH):
                                        t.matmul(pK[:], src_dm[:, 128 * c:128 * (c + 1)],
                                                 wk_sb[:, c, :], start=(c == 0), stop=(c == NCH - 1))
                                        mi = t.matmul(pV[:], src_dm[:, 128 * c:128 * (c + 1)],
                                                      wv_sb[:, c, :], start=(c == 0), stop=(c == NCH - 1))
                                        if c == NCH - 1:
                                            Sm["te_kv"].inc(mi)
                                    if tb == NTB - 1:
                                        kv_ktr(t, tb)

                            @blk.scalar
                            def _(s):
                                for tb in range(NTB):
                                    kv_epi_scalar(s, tb)
                                    if tb % 2 == 0 and tb + 3 < NTB:
                                        Sm["ld_blob2"].inc(s.dma_start(
                                            out=xn_blob3[:, (tb + 3) % 4, :],
                                            in_=xn_dram[tb + 3]), 16)
                                kv_kst_evac(s, NTB - 1)

                            @blk.vector
                            def _(v):
                                for tb in range(NTB):
                                    kv_rotary_dv(v, tb)

                    if first:
                        with nc.Block() as blk:
                            @blk.scalar
                            def _(s):
                                Sm["st_blob"].wait(s)

            def emit_flash_phase(rnd):
                with ExitStack() as pctx:
                    pen = pctx.enter_context
                    wq_sb = pen(nc.sbuf_tensor(f"wq_sb{rnd}", [P, NCH, 512], BF16))
                    xn_blob = pen(nc.sbuf_tensor(f"xn_blob{rnd}", [P, 3, D], BF16))
                    q_tm = pen(nc.sbuf_tensor(f"q_tm{rnd}", [P, 512], F32))
                    q_bf = pen(nc.sbuf_tensor(f"q_bf{rnd}", [P, 512], BF16))
                    q_fm = pen(nc.sbuf_tensor(f"q_fm{rnd}", [P, 4, P], BF16))
                    q_rsc = pen(nc.sbuf_tensor(f"q_rsc{rnd}", [P, 128], F32))
                    S_row = pen(nc.sbuf_tensor(f"S_row{rnd}", [P, S], F32))
                    P_row2 = pen(nc.sbuf_tensor(f"P_row{rnd}", [P, 2, S], BF16))
                    Pt_sb = pen(nc.sbuf_tensor(f"Pt_sb{rnd}", [P, 2, 512], BF16))
                    dmask = pen(nc.sbuf_tensor(f"dmask{rnd}", [P, 4, 512], BF16))
                    mxp = pen(nc.sbuf_tensor(f"mxp{rnd}", [P, 8], F32))
                    lp = pen(nc.sbuf_tensor(f"lp{rnd}", [P, 8], F32))
                    negm2 = pen(nc.sbuf_tensor(f"negm{rnd}", [P, 2], F32))
                    lsum = pen(nc.sbuf_tensor(f"lsum{rnd}", [P, 1], F32))
                    rtmp7 = pen(nc.sbuf_tensor(f"rtmp7_{rnd}", [P, 1], F32))
                    linv = pen(nc.sbuf_tensor(f"linv{rnd}", [P, 1], F32))
                    astage = pen(nc.sbuf_tensor(f"astage{rnd}", [P, 2, 512], BF16))
                    pQ = pen(nc.psum_tensor(f"pQ{rnd}", [P, 512], F32))
                    pS0 = pen(nc.psum_tensor(f"pS0{rnd}", [P, 512], F32))
                    pS1 = pen(nc.psum_tensor(f"pS1{rnd}", [P, 512], F32))
                    p_qtr = pen(nc.psum_tensor(f"p_qtr{rnd}", [P, 1024], BF16))
                    p_pt0 = pen(nc.psum_tensor(f"p_pt0{rnd}", [P, 1024], BF16))
                    p_pt1 = pen(nc.psum_tensor(f"p_pt1{rnd}", [P, 1024], BF16))
                    pO0 = pen(nc.psum_tensor(f"pO0{rnd}", [P, 512], F32))
                    pO1 = pen(nc.psum_tensor(f"pO1{rnd}", [P, 512], F32))
                    pS = [pS0, pS1]
                    p_pt = [p_pt0, p_pt1]
                    pO = [pO0, pO1]

                    Bq = Sm["te_q"].v
                    Bseq = Sm["se_q"].v
                    Bdvq = Sm["dv_q"].v
                    Bqtr = Sm["te_qtr"].v
                    Bqfm = Sm["dv_qfm"].v
                    Bnegm = Sm["dv_negm"].v
                    Bexp = Sm["se_exp"].v
                    Bl = Sm["dv_l"].v
                    Bo = Sm["dv_o"].v
                    Battn = Sm["st_attn"].v
                    Bblob = Sm["ld_blob"].v
                    NKT = [qb // 4 + 1 for qb in range(NTB)]
                    TES = {}
                    cum = 0
                    for qb in range(NTB):
                        for h in (0, 1):
                            TES[(qb, h)] = cum
                            cum += NKT[qb]
                    TES_END = cum
                    Bs = Sm["te_s"].v
                    Bsev = Sm["dv_sevac"].v
                    Bpt = Sm["te_pt"].v
                    Bptv = Sm["dv_ptevac"].v
                    Bpv = Sm["te_pv"].v
                    linv_h2 = [linv[:], lsum[:]]

                    with nc.Block() as blk:
                        @blk.gpsimd
                        def _(g):
                            for a in range(4):
                                g.memset(dmask[:, a, :], 1.0)
                                ai = g.affine_select(out=dmask[:, a, :], in_=dmask[:, a, :],
                                                     compare_op=ALU.is_ge, fill=0.0, base=128 * a,
                                                     pattern=[[-1, 512]], channel_multiplier=1)
                                if a == 3:
                                    Sm["gp_fl"].inc(ai)
                            if rnd == 0:
                                Sm["ld_w"].inc(g.dma_start(
                                    out=wq_sb[:],
                                    in_=wq_d.rearrange("(c p) n -> p c n", p=P)[:, :, 0:512]), 16)
                                if upto >= 6:
                                    for c4 in range(4):
                                        Sm["cast_w"].inc(g.dma_start(
                                            out=wobf[:, 1024 * c4:1024 * (c4 + 1)],
                                            in_=wo_d[:, 1024 * c4:1024 * (c4 + 1)]), 16)
                            else:
                                g.wait_ge(Sm["cast_w"].h, 48)
                                Sm["ld_w"].inc(g.dma_start(
                                    out=wq_sb[:],
                                    in_=wqbf1.rearrange("(c p) n -> p c n", p=P)), 16)

                        ld_w_t = Sm["ld_w"].v
                        gp_t = Sm["gp_fl"].v

                        @blk.sync
                        def _(sync):
                            Sm["ld_blob"].inc(sync.dma_start(
                                out=xn_blob[:, 0, :], in_=xn_dram[0]), 16)
                            for qb in range(1, NTB):
                                sync.wait_ge(Sm["te_q"].h, max(0, Bq + qb - 2))
                                Sm["ld_blob"].inc(sync.dma_start(
                                    out=xn_blob[:, qb % 3, :],
                                    in_=xn_dram[qb]), 16)

                        @blk.tensor
                        def _(t):
                            def ptpv(qb, h):
                                nkt = NKT[qb]
                                t.wait_ge(Sm["dv_l"].h, Bl + 2 * qb + h + 1)
                                t.wait_ge(Sm["dv_o"].h, max(0, Bo + 2 * qb + h - 1))
                                t.wait_ge(Sm["dv_ptevac"].h, Bptv + TES[(qb, h)])

                                def ptr_group(kt):
                                    if kt >= 2:
                                        t.wait_ge(Sm["dv_ptevac"].h, Bptv + TES[(qb, h)] + kt - 1)
                                    for u in range(4):
                                        ti = t.transpose(
                                            p_pt[kt % 2][:, 128 * u:128 * (u + 1)],
                                            P_row2[:, h, 512 * kt + 128 * u:512 * kt + 128 * (u + 1)],
                                            ident[:])
                                        if u == 3:
                                            Sm["te_pt"].inc(ti)

                                def pv_group(kt):
                                    t.wait_ge(Sm["dv_ptevac"].h, Bptv + TES[(qb, h)] + kt + 1)
                                    for u in range(4):
                                        mi = t.matmul(
                                            pO[h][:, 0:256],
                                            Pt_sb[:, kt % 2, 128 * u:128 * (u + 1)],
                                            Vst[:, 4 * kt + u, 256 * h:256 * (h + 1)],
                                            start=(kt == 0 and u == 0),
                                            stop=(kt == nkt - 1 and u == 3))
                                    Sm["te_pv"].inc(mi)

                                ptr_group(0)
                                if nkt > 1:
                                    ptr_group(1)
                                for kt in range(nkt):
                                    pv_group(kt)
                                    if kt + 2 < nkt:
                                        ptr_group(kt + 2)

                            for qb in range(NTB):
                                nkt = NKT[qb]
                                t.wait_ge(Sm["ld_blob"].h, Bblob + 16 * (qb + 1))
                                if qb == 0:
                                    t.wait_ge(Sm["ld_w"].h, ld_w_t)
                                t.wait_ge(Sm["se_q"].h, max(0, Bseq + qb))
                                xb = xn_blob[:, qb % 3, :]
                                for c in range(NCH):
                                    mi = t.matmul(pQ[:], xb[:, 128 * c:128 * (c + 1)],
                                                  wq_sb[:, c, :], start=(c == 0), stop=(c == NCH - 1))
                                Sm["te_q"].inc(mi)
                                t.wait_ge(Sm["dv_q"].h, Bdvq + qb + 1)
                                t.wait_ge(Sm["dv_qfm"].h, max(0, Bqfm + qb))
                                for u in range(4):
                                    ti = t.transpose(p_qtr[:, 128 * u:128 * (u + 1)],
                                                     q_bf[:, 128 * u:128 * (u + 1)], ident[:])
                                    if u == 3:
                                        Sm["te_qtr"].inc(ti)
                                t.wait_ge(Sm["dv_qfm"].h, Bqfm + qb + 1)
                                for h in (0, 1):
                                    t.wait_ge(Sm["dv_sevac"].h, Bsev + TES[(qb, h)])
                                    for kt in range(nkt):
                                        if kt >= 2:
                                            t.wait_ge(Sm["dv_sevac"].h, Bsev + TES[(qb, h)] + kt - 1)
                                        bank = pS[kt % 2]
                                        t.matmul(bank[:], q_fm[:, 2 * h, :],
                                                 Kst[:, 2 * h, 512 * kt:512 * (kt + 1)],
                                                 start=True, stop=False)
                                        mi = t.matmul(bank[:], q_fm[:, 2 * h + 1, :],
                                                      Kst[:, 2 * h + 1, 512 * kt:512 * (kt + 1)],
                                                      start=False, stop=True)
                                        Sm["te_s"].inc(mi)
                                ptpv(qb, 0)
                                ptpv(qb, 1)

                        @blk.scalar
                        def _(s):
                            def attn_store(qb):
                                s.wait_ge(Sm["dv_o"].h, Bo + 2 * (qb + 1))
                                Sm["st_attn"].inc(s.dma_start(
                                    out=attn_myr[rnd][P * qb:P * (qb + 1), :],
                                    in_=astage[:, qb % 2, :]), 16)

                            for qb in range(NTB):
                                nkt = NKT[qb]
                                s.wait_ge(Sm["te_q"].h, Bq + qb + 1)
                                s.wait_ge(Sm["dv_q"].h, max(0, Bdvq + qb))
                                Sm["se_q"].inc(s.activation(
                                    out=q_tm[:], in_=pQ[:], func=AF.Identity, scale=c_inv16[:]))
                                if qb >= 1:
                                    attn_store(qb - 1)
                                for h in (0, 1):
                                    s.wait_ge(Sm["dv_negm"].h, Bnegm + 2 * qb + h + 1)
                                    s.wait_ge(Sm["dv_l"].h, max(0, Bl + 2 * qb + h))
                                    if qb >= 1:
                                        prev_done = TES[(qb - 1, h)] + NKT[qb - 1]
                                        s.wait_ge(Sm["te_pt"].h, Bpt + prev_done)
                                    for kt in range(nkt):
                                        ai = s.activation(
                                            out=P_row2[:, h, 512 * kt:512 * (kt + 1)],
                                            in_=S_row[:, 512 * kt:512 * (kt + 1)],
                                            func=AF.Exp, bias=negm2[:, h:h + 1],
                                            accum_out=lp[:, kt:kt + 1])
                                    Sm["se_exp"].inc(ai)
                                for h in (0, 1):
                                    s.wait_ge(Sm["te_pv"].h, Bpv + TES[(qb, h)] + nkt)
                                    s.wait_ge(Sm["dv_l"].h, Bl + 2 * qb + h + 1)
                                    if h == 0:
                                        s.wait_ge(Sm["st_attn"].h, max(0, Battn + 16 * (qb - 1)))
                                    Sm["dv_o"].inc(s.activation(
                                        out=astage[:, qb % 2, 256 * h:256 * (h + 1)],
                                        in_=pO[h][:, 0:256], func=AF.Identity,
                                        scale=linv_h2[h]))
                                if qb == NTB - 1:
                                    attn_store(qb)

                        @blk.vector
                        def _(v):
                            for qb in range(NTB):
                                nkt = NKT[qb]
                                if qb == 0 and rnd == 0:
                                    Sm["ld_aux"].wait(v)
                                if qb == 0:
                                    v.wait_ge(Sm["gp_fl"].h, gp_t)
                                v.wait_ge(Sm["se_q"].h, Bseq + qb + 1)
                                cos = rope_sb[:, qb, 0:64]
                                sin = rope_sb[:, qb, 64:128]
                                for h in (0, 1):
                                    qr = q_tm[:, 256 * h:256 * h + 64]
                                    t1 = q_rsc[:, 0:64]
                                    t2 = q_rsc[:, 64:128]
                                    v.tensor_mul(t1, qr, cos)
                                    Sm["dv_fl"].inc(v.tensor_mul(t2, qr, sin))
                                    Sm["dv_fl"].wait(v)
                                    ev = lambda a: a.rearrange("p (i two) -> p i two", two=2)[:, :, 0]
                                    od = lambda a: a.rearrange("p (i two) -> p i two", two=2)[:, :, 1]
                                    v.tensor_sub(ev(qr), ev(t1), od(t2))
                                    Sm["dv_fl"].inc(v.tensor_add(od(qr), od(t1), ev(t2)))
                                    Sm["dv_fl"].wait(v)
                                v.wait_ge(Sm["te_qtr"].h, max(0, Bqtr + qb))
                                Sm["dv_q"].inc(v.tensor_copy(q_bf[:], q_tm[:]))
                                v.wait_ge(Sm["te_qtr"].h, Bqtr + qb + 1)
                                v.wait_ge(Sm["te_s"].h, Bs + TES[(qb, 0)])
                                Sm["dv_qfm"].inc(v.tensor_copy(
                                    q_fm.rearrange("p u q -> p (u q)"),
                                    p_qtr[:, 0:512]))
                                for h in (0, 1):
                                    v.wait_ge(Sm["se_exp"].h, max(0, Bexp + 2 * qb + h))
                                    for kt in range(nkt):
                                        v.wait_ge(Sm["te_s"].h, Bs + TES[(qb, h)] + kt + 1)
                                        Sm["dv_sevac"].inc(v.tensor_scalar(
                                            out=S_row[:, 512 * kt:512 * (kt + 1)],
                                            in0=pS[kt % 2][:], scalar1=0.0, scalar2=None,
                                            op0=ALU.add, op1=ALU.max,
                                            accum_out=mxp[:, kt:kt + 1]))
                                    Sm["dv_sevac"].wait(v)
                                    Sm["dv_negm"].inc(v.reduce_max(
                                        negm2[:, h:h + 1], mxp[:, 0:nkt], axis=AX.X, negate=True))
                                    v.wait_ge(Sm["se_exp"].h, Bexp + 2 * qb + h + 1)
                                    dtile = P_row2[:, h, 512 * (nkt - 1):512 * nkt]
                                    Sm["dv_fl"].inc(v.scalar_tensor_tensor(
                                        out=dtile, in0=dtile, scalar=1.0,
                                        in1=dmask[:, qb % 4, :], op0=ALU.mult, op1=ALU.mult,
                                        accum_out=lp[:, nkt - 1:nkt]))
                                    Sm["dv_fl"].wait(v)
                                    Sm["dv_fl"].inc(v.reduce_sum(rtmp7[:, 0:1], lp[:, 0:nkt], axis=AX.X))
                                    Sm["dv_fl"].wait(v)
                                    v.wait_ge(Sm["dv_o"].h, max(0, Bo + 2 * qb + h - 1))
                                    Sm["dv_l"].inc(v.reciprocal(linv_h2[h], rtmp7[:, 0:1]))
                                for h in (0, 1):
                                    v.wait_ge(Sm["te_pv"].h, Bpv + TES[(qb, h)])
                                    for kt in range(NKT[qb]):
                                        v.wait_ge(Sm["te_pt"].h, Bpt + TES[(qb, h)] + kt + 1)
                                        if kt >= 2:
                                            v.wait_ge(Sm["te_pv"].h, Bpv + TES[(qb, h)] + kt - 1)
                                        Sm["dv_ptevac"].inc(v.tensor_copy(
                                            Pt_sb[:, kt % 2, :],
                                            p_pt[kt % 2][:, 0:512]))

                    with nc.Block() as blk:
                        @blk.scalar
                        def _(s):
                            Sm["st_attn"].wait(s)

            def emit_gather(rnd):
                with nc.Block() as blk:
                    @blk.gpsimd
                    def _(g):
                        Sm["st_attn"].wait(g)
                        Sm["cc"].inc(g.collective_compute(
                            "AllGather", ALU.bypass,
                            replica_groups=[[0, 1, 2, 3], [4, 5, 6, 7]],
                            ins=[attn_myr[rnd]],
                            outs=[attn_gr[rnd]]), 1)

            emit_kv_phase(0, True)
            if "xn" in taps:
                _emit_dram_tap(nc, Sm, tap_outs["xn"],
                               xn_dram.rearrange("tb p d -> (tb p) d"))
            if "kst" in taps:
                _emit_sbuf_tap(nc, Sm, tap_outs["kst"], Kst.rearrange("p u s -> p (u s)"))
            if "vst" in taps:
                _emit_sbuf_tap(nc, Sm, tap_outs["vst"], Vst.rearrange("p tb n -> p (tb n)"))
            if upto >= 2:
                emit_flash_phase(0)
            if upto >= 5:
                emit_gather(0)
            if upto >= 3:
                emit_kv_phase(1, False)
            if upto >= 4:
                emit_flash_phase(1)
            if upto >= 5:
                emit_gather(1)

        if upto >= 6:
            with ExitStack() as pctx:
                pen = pctx.enter_context
                attnT = pen(nc.sbuf_tensor("attnT", [P, NTBL, D], BF16))
                wo_sb = pen(nc.sbuf_tensor("wo_sb", [P, 3, 16, 512], BF16))
                xr_sb = pen(nc.sbuf_tensor("xr_sb", [P, NTBL, 512], F32))
                x2st = pen(nc.sbuf_tensor("x2st", [P, 2, 512], F32))
                wo_b = Sm["ld_wo"].v
                xr_b = Sm["ld_x2r"].v
                x2_b = Sm["st_x2"].v

                def wo_src(step):
                    o, ph = step // 2, step % 2
                    return wobf.rearrange("(c p) n -> p c n", p=P)[
                        :, 16 * ph:16 * (ph + 1), 512 * o:512 * (o + 1)]

                with ExitStack() as sctx:
                    sen = sctx.enter_context
                    attn_tm = sen(nc.sbuf_tensor("attn_tm", [P, NTBL, D], BF16))
                    p_at = sen(nc.psum_tensor("p_at", [P, 2, 1024], BF16))
                    at_b = Sm["ld_at"].v
                    at2_b = Sm["ld_at2"].v
                    JL = [0, 2, 4, 6, 1, 3, 5, 7]
                    with nc.Block() as blk:
                        Tat = Sm["te_at"].v
                        Dat = Sm["dv_at"].v

                        @blk.gpsimd
                        def _(g):
                            g.wait_ge(Sm["cc"].h, Sm["cc"].v - 1)
                            gm = g.partition_id() % 4
                            big = attn_gr[0].rearrange("(r s) n -> s r n", r=4)
                            for tb_ in range(NTBL):
                                Sm["ld_at"].inc(g.dma_start(
                                    out=attn_tm[:, tb_, :].rearrange("p (r n) -> p r n", r=4)[:, :, 0:512],
                                    in_=big[bass.ds(gm * TSL + P * tb_, P), :, :]), 16)

                        @blk.scalar
                        def _(sc):
                            Sm["cc"].wait(sc)
                            gm2 = sc.partition_id() % 4
                            big2 = attn_gr[1].rearrange("(r s) n -> s r n", r=4)
                            for tb_ in range(NTBL):
                                Sm["ld_at2"].inc(sc.dma_start(
                                    out=attn_tm[:, tb_, :].rearrange("p (r n) -> p r n", r=4)[:, :, 512:1024],
                                    in_=big2[bass.ds(gm2 * TSL + P * tb_, P), :, :]), 16)

                        @blk.sync
                        def _(sync):
                            sync.wait_ge(Sm["cast_w"].h, 112)
                            for st_ in range(3):
                                Sm["ld_wo"].inc(sync.dma_start(
                                    out=wo_sb[:, st_, :, :], in_=wo_src(st_)), 16)
                            Sm["ld_x2r"].inc(sync.dma_start(
                                out=xr_sb[:],
                                in_=xres_d.rearrange("(tb p) n -> p tb n", p=P)[:, :, 0:512]), 16)

                        @blk.tensor
                        def _(t):
                            for tblk in range(NTBL):
                                for idx, j in enumerate(JL):
                                    if idx == 0:
                                        t.wait_ge(Sm["ld_at"].h, at_b + 16 * (tblk + 1))
                                    if idx == 4:
                                        t.wait_ge(Sm["ld_at2"].h, at2_b + 16 * (tblk + 1))
                                    t.wait_ge(Sm["dv_at"].h, max(0, Dat + 8 * tblk + idx - 1))
                                    for u in range(4):
                                        c = 4 * j + u
                                        ti = t.transpose(
                                            p_at[:, idx % 2, 128 * u:128 * (u + 1)],
                                            attn_tm[:, tblk, 128 * c:128 * (c + 1)], ident[:])
                                        if u == 3:
                                            Sm["te_at"].inc(ti)

                        @blk.vector
                        def _(v):
                            for tblk in range(NTBL):
                                for idx, j in enumerate(JL):
                                    v.wait_ge(Sm["te_at"].h, Tat + 8 * tblk + idx + 1)
                                    Sm["dv_at"].inc(v.tensor_copy(
                                        attnT[:, tblk, 512 * j:512 * (j + 1)],
                                        p_at[:, idx % 2, 0:512]))

                with ExitStack() as sctx:
                    sen = sctx.enter_context
                    pF6 = [sen(nc.psum_tensor(f"pF6_{i}", [P, 512], F32)) for i in range(NTBL)]
                    with nc.Block() as blk:
                        To = Sm["te_o"].v
                        Toh = Sm["te_oh"].v
                        Dx2 = Sm["dv_x2"].v

                        @blk.tensor
                        def _(t):
                            for step in range(16):
                                o, ph = step // 2, step % 2
                                t.wait_ge(Sm["ld_wo"].h, wo_b + 16 * (step + 1))
                                for cl in range(16):
                                    c = 16 * ph + cl
                                    for tblk in range(NTBL):
                                        if c == 0:
                                            t.wait_ge(Sm["dv_x2"].h,
                                                      max(0, Dx2 + 8 * (o - 1) + tblk + 1))
                                        mi = t.matmul(
                                            pF6[tblk][:],
                                            attnT[:, tblk, 128 * c:128 * (c + 1)],
                                            wo_sb[:, step % 3, cl, :],
                                            start=(c == 0), stop=(c == NCH - 1))
                                        if c == NCH - 1 and tblk < NTBL - 1:
                                            Sm["te_o"].inc(mi)
                                        if cl == 15 and tblk == NTBL - 1:
                                            Sm["te_oh"].inc(mi)

                        @blk.vector
                        def _(v):
                            for o in range(8):
                                v.wait_ge(Sm["ld_x2r"].h, xr_b + 16 * (o + 1))
                                for tblk in range(NTBL):
                                    if tblk < NTBL - 1:
                                        v.wait_ge(Sm["te_o"].h, To + 7 * o + tblk + 1)
                                    else:
                                        v.wait_ge(Sm["te_oh"].h, Toh + 2 * o + 2)
                                    v.wait_ge(Sm["st_x2"].h, max(0, x2_b + 16 * (8 * o + tblk - 1)))
                                    Sm["dv_x2"].inc(v.tensor_add(
                                        x2st[:, tblk % 2, :], pF6[tblk][:],
                                        xr_sb[:, tblk, :]))

                        @blk.scalar
                        def _(sc):
                            for o in range(8):
                                for tblk in range(NTBL):
                                    sc.wait_ge(Sm["dv_x2"].h, Dx2 + 8 * o + tblk + 1)
                                    Sm["st_x2"].inc(sc.dma_start(
                                        out=x2_dram[:, 512 * o:512 * (o + 1)]
                                        .rearrange("(tb p) n -> p tb n", p=P)[:, tblk, :],
                                        in_=x2st[:, tblk % 2, :]), 16)

                        @blk.sync
                        def _(sync):
                            for st_ in range(3, 16):
                                sync.wait_ge(Sm["te_oh"].h, Toh + st_ - 2)
                                Sm["ld_wo"].inc(sync.dma_start(
                                    out=wo_sb[:, st_ % 3, :, :], in_=wo_src(st_)), 16)
                                if st_ % 2 == 1:
                                    o_next = (st_ - 1) // 2
                                    sync.wait_ge(Sm["dv_x2"].h, Dx2 + 8 * o_next)
                                    Sm["ld_x2r"].inc(sync.dma_start(
                                        out=xr_sb[:],
                                        in_=xres_d.rearrange("(tb p) n -> p tb n", p=P)[:, :, 512 * o_next:512 * (o_next + 1)]), 16)

                    with nc.Block() as blk:
                        @blk.gpsimd
                        def _(g):
                            Sm["st_x2"].wait(g)
            if "x2" in taps:
                _emit_dram_tap(nc, Sm, tap_outs["x2"], x2_dram)

        if upto >= 7:
            with ExitStack() as pctx:
                pen = pctx.enter_context
                bufA = pen(nc.sbuf_tensor("bufA", [P, NTBL, D], BF16))
                bufB = pen(nc.sbuf_tensor("bufB", [P, NTBL, D], BF16))
                rs_prev = pen(nc.sbuf_tensor("rs_prev", [P, NTBL], F32))
                rs_next = pen(nc.sbuf_tensor("rs_next", [P, NTBL], F32))
                bL = pen(nc.sbuf_tensor("bL", [P, NTBL], F32))
                hp = pen(nc.sbuf_tensor("hp", [P, NTBL, 16], F32))
                rtmp = pen(nc.sbuf_tensor("rtmp", [P, 2], F32))

                with ExitStack() as sctx:
                    sen = sctx.enter_context
                    x2row = sen(nc.sbuf_tensor("x2row", [P, 2, D], F32))
                    x2n_tm = sen(nc.sbuf_tensor("x2n_tm", [P, D], BF16))
                    p_tr7 = sen(nc.psum_tensor("p_tr7", [P, 2, 1024], BF16))
                    x2r_b = Sm["ld_x2r"].v
                    var_b = Sm["dv_var"].v
                    rstd_b = Sm["se_rstd"].v
                    xn_b = Sm["se_xn"].v
                    tr_b = Sm["te_tr"].v
                    evac_b = Sm["dv_evac"].v
                    sq_b = Sm["se_sq"].v

                    with nc.Block() as blk:
                        @blk.sync
                        def _(sync):
                            Sm["st_x2"].wait(sync)
                            Sm["ld_x2r"].inc(sync.dma_start(
                                out=x2row[:, 0, :], in_=x2_dram[0:P, :]), 16)

                    with nc.Block() as blk:
                        @blk.sync
                        def _(sync):
                            for tblk in range(1, NTBL):
                                sync.wait_ge(Sm["se_xn"].h, max(0, xn_b + tblk - 1))
                                Sm["ld_x2r"].inc(sync.dma_start(
                                    out=x2row[:, tblk % 2, :],
                                    in_=x2_dram[P * tblk:P * (tblk + 1), :]), 16)

                        def ln2_chain(v, tblk):
                            xt = x2row[:, tblk % 2, :]
                            v.wait_ge(Sm["ld_x2r"].h, x2r_b + 16 * (tblk + 1))
                            if tblk >= 1:
                                v.wait_ge(Sm["te_tr"].h, tr_b + 8 * tblk)
                                v.wait_ge(Sm["se_sq"].h, sq_b + tblk)
                                v.wait_ge(Sm["se_rstd"].h, rstd_b + tblk)
                            Sm["dv_fl"].inc(v.reduce_sum(st_sum[:], xt, axis=AX.X))
                            Sm["dv_fl"].inc(v.scalar_tensor_tensor(
                                out=x2n_tm[:], in0=xt, scalar=1.0, in1=xt,
                                op0=ALU.mult, op1=ALU.mult, accum_out=st_ssq[:]))
                            Sm["dv_fl"].wait(v)
                            Sm["dv_fl"].inc(v.tensor_scalar(
                                out=st_mean[:], in0=st_sum[:],
                                scalar1=-1.0 / D, scalar2=None, op0=ALU.mult))
                            Sm["dv_fl"].inc(v.scalar_tensor_tensor(
                                out=st_m2[:], in0=st_sum[:], scalar=1.0 / (D * D),
                                in1=st_sum[:], op0=ALU.mult, op1=ALU.mult))
                            Sm["dv_fl"].wait(v)
                            Sm["dv_var"].inc(v.scalar_tensor_tensor(
                                out=st_var[:], in0=st_ssq[:], scalar=1.0 / D,
                                in1=st_m2[:], op0=ALU.mult, op1=ALU.subtract))
                            v.wait_ge(Sm["se_rstd"].h, rstd_b + tblk + 1)
                            Sm["se_xn"].inc(v.tensor_scalar(
                                out=x2n_tm[:], in0=xt, scalar1=st_mean[:],
                                scalar2=st_rstd[:], op0=ALU.add, op1=ALU.mult))

                        @blk.vector
                        def _(v):
                            ln2_chain(v, 0)
                            for tblk in range(NTBL):
                                for j in range(8):
                                    v.wait_ge(Sm["te_tr"].h, tr_b + 8 * tblk + j + 1)
                                    Sm["dv_evac"].inc(v.tensor_copy(
                                        bufA[:, tblk, 512 * j:512 * (j + 1)],
                                        p_tr7[:, j % 2, 0:512]))
                                if tblk + 1 < NTBL:
                                    ln2_chain(v, tblk + 1)

                        @blk.scalar
                        def _(s):
                            for tblk in range(NTBL):
                                s.wait_ge(Sm["dv_var"].h, var_b + tblk + 1)
                                if tblk >= 1:
                                    s.wait_ge(Sm["se_xn"].h, xn_b + tblk)
                                Sm["se_fl"].inc(s.activation(
                                    out=st_rstd[:], in_=st_var[:], func=AF.Ln, bias=c_eps[:]))
                                Sm["se_fl"].wait(s)
                                Sm["se_rstd"].inc(s.activation(
                                    out=st_rstd[:], in_=st_rstd[:], func=AF.Exp, scale=c_mhalf[:]))
                                s.wait_ge(Sm["se_xn"].h, xn_b + tblk + 1)
                                Sm["se_sq"].inc(s.activation(
                                    out=x2n_tm[:], in_=x2n_tm[:], func=AF.Identity,
                                    accum_out=rs_prev[:, tblk:tblk + 1]))

                        @blk.tensor
                        def _(t):
                            for tblk in range(NTBL):
                                t.wait_ge(Sm["se_xn"].h, xn_b + tblk + 1)
                                for j in range(8):
                                    t.wait_ge(Sm["dv_evac"].h, max(0, evac_b + 8 * tblk + j - 1))
                                    for u in range(4):
                                        c = 4 * j + u
                                        ti = t.transpose(
                                            p_tr7[:, j % 2, 128 * u:128 * (u + 1)],
                                            x2n_tm[:, 128 * c:128 * (c + 1)], ident[:])
                                        if u == 3:
                                            Sm["te_tr"].inc(ti)

                with ExitStack() as sctx:
                    sen = sctx.enter_context
                    wm_sb = sen(nc.sbuf_tensor("wm_sb", [P, 2, 16, 512], BF16))
                    x2c = sen(nc.sbuf_tensor("x2c", [P, NTBL, 512], F32))
                    ostage = sen(nc.sbuf_tensor("ostage", [P, 2, 512], F32))
                    otmp = sen(nc.sbuf_tensor("otmp", [P, 512], F32))

                    def mask_src(L, step):
                        o, ph = step // 2, step % 2
                        return km_d[L].rearrange("(c p) n -> p c n", p=P)[
                            :, 16 * ph:16 * (ph + 1), 512 * o:512 * (o + 1)]

                    def emit_layer(L, src_buf, dst_buf, pFF):
                        msk_b = Sm["ld_msk"].v
                        xc_b = Sm["ld_xc"].v
                        out_b = Sm["st_out"].v
                        with nc.Block() as blk:
                            Tff = Sm["te_ff"].v
                            Tfh = Sm["te_fh"].v
                            Wfx = Sm["dv_wfix"].v
                            Sff = Sm["se_ff"].v
                            Dff = Sm["dv_ff"].v

                            @blk.gpsimd
                            def _(g):
                                for st_ in range(16):
                                    if st_ >= 2:
                                        g.wait_ge(Sm["te_fh"].h, Tfh + st_ - 1)
                                    Sm["ld_msk"].inc(g.dma_start(
                                        out=wm_sb[:, st_ % 2, :, :], in_=mask_src(L, st_)), 16)

                            @blk.tensor
                            def _(t):
                                for step in range(16):
                                    o, ph = step // 2, step % 2
                                    t.wait_ge(Sm["ld_msk"].h, msk_b + 16 * (step + 1))
                                    if L >= 1 and ph == 1:
                                        t.wait_ge(Sm["dv_wfix"].h, Wfx + (step + 1) // 2)
                                    for cl in range(16):
                                        c = 16 * ph + cl
                                        for tblk in range(NTBL):
                                            if c == 0:
                                                if L < 2:
                                                    t.wait_ge(Sm["se_ff"].h,
                                                              max(0, Sff + 8 * (o - 1) + tblk + 1))
                                                else:
                                                    t.wait_ge(Sm["dv_ff"].h,
                                                              max(0, Dff + 8 * (o - 1) + tblk + 1))
                                            mi = t.matmul(
                                                pFF[tblk][:],
                                                src_buf[:, tblk, 128 * c:128 * (c + 1)],
                                                wm_sb[:, step % 2, cl, :],
                                                start=(c == 0), stop=(c == NCH - 1))
                                            if c == NCH - 1 and tblk < NTBL - 1:
                                                Sm["te_ff"].inc(mi)
                                            if cl == 15 and tblk == NTBL - 1:
                                                Sm["te_fh"].inc(mi)

                            if L >= 1:
                                @blk.vector
                                def _(v):
                                    if L == 2:
                                        Sm["dv_rs"].wait(v)
                                        for tblk in range(NTBL):
                                            v.tensor_mul(bL[:, tblk:tblk + 1],
                                                         rs_next[:, tblk:tblk + 1],
                                                         sderiv[:, 5:6])
                                    for o in range(8):
                                        step = 2 * o + 1
                                        v.wait_ge(Sm["ld_msk"].h, msk_b + 16 * (step + 1))
                                        buf = step % 2
                                        v.tensor_scalar(
                                            out=wm_sb[:, buf, 9, :], in0=wm_sb[:, buf, 9, :],
                                            scalar1=rs_mix[:], scalar2=None, op0=ALU.mult)
                                        Sm["dv_wfix"].inc(v.tensor_scalar(
                                            out=wm_sb[:, buf, 10:16, :].rearrange("p c n -> p (c n)"),
                                            in0=wm_sb[:, buf, 10:16, :].rearrange("p c n -> p (c n)"),
                                            scalar1=-4.0, scalar2=None, op0=ALU.mult))
                                        if L == 2:
                                            v.wait_ge(Sm["ld_xc"].h, xc_b + 16 * (o + 1))
                                            for tblk in range(NTBL):
                                                if tblk < NTBL - 1:
                                                    v.wait_ge(Sm["te_ff"].h, Tff + 7 * o + tblk + 1)
                                                else:
                                                    v.wait_ge(Sm["te_fh"].h, Tfh + 2 * o + 2)
                                                v.wait_ge(Sm["st_out"].h, max(0, out_b + 16 * (8 * o + tblk - 1)))
                                                Sm["dv_fl"].inc(v.tensor_scalar(
                                                    out=otmp[:], in0=x2c[:, tblk, :],
                                                    scalar1=bL[:, tblk:tblk + 1],
                                                    scalar2=None, op0=ALU.add))
                                                Sm["dv_fl"].wait(v)
                                                Sm["dv_ff"].inc(v.scalar_tensor_tensor(
                                                    out=ostage[:, tblk % 2, :],
                                                    in0=pFF[tblk][:], scalar=sderiv[:, 2:3],
                                                    in1=otmp[:], op0=ALU.mult, op1=ALU.add))

                            if L < 2:
                                @blk.scalar
                                def _(s):
                                    if L == 1:
                                        Sm["dv_rs"].wait(s)
                                        s.copy(rs_prev[:], rs_next[:])
                                    for tblk in range(NTBL):
                                        s.activation(out=bL[:, tblk:tblk + 1],
                                                     in_=rs_prev[:, tblk:tblk + 1],
                                                     func=AF.Identity,
                                                     scale=sderiv[:, 3 + L:4 + L])
                                    for o in range(8):
                                        for tblk in range(NTBL):
                                            if tblk < NTBL - 1:
                                                s.wait_ge(Sm["te_ff"].h, Tff + 7 * o + tblk + 1)
                                            else:
                                                s.wait_ge(Sm["te_fh"].h, Tfh + 2 * o + 2)
                                            scale_ap = sderiv[:, L:L + 1]
                                            bias_ap = bL[:, tblk:tblk + 1]
                                            if o == 6:
                                                si = s.activation(
                                                    out=dst_buf[:, tblk, 3072:NE],
                                                    in_=pFF[tblk][:, 0:NE - 3072],
                                                    func=AF.Relu, scale=scale_ap, bias=bias_ap,
                                                    accum_out=hp[:, tblk, 6:7])
                                                si = s.activation(
                                                    out=dst_buf[:, tblk, NE:3584],
                                                    in_=pFF[tblk][:, NE - 3072:512],
                                                    func=AF.Relu, scale=scale_ap, bias=bias_ap,
                                                    accum_out=hp[:, tblk, 7:8])
                                            else:
                                                col = o if o < 6 else 8
                                                si = s.activation(
                                                    out=dst_buf[:, tblk, 512 * o:512 * (o + 1)],
                                                    in_=pFF[tblk][:],
                                                    func=AF.Relu, scale=scale_ap, bias=bias_ap,
                                                    accum_out=hp[:, tblk, col:col + 1])
                                            Sm["se_ff"].inc(si)
                            else:
                                @blk.scalar
                                def _(sc):
                                    for o in range(8):
                                        for tblk in range(NTBL):
                                            sc.wait_ge(Sm["dv_ff"].h, Dff + 8 * o + tblk + 1)
                                            Sm["st_out"].inc(sc.dma_start(
                                                out=out_d[:, 512 * o:512 * (o + 1)]
                                                .rearrange("(tb p) n -> p tb n", p=P)[:, tblk, :],
                                                in_=ostage[:, tblk % 2, :]), 16)

                                @blk.sync
                                def _(sync):
                                    Sm["ld_xc"].inc(sync.dma_start(
                                        out=x2c[:],
                                        in_=x2_dram.rearrange("(tb p) n -> p tb n", p=P)[:, :, 0:512]), 16)
                                    for o in range(7):
                                        sync.wait_ge(Sm["dv_ff"].h, Dff + 8 * (o + 1))
                                        Sm["ld_xc"].inc(sync.dma_start(
                                            out=x2c[:],
                                            in_=x2_dram.rearrange("(tb p) n -> p tb n", p=P)[:, :, 512 * (o + 1):512 * (o + 2)]), 16)

                        if L < 2:
                            with nc.Block() as blk:
                                @blk.vector
                                def _(v):
                                    Sm["se_ff"].wait(v)
                                    for tblk in range(NTBL):
                                        v.reduce_sum(rtmp[:, 0:1], hp[:, tblk, 0:7], axis=AX.X)
                                        Sm["dv_fl"].inc(v.reduce_sum(rtmp[:, 1:2], hp[:, tblk, 7:9], axis=AX.X), 2)
                                        Sm["dv_fl"].wait(v)
                                        Sm["dv_rs"].inc(v.scalar_tensor_tensor(
                                            out=rs_next[:, tblk:tblk + 1], in0=rtmp[:, 1:2],
                                            scalar=-4.0, in1=rtmp[:, 0:1],
                                            op0=ALU.mult, op1=ALU.add))

                    _trn = [0]

                    def emit_transpose(src_buf, dst_buf):
                        _trn[0] += 1
                        with ExitStack() as tctx:
                            p_trx = tctx.enter_context(nc.psum_tensor(f"p_trx{_trn[0]}", [P, 2, 1024], BF16))
                            with nc.Block() as blk:
                                tetr_b = Sm["te_tr"].v
                                dvev_b = Sm["dv_evac"].v

                                @blk.tensor
                                def _(t):
                                    for tblk in range(NTBL):
                                        for j in range(8):
                                            t.wait_ge(Sm["dv_evac"].h,
                                                      max(0, dvev_b + 8 * tblk + j - 1))
                                            for u in range(4):
                                                c = 4 * j + u
                                                ti = t.transpose(
                                                    p_trx[:, j % 2, 128 * u:128 * (u + 1)],
                                                    src_buf[:, tblk, 128 * c:128 * (c + 1)], ident[:])
                                                if u == 3:
                                                    Sm["te_tr"].inc(ti)

                                @blk.vector
                                def _(v):
                                    for tblk in range(NTBL):
                                        for j in range(8):
                                            v.wait_ge(Sm["te_tr"].h, tetr_b + 8 * tblk + j + 1)
                                            Sm["dv_evac"].inc(v.tensor_copy(
                                                dst_buf[:, tblk, 512 * j:512 * (j + 1)],
                                                p_trx[:, j % 2, 0:512]))

                    def layer_scoped(L, src_buf, dst_buf):
                        with ExitStack() as lctx:
                            pFF = [lctx.enter_context(
                                nc.psum_tensor(f"pFF{L}_{i}", [P, 512], F32))
                                for i in range(NTBL)]
                            emit_layer(L, src_buf, dst_buf, pFF)

                    if "x2nT" in taps:
                        _emit_sbuf_tap(nc, Sm, tap_outs["x2nT"], bufA.rearrange("p tb d -> p (tb d)"))
                    layer_scoped(0, bufA, bufB)
                    emit_transpose(bufB, bufA)
                    if "h1T" in taps:
                        _emit_sbuf_tap(nc, Sm, tap_outs["h1T"], bufA.rearrange("p tb d -> p (tb d)"))
                    layer_scoped(1, bufA, bufB)
                    emit_transpose(bufB, bufA)
                    if "h2T" in taps:
                        _emit_sbuf_tap(nc, Sm, tap_outs["h2T"], bufA.rearrange("p tb d -> p (tb d)"))
                    layer_scoped(2, bufA, None)

                    with nc.Block() as blk:
                        @blk.gpsimd
                        def _(g):
                            Sm["st_out"].wait(g)

    return nc


def _emit_dram_tap(nc, Sm, dst_f32, src_ap):
    rows = src_ap.shape[0]
    step = P
    _TAPN[0] += 1
    with nc.sbuf_tensor(f"tapbuf_{_TAPN[0]}", [P, src_ap.shape[1]], F32) as tb_:
        for r0 in range(0, rows, step):
            with nc.Block() as blk:
                @blk.gpsimd
                def _(g):
                    Sm["st_tap"].inc(g.dma_start(out=tb_[0:min(step, rows - r0), :],
                                                 in_=src_ap[r0:r0 + min(step, rows - r0), :]), 16)
                    Sm["st_tap"].wait(g)
                    Sm["st_tap"].inc(g.dma_start(out=dst_f32[r0:r0 + min(step, rows - r0), :],
                                                 in_=tb_[0:min(step, rows - r0), :]), 16)
                    Sm["st_tap"].wait(g)


_TAPN = [0]


def _emit_sbuf_tap(nc, Sm, dst_f32, src_sbuf_ap):
    cols = src_sbuf_ap.shape[1]
    CH = 4096
    _TAPN[0] += 1
    with nc.sbuf_tensor(f"tapbuf2_{_TAPN[0]}", [P, CH], F32) as tb_:
        for c0 in range(0, cols, CH):
            w = min(CH, cols - c0)
            with nc.Block() as blk:
                @blk.vector
                def _(v):
                    v.tensor_copy(tb_[:, 0:w], src_sbuf_ap[:, c0:c0 + w])
            with nc.Block() as blk:
                @blk.gpsimd
                def _(g):
                    Sm["st_tap"].inc(g.dma_start(out=dst_f32[:, c0:c0 + w], in_=tb_[:, 0:w]), 16)
                    Sm["st_tap"].wait(g)


def _host_prep(inputs):
    x = np.asarray(inputs["L0_attn_hidden_states"], np.float32)
    wq = np.asarray(inputs["L0_attn_wq"], np.float32)
    wk = np.asarray(inputs["L0_attn_wk"], np.float32)
    wv = np.asarray(inputs["L0_attn_wv"], np.float32)
    wo = np.asarray(inputs["L0_attn_wo"], np.float32)
    k0 = np.ascontiguousarray(np.asarray(inputs["L0_ffn_k0"]).view(np.uint8))
    k1 = np.ascontiguousarray(np.asarray(inputs["L0_ffn_k1"]).view(np.uint8))
    k2 = np.ascontiguousarray(np.asarray(inputs["L0_ffn_k2"]).view(np.uint8))
    sv = np.ascontiguousarray(np.tile(np.array(
        [np.float32(inputs["L0_ffn_s0"]), np.float32(inputs["L0_ffn_s1"]),
         np.float32(inputs["L0_ffn_s2"])], dtype=np.float32)[None, :], (P, 1)))
    inv = 1.0 / 10000.0 ** (np.arange(0, 64, 2, dtype=np.float32) / 64.0)
    ang = np.arange(S, dtype=np.float32)[:, None] * inv[None, :]
    cosr = np.repeat(np.cos(ang), 2, axis=1).astype(np.float32)
    sinr = np.repeat(np.sin(ang), 2, axis=1).astype(np.float32)
    rope = np.ascontiguousarray(np.concatenate([cosr, sinr], axis=1))
    woT = np.ascontiguousarray(wo.T)

    in_maps = []
    for c in range(8):
        b, g = c // 4, c % 4
        sl = slice(1024 * g, 1024 * (g + 1))
        in_maps.append({
            "x": np.ascontiguousarray(x[b]),
            "x_res": np.ascontiguousarray(x[b, sl, :]),
            "wq_sl": np.ascontiguousarray(wq[sl, :].T),
            "wk_sl": np.ascontiguousarray(wk[sl, :].T),
            "wv_sl": np.ascontiguousarray(wv[sl, :].T),
            "woT": woT,
            "k0": k0, "k1": k1, "k2": k2,
            "rope": rope, "svec": sv,
        })
    return in_maps


def kernel(**inputs):
    in_maps = _host_prep(inputs)
    nc = build_nc(upto=7)
    res = run_bass_kernel_spmd(nc, in_maps, core_ids=list(range(8)))
    out = np.empty((2, S, D), np.float32)
    for c in range(8):
        b, g = c // 4, c % 4
        out[b, 1024 * g:1024 * (g + 1), :] = res.results[c]["out"]
    return out


# version 14
# speedup vs baseline: 1.2151x; 1.0615x over previous
import numpy as np
import ml_dtypes
import concourse.bass as bass
import concourse.mybir as mybir
from concourse.bass_utils import run_bass_kernel_spmd
from contextlib import ExitStack

F32 = mybir.dt.float32
BF16 = mybir.dt.bfloat16
U8 = mybir.dt.uint8
AF = mybir.ActivationFunctionType
ALU = mybir.AluOpType
AX = mybir.AxisListType
BFNP = ml_dtypes.bfloat16

P = 128
S, D = 4096, 4096
NTB = S // P
TSL = 1024
NTBL = TSL // P
NE = 3277
EPS = 1e-5
NCH = D // P


class Ctr:
    def __init__(self, h):
        self.h = h
        self.v = 0

    def inc(self, instr, n=1):
        self.v += n
        instr.then_inc(self.h, n)

    def wait(self, eng, v=None):
        eng.wait_ge(self.h, max(0, self.v if v is None else v))


def build_nc(upto=7, taps=()):
    nc = bass.Bass("TRN2", target_bir_lowering=False)

    x_d = nc.dram_tensor("x", [S, D], F32, kind="ExternalInput").ap()
    xres_d = nc.dram_tensor("x_res", [TSL, D], F32, kind="ExternalInput").ap()
    wq_d = nc.dram_tensor("wq_sl", [D, 1024], F32, kind="ExternalInput").ap()
    wk_d = nc.dram_tensor("wk_sl", [D, 1024], F32, kind="ExternalInput").ap()
    wv_d = nc.dram_tensor("wv_sl", [D, 1024], F32, kind="ExternalInput").ap()
    wo_d = nc.dram_tensor("woT", [D, D], F32, kind="ExternalInput").ap()
    km_d = [nc.dram_tensor(f"k{i}", [D, D], U8, kind="ExternalInput").ap() for i in range(3)]
    rope_d = nc.dram_tensor("rope", [S, 128], F32, kind="ExternalInput").ap()
    svec_d = nc.dram_tensor("svec", [P, 3], F32, kind="ExternalInput").ap()
    out_d = nc.dram_tensor("out", [TSL, D], F32, kind="ExternalOutput").ap()

    xn_dram = nc.dram_tensor("xn_dram", [NTB, P, D], BF16, kind="Internal").ap()
    attn_myr = [nc.dram_tensor(f"attn_my{r}", [S, 512], BF16, kind="Internal").ap()
                for r in range(2)]
    attn_gr = [nc.dram_tensor(f"attn_g{r}", [4 * S, 512], BF16, kind="Internal").ap()
               for r in range(2)]
    x2_dram = nc.dram_tensor("x2_dram", [TSL, D], F32, kind="Internal").ap()
    wkbf1 = nc.dram_tensor("wkbf1", [D, 512], BF16, kind="Internal").ap()
    wvbf1 = nc.dram_tensor("wvbf1", [D, 512], BF16, kind="Internal").ap()
    wqbf1 = nc.dram_tensor("wqbf1", [D, 512], BF16, kind="Internal").ap()
    wobf = nc.dram_tensor("wobf", [D, D], BF16, kind="Internal").ap()

    tap_outs = {}
    for t in taps:
        shp = {"xn": [NTB * P, D], "kst": [P, 4 * S], "vst": [P, NTB * 512],
               "attn_my": [S, 1024], "attn_g": [4 * S, 1024], "x2": [TSL, D],
               "x2nT": [P, NTBL * D], "h1T": [P, NTBL * D], "h2T": [P, NTBL * D]}[t]
        tap_outs[t] = nc.dram_tensor("tap_" + t, shp, F32, kind="ExternalOutput").ap()

    with ExitStack() as ctx:
        en = ctx.enter_context

        ident = en(nc.sbuf_tensor("ident", [P, P], BF16))
        rs_mix = en(nc.sbuf_tensor("rs_mix", [P, 1], F32))
        svec = en(nc.sbuf_tensor("svec_sb", [P, 3], F32))
        sderiv = en(nc.sbuf_tensor("sderiv", [P, 8], F32))
        st_sum = en(nc.sbuf_tensor("st_sum", [P, 1], F32))
        st_ssq = en(nc.sbuf_tensor("st_ssq", [P, 1], F32))
        st_mean = en(nc.sbuf_tensor("st_mean", [P, 1], F32))
        st_m2 = en(nc.sbuf_tensor("st_m2", [P, 1], F32))
        st_var = en(nc.sbuf_tensor("st_var", [P, 1], F32))
        st_rstd = en(nc.sbuf_tensor("st_rstd", [P, 1], F32))
        st_negmr = en(nc.sbuf_tensor("st_negmr", [P, 1], F32))
        c_eps = en(nc.sbuf_tensor("c_eps", [P, 1], F32))
        c_mhalf = en(nc.sbuf_tensor("c_mhalf", [P, 1], F32))
        c_inv16 = en(nc.sbuf_tensor("c_inv16", [P, 1], F32))

        Sm = {n: Ctr(en(nc.semaphore(n))) for n in [
            "ld_x", "ld_w", "ld_blob", "ld_aux", "st_blob",
            "se_sq", "dv_var", "se_rstd", "dv_negmr", "se_xn",
            "te_tr", "dv_evac", "te_kv", "se_kv", "dv_kbf", "te_ktr", "dv_ktr",
            "te_q", "se_q", "dv_q", "te_qtr", "dv_qfm",
            "te_s", "dv_sevac", "dv_negm", "se_exp", "dv_l",
            "te_pt", "dv_ptevac", "te_pv", "dv_o", "st_attn",
            "cc", "ld_at", "te_at", "dv_at", "ld_wo", "te_o", "dv_x2", "st_x2",
            "ld_x2r", "ld_msk", "dv_wfix", "te_ff", "se_ff", "dv_ff", "st_out",
            "ld_xc", "dv_rs", "st_tap", "te_oh", "te_fh", "dv_fl", "se_fl", "gp_fl", "ld_at2", "cast_w", "ld_blob2",
        ]}

        with nc.Block() as blk:
            @blk.gpsimd
            def _(g):
                g.memset(ident[:], 0.0)
                g.affine_select(out=ident[:], in_=ident[:], compare_op=ALU.not_equal,
                                fill=1.0, base=0, pattern=[[-1, P]], channel_multiplier=1)
                g.memset(rs_mix[:], 1.0)
                g.affine_select(out=rs_mix[:], in_=rs_mix[:], compare_op=ALU.is_ge,
                                fill=-4.0, base=76, pattern=[[0, 1]], channel_multiplier=-1)
                g.memset(c_eps[:], EPS)
                g.memset(c_mhalf[:], -0.5)
                g.memset(c_inv16[:], 1.0 / 16.0)
                Sm["ld_aux"].inc(g.dma_start(out=svec[:], in_=svec_d), 16)

        with nc.Block() as blk:
            @blk.vector
            def _(v):
                Sm["ld_aux"].wait(v)
                for col, (src, sc) in enumerate([(0, 4.0), (1, 4.0), (2, 2.0),
                                                 (0, -2.0), (1, -2.0), (2, -1.0)]):
                    v.tensor_scalar(out=sderiv[:, col:col + 1], in0=svec[:, src:src + 1],
                                    scalar1=sc, scalar2=None, op0=ALU.mult)

        with ExitStack() as actx:
            aen = actx.enter_context
            rope_sb = aen(nc.sbuf_tensor("rope_sb", [P, NTB, 128], F32))
            Kst = aen(nc.sbuf_tensor("Kst", [P, 4, S], BF16))
            Vst = aen(nc.sbuf_tensor("Vst", [P, NTB, 512], BF16))

            def emit_kv_phase(rnd, first):
                with ExitStack() as pctx:
                    pen = pctx.enter_context
                    wk_sb = pen(nc.sbuf_tensor(f"wk_sb{rnd}", [P, NCH, 512], BF16))
                    wv_sb = pen(nc.sbuf_tensor(f"wv_sb{rnd}", [P, NCH, 512], BF16))
                    xn_dm = pen(nc.sbuf_tensor(f"xn_dm{rnd}", [P, D], BF16))
                    k_tm = pen(nc.sbuf_tensor(f"k_tm{rnd}", [P, 512], F32))
                    k_bf = pen(nc.sbuf_tensor(f"k_bf{rnd}", [P, 512], BF16))
                    k_rsc = pen(nc.sbuf_tensor(f"k_rsc{rnd}", [P, 128], F32))
                    pK = pen(nc.psum_tensor(f"pK{rnd}", [P, 512], F32))
                    pV = pen(nc.psum_tensor(f"pV{rnd}", [P, 512], F32))
                    p_ktr = pen(nc.psum_tensor(f"p_ktr{rnd}", [P, 1024], BF16))
                    if first:
                        x_tile = pen(nc.sbuf_tensor("x_tile", [P, 2, D], F32))
                        xn_tm = pen(nc.sbuf_tensor("xn_tm", [P, D], BF16))
                        p_tr = pen(nc.psum_tensor("p_tr", [P, 2, 1024], BF16))
                    else:
                        xn_blob3 = pen(nc.sbuf_tensor("xn_blob3", [P, 4, D], BF16))

                    ld_x_b = Sm["ld_x"].v
                    ld_blob_b = Sm["ld_blob"].v
                    ld_blob2_b = Sm["ld_blob2"].v
                    kv_b = Sm["te_kv"].v
                    sekv_b = Sm["se_kv"].v
                    evac_b = Sm["dv_evac"].v
                    tr_b = Sm["te_tr"].v
                    blob_b = Sm["st_blob"].v
                    ktr_b = Sm["te_ktr"].v
                    dkt_b = Sm["dv_ktr"].v
                    kbf_b = Sm["dv_kbf"].v
                    var_b = Sm["dv_var"].v
                    rstd_b = Sm["se_rstd"].v
                    xn_b = Sm["se_xn"].v

                    def kv_rotary_dv(v, tb):
                        v.wait_ge(Sm["se_kv"].h, sekv_b + 2 * tb + 1)
                        cos = rope_sb[:, tb, 0:64]
                        sin = rope_sb[:, tb, 64:128]
                        for h in (0, 1):
                            kr = k_tm[:, 256 * h:256 * h + 64]
                            t1 = k_rsc[:, 0:64]
                            t2 = k_rsc[:, 64:128]
                            v.tensor_mul(t1, kr, cos)
                            Sm["dv_fl"].inc(v.tensor_mul(t2, kr, sin))
                            Sm["dv_fl"].wait(v)
                            ev = lambda a: a.rearrange("p (i two) -> p i two", two=2)[:, :, 0]
                            od = lambda a: a.rearrange("p (i two) -> p i two", two=2)[:, :, 1]
                            v.tensor_sub(ev(kr), ev(t1), od(t2))
                            Sm["dv_fl"].inc(v.tensor_add(od(kr), od(t1), ev(t2)))
                            Sm["dv_fl"].wait(v)
                        v.wait_ge(Sm["te_ktr"].h, max(0, ktr_b + tb))
                        Sm["dv_kbf"].inc(v.tensor_copy(k_bf[:], k_tm[:]))

                    def kv_kst_evac(s, tb):
                        s.wait_ge(Sm["te_ktr"].h, ktr_b + tb + 1)
                        Sm["dv_ktr"].inc(s.copy(
                            Kst[:, :, P * tb:P * (tb + 1)],
                            p_ktr.rearrange("p (u q) -> p u q", q=P)[:, 0:4, :]))

                    def kv_epi_scalar(s, tb):
                        if tb >= 1:
                            kv_kst_evac(s, tb - 1)
                        s.wait_ge(Sm["te_kv"].h, kv_b + tb + 1)
                        s.wait_ge(Sm["dv_kbf"].h, max(0, kbf_b + tb))
                        Sm["se_kv"].inc(s.copy(k_tm[:], pK[:]))
                        Sm["se_kv"].inc(s.activation(
                            out=Vst[:, tb, :], in_=pV[:], func=AF.Copy))

                    def kv_ktr(t, tb):
                        t.wait_ge(Sm["dv_kbf"].h, kbf_b + tb + 1)
                        t.wait_ge(Sm["dv_ktr"].h, max(0, dkt_b + tb))
                        for u in range(4):
                            ti = t.transpose(p_ktr[:, 128 * u:128 * (u + 1)],
                                             k_bf[:, 128 * u:128 * (u + 1)], ident[:])
                            if u == 3:
                                Sm["te_ktr"].inc(ti)

                    with nc.Block() as blk:
                        @blk.gpsimd
                        def _(g):
                            if first:
                                Sm["ld_w"].inc(g.dma_start(
                                    out=wk_sb[:],
                                    in_=wk_d.rearrange("(c p) n -> p c n", p=P)[:, :, 0:512]), 16)
                                Sm["ld_w"].inc(g.dma_start(
                                    out=wv_sb[:],
                                    in_=wv_d.rearrange("(c p) n -> p c n", p=P)[:, :, 0:512]), 16)
                                if upto >= 3:
                                    Sm["cast_w"].inc(g.dma_start(
                                        out=wkbf1[:], in_=wk_d[:, 512:1024]), 16)
                                    Sm["cast_w"].inc(g.dma_start(
                                        out=wvbf1[:], in_=wv_d[:, 512:1024]), 16)
                                if upto >= 4:
                                    Sm["cast_w"].inc(g.dma_start(
                                        out=wqbf1[:], in_=wq_d[:, 512:1024]), 16)
                            else:
                                g.wait_ge(Sm["cast_w"].h, 32)
                                Sm["ld_w"].inc(g.dma_start(
                                    out=wk_sb[:],
                                    in_=wkbf1.rearrange("(c p) n -> p c n", p=P)), 16)
                                Sm["ld_w"].inc(g.dma_start(
                                    out=wv_sb[:],
                                    in_=wvbf1.rearrange("(c p) n -> p c n", p=P)), 16)

                        ld_w_t = Sm["ld_w"].v

                        @blk.sync
                        def _(sync):
                            if first:
                                Sm["ld_aux"].inc(sync.dma_start(
                                    out=rope_sb[:],
                                    in_=rope_d.rearrange("(tb p) f -> p tb f", p=P)), 16)
                                Sm["ld_x"].inc(sync.dma_start(
                                    out=x_tile[:, 0, :], in_=x_d[0:P, :]), 16)
                            else:
                                Sm["ld_blob"].inc(sync.dma_start(
                                    out=xn_blob3[:, 0, :], in_=xn_dram[0]), 16)
                                Sm["ld_blob2"].inc(sync.dma_start(
                                    out=xn_blob3[:, 1, :], in_=xn_dram[1]), 16)
                            for tb in range(1, NTB):
                                if first:
                                    sync.wait_ge(Sm["se_xn"].h, max(0, xn_b + tb - 1))
                                    Sm["ld_x"].inc(sync.dma_start(
                                        out=x_tile[:, tb % 2, :],
                                        in_=x_d[P * tb:P * (tb + 1), :]), 16)
                                elif tb % 2 == 0:
                                    sync.wait_ge(Sm["te_kv"].h, max(0, kv_b + tb - 3))
                                    Sm["ld_blob"].inc(sync.dma_start(
                                        out=xn_blob3[:, tb % 4, :],
                                        in_=xn_dram[tb]), 16)

                        if first:
                            def ln_chain(v, tb):
                                xt = x_tile[:, tb % 2, :]
                                v.wait_ge(Sm["ld_x"].h, ld_x_b + 16 * (tb + 1))
                                if tb >= 1:
                                    v.wait_ge(Sm["te_tr"].h, tr_b + 8 * tb)
                                    v.wait_ge(Sm["se_rstd"].h, rstd_b + tb)
                                Sm["dv_fl"].inc(v.reduce_sum(st_sum[:], xt, axis=AX.X))
                                Sm["dv_fl"].inc(v.scalar_tensor_tensor(
                                    out=xn_tm[:], in0=xt, scalar=1.0, in1=xt,
                                    op0=ALU.mult, op1=ALU.mult, accum_out=st_ssq[:]))
                                Sm["dv_fl"].wait(v)
                                Sm["dv_fl"].inc(v.tensor_scalar(
                                    out=st_mean[:], in0=st_sum[:],
                                    scalar1=-1.0 / D, scalar2=None, op0=ALU.mult))
                                Sm["dv_fl"].inc(v.scalar_tensor_tensor(
                                    out=st_m2[:], in0=st_sum[:], scalar=1.0 / (D * D),
                                    in1=st_sum[:], op0=ALU.mult, op1=ALU.mult))
                                Sm["dv_fl"].wait(v)
                                Sm["dv_var"].inc(v.scalar_tensor_tensor(
                                    out=st_var[:], in0=st_ssq[:], scalar=1.0 / D,
                                    in1=st_m2[:], op0=ALU.mult, op1=ALU.subtract))
                                v.wait_ge(Sm["se_rstd"].h, rstd_b + tb + 1)
                                Sm["se_xn"].inc(v.tensor_scalar(
                                    out=xn_tm[:], in0=xt, scalar1=st_mean[:],
                                    scalar2=st_rstd[:], op0=ALU.add, op1=ALU.mult))

                            @blk.vector
                            def _(v):
                                Sm["ld_aux"].wait(v)
                                ln_chain(v, 0)
                                for tb in range(NTB):
                                    for j in range(8):
                                        v.wait_ge(Sm["te_tr"].h, tr_b + 8 * tb + j + 1)
                                        if j == 0:
                                            v.wait_ge(Sm["te_kv"].h, max(0, kv_b + tb))
                                            v.wait_ge(Sm["st_blob"].h, max(0, blob_b + 16 * tb))
                                        Sm["dv_evac"].inc(v.tensor_copy(
                                            xn_dm[:, 512 * j:512 * (j + 1)],
                                            p_tr[:, j % 2, 0:512]))
                                    if tb + 1 < NTB:
                                        ln_chain(v, tb + 1)
                                    kv_rotary_dv(v, tb)

                            @blk.scalar
                            def _(s):
                                for tb in range(NTB):
                                    s.wait_ge(Sm["dv_var"].h, var_b + tb + 1)
                                    if tb >= 1:
                                        s.wait_ge(Sm["se_xn"].h, xn_b + tb)
                                    Sm["se_fl"].inc(s.activation(
                                        out=st_rstd[:], in_=st_var[:],
                                        func=AF.Ln, bias=c_eps[:]))
                                    Sm["se_fl"].wait(s)
                                    Sm["se_rstd"].inc(s.activation(
                                        out=st_rstd[:], in_=st_rstd[:], func=AF.Exp,
                                        scale=c_mhalf[:]))
                                    kv_epi_scalar(s, tb)
                                    s.wait_ge(Sm["dv_evac"].h, evac_b + 8 * (tb + 1))
                                    Sm["st_blob"].inc(s.dma_start(
                                        out=xn_dram[tb], in_=xn_dm[:]), 16)
                                kv_kst_evac(s, NTB - 1)

                            @blk.tensor
                            def _(t):
                                for tb in range(NTB):
                                    t.wait_ge(Sm["se_xn"].h, xn_b + tb + 1)

                                    def tgroup(j):
                                        t.wait_ge(Sm["dv_evac"].h, max(0, evac_b + 8 * tb + j - 1))
                                        for u in range(4):
                                            c = 4 * j + u
                                            ti = t.transpose(
                                                p_tr[:, j % 2, 128 * u:128 * (u + 1)],
                                                xn_tm[:, 128 * c:128 * (c + 1)], ident[:])
                                            if u == 3:
                                                Sm["te_tr"].inc(ti)

                                    def mmgroup(j):
                                        t.wait_ge(Sm["dv_evac"].h, evac_b + 8 * tb + j + 1)
                                        if j == 0:
                                            t.wait_ge(Sm["ld_w"].h, ld_w_t)
                                            t.wait_ge(Sm["se_kv"].h, max(0, sekv_b + 2 * tb))
                                        for u in range(4):
                                            c = 4 * j + u
                                            t.matmul(pK[:], xn_dm[:, 128 * c:128 * (c + 1)],
                                                     wk_sb[:, c, :], start=(c == 0), stop=(c == NCH - 1))
                                            mi = t.matmul(pV[:], xn_dm[:, 128 * c:128 * (c + 1)],
                                                          wv_sb[:, c, :], start=(c == 0), stop=(c == NCH - 1))
                                            if c == NCH - 1:
                                                Sm["te_kv"].inc(mi)

                                    tgroup(0)
                                    tgroup(1)
                                    if tb >= 1:
                                        kv_ktr(t, tb - 1)
                                    for j in range(8):
                                        mmgroup(j)
                                        if j + 2 < 8:
                                            tgroup(j + 2)
                                    if tb == NTB - 1:
                                        kv_ktr(t, tb)
                        else:
                            @blk.tensor
                            def _(t):
                                for tb in range(NTB):
                                    src_dm = xn_blob3[:, tb % 4, :]
                                    t.wait_ge(Sm["ld_blob"].h, ld_blob_b + 16 * (tb // 2 + 1))
                                    t.wait_ge(Sm["ld_blob2"].h, ld_blob2_b + 16 * ((tb + 1) // 2))
                                    if tb == 0:
                                        t.wait_ge(Sm["ld_w"].h, ld_w_t)
                                    t.wait_ge(Sm["se_kv"].h, max(0, sekv_b + 2 * tb))
                                    if tb >= 1:
                                        kv_ktr(t, tb - 1)
                                    for c in range(NCH):
                                        t.matmul(pK[:], src_dm[:, 128 * c:128 * (c + 1)],
                                                 wk_sb[:, c, :], start=(c == 0), stop=(c == NCH - 1))
                                        mi = t.matmul(pV[:], src_dm[:, 128 * c:128 * (c + 1)],
                                                      wv_sb[:, c, :], start=(c == 0), stop=(c == NCH - 1))
                                        if c == NCH - 1:
                                            Sm["te_kv"].inc(mi)
                                    if tb == NTB - 1:
                                        kv_ktr(t, tb)

                            @blk.scalar
                            def _(s):
                                for tb in range(NTB):
                                    kv_epi_scalar(s, tb)
                                    if tb % 2 == 0 and tb + 3 < NTB:
                                        Sm["ld_blob2"].inc(s.dma_start(
                                            out=xn_blob3[:, (tb + 3) % 4, :],
                                            in_=xn_dram[tb + 3]), 16)
                                kv_kst_evac(s, NTB - 1)

                            @blk.vector
                            def _(v):
                                for tb in range(NTB):
                                    kv_rotary_dv(v, tb)

                    if first:
                        with nc.Block() as blk:
                            @blk.scalar
                            def _(s):
                                Sm["st_blob"].wait(s)

            def emit_flash_phase(rnd):
                with ExitStack() as pctx:
                    pen = pctx.enter_context
                    wq_sb = pen(nc.sbuf_tensor(f"wq_sb{rnd}", [P, NCH, 512], BF16))
                    xn_blob = pen(nc.sbuf_tensor(f"xn_blob{rnd}", [P, 3, D], BF16))
                    q_tm = pen(nc.sbuf_tensor(f"q_tm{rnd}", [P, 512], F32))
                    q_bf = pen(nc.sbuf_tensor(f"q_bf{rnd}", [P, 512], BF16))
                    q_fm = pen(nc.sbuf_tensor(f"q_fm{rnd}", [P, 4, P], BF16))
                    q_rsc = pen(nc.sbuf_tensor(f"q_rsc{rnd}", [P, 128], F32))
                    S_row = pen(nc.sbuf_tensor(f"S_row{rnd}", [P, S], F32))
                    P_row2 = pen(nc.sbuf_tensor(f"P_row{rnd}", [P, 2, S], BF16))
                    Pt_sb = pen(nc.sbuf_tensor(f"Pt_sb{rnd}", [P, 2, 512], BF16))
                    dmask = pen(nc.sbuf_tensor(f"dmask{rnd}", [P, 4, 512], BF16))
                    mxp = pen(nc.sbuf_tensor(f"mxp{rnd}", [P, 8], F32))
                    lp = pen(nc.sbuf_tensor(f"lp{rnd}", [P, 8], F32))
                    negm2 = pen(nc.sbuf_tensor(f"negm{rnd}", [P, 2], F32))
                    lsum = pen(nc.sbuf_tensor(f"lsum{rnd}", [P, 1], F32))
                    rtmp7 = pen(nc.sbuf_tensor(f"rtmp7_{rnd}", [P, 1], F32))
                    linv = pen(nc.sbuf_tensor(f"linv{rnd}", [P, 1], F32))
                    astage = pen(nc.sbuf_tensor(f"astage{rnd}", [P, 2, 512], BF16))
                    pQ = pen(nc.psum_tensor(f"pQ{rnd}", [P, 512], F32))
                    pS0 = pen(nc.psum_tensor(f"pS0{rnd}", [P, 512], F32))
                    pS1 = pen(nc.psum_tensor(f"pS1{rnd}", [P, 512], F32))
                    pS2 = pen(nc.psum_tensor(f"pS2{rnd}", [P, 512], F32))
                    p_qtr = pen(nc.psum_tensor(f"p_qtr{rnd}", [P, 1024], BF16))
                    p_pt0 = pen(nc.psum_tensor(f"p_pt0{rnd}", [P, 1024], BF16))
                    p_pt1 = pen(nc.psum_tensor(f"p_pt1{rnd}", [P, 1024], BF16))
                    pOb = pen(nc.psum_tensor(f"pOb{rnd}", [P, 512], F32))
                    pS = [pS0, pS1, pS2]
                    p_pt = [p_pt0, p_pt1]
                    pO = [pOb[:, 0:256], pOb[:, 256:512]]

                    Bq = Sm["te_q"].v
                    Bseq = Sm["se_q"].v
                    Bdvq = Sm["dv_q"].v
                    Bqtr = Sm["te_qtr"].v
                    Bqfm = Sm["dv_qfm"].v
                    Bnegm = Sm["dv_negm"].v
                    Bexp = Sm["se_exp"].v
                    Bl = Sm["dv_l"].v
                    Bo = Sm["dv_o"].v
                    Battn = Sm["st_attn"].v
                    Bblob = Sm["ld_blob"].v
                    NKT = [qb // 4 + 1 for qb in range(NTB)]
                    TES = {}
                    cum = 0
                    for qb in range(NTB):
                        for h in (0, 1):
                            TES[(qb, h)] = cum
                            cum += NKT[qb]
                    TES_END = cum
                    Bs = Sm["te_s"].v
                    Bsev = Sm["dv_sevac"].v
                    Bpt = Sm["te_pt"].v
                    Bptv = Sm["dv_ptevac"].v
                    Bpv = Sm["te_pv"].v
                    linv_h2 = [linv[:], lsum[:]]

                    with nc.Block() as blk:
                        @blk.gpsimd
                        def _(g):
                            for a in range(4):
                                g.memset(dmask[:, a, :], 1.0)
                                ai = g.affine_select(out=dmask[:, a, :], in_=dmask[:, a, :],
                                                     compare_op=ALU.is_ge, fill=0.0, base=128 * a,
                                                     pattern=[[-1, 512]], channel_multiplier=1)
                                if a == 3:
                                    Sm["gp_fl"].inc(ai)
                            if rnd == 0:
                                Sm["ld_w"].inc(g.dma_start(
                                    out=wq_sb[:],
                                    in_=wq_d.rearrange("(c p) n -> p c n", p=P)[:, :, 0:512]), 16)
                                if upto >= 6:
                                    for c4 in range(4):
                                        Sm["cast_w"].inc(g.dma_start(
                                            out=wobf[:, 1024 * c4:1024 * (c4 + 1)],
                                            in_=wo_d[:, 1024 * c4:1024 * (c4 + 1)]), 16)
                            else:
                                g.wait_ge(Sm["cast_w"].h, 48)
                                Sm["ld_w"].inc(g.dma_start(
                                    out=wq_sb[:],
                                    in_=wqbf1.rearrange("(c p) n -> p c n", p=P)), 16)

                        ld_w_t = Sm["ld_w"].v
                        gp_t = Sm["gp_fl"].v

                        @blk.sync
                        def _(sync):
                            Sm["ld_blob"].inc(sync.dma_start(
                                out=xn_blob[:, 0, :], in_=xn_dram[0]), 16)
                            for qb in range(1, NTB):
                                sync.wait_ge(Sm["te_q"].h, max(0, Bq + qb - 2))
                                Sm["ld_blob"].inc(sync.dma_start(
                                    out=xn_blob[:, qb % 3, :],
                                    in_=xn_dram[qb]), 16)

                        @blk.tensor
                        def _(t):
                            def ptpv(qb, h):
                                nkt = NKT[qb]
                                t.wait_ge(Sm["dv_l"].h, Bl + 2 * qb + h + 1)
                                t.wait_ge(Sm["dv_o"].h, max(0, Bo + 2 * qb))
                                t.wait_ge(Sm["dv_ptevac"].h, Bptv + TES[(qb, h)])

                                def ptr_group(kt):
                                    if kt >= 2:
                                        t.wait_ge(Sm["dv_ptevac"].h, Bptv + TES[(qb, h)] + kt - 1)
                                    for u in range(4):
                                        ti = t.transpose(
                                            p_pt[kt % 2][:, 128 * u:128 * (u + 1)],
                                            P_row2[:, h, 512 * kt + 128 * u:512 * kt + 128 * (u + 1)],
                                            ident[:])
                                        if u == 3:
                                            Sm["te_pt"].inc(ti)

                                def pv_group(kt):
                                    t.wait_ge(Sm["dv_ptevac"].h, Bptv + TES[(qb, h)] + kt + 1)
                                    for u in range(4):
                                        mi = t.matmul(
                                            pO[h],
                                            Pt_sb[:, kt % 2, 128 * u:128 * (u + 1)],
                                            Vst[:, 4 * kt + u, 256 * h:256 * (h + 1)],
                                            start=(kt == 0 and u == 0),
                                            stop=(kt == nkt - 1 and u == 3))
                                    Sm["te_pv"].inc(mi)

                                ptr_group(0)
                                if nkt > 1:
                                    ptr_group(1)
                                for kt in range(nkt):
                                    pv_group(kt)
                                    if kt + 2 < nkt:
                                        ptr_group(kt + 2)

                            def q_mm(qb):
                                t.wait_ge(Sm["ld_blob"].h, Bblob + 16 * (qb + 1))
                                if qb == 0:
                                    t.wait_ge(Sm["ld_w"].h, ld_w_t)
                                t.wait_ge(Sm["se_q"].h, max(0, Bseq + qb))
                                xb = xn_blob[:, qb % 3, :]
                                for c in range(NCH):
                                    mi = t.matmul(pQ[:], xb[:, 128 * c:128 * (c + 1)],
                                                  wq_sb[:, c, :], start=(c == 0), stop=(c == NCH - 1))
                                Sm["te_q"].inc(mi)

                            q_mm(0)
                            for qb in range(NTB):
                                nkt = NKT[qb]
                                t.wait_ge(Sm["dv_q"].h, Bdvq + qb + 1)
                                t.wait_ge(Sm["dv_qfm"].h, max(0, Bqfm + qb))
                                for u in range(4):
                                    ti = t.transpose(p_qtr[:, 128 * u:128 * (u + 1)],
                                                     q_bf[:, 128 * u:128 * (u + 1)], ident[:])
                                    if u == 3:
                                        Sm["te_qtr"].inc(ti)
                                t.wait_ge(Sm["dv_qfm"].h, Bqfm + qb + 1)
                                for h in (0, 1):
                                    t.wait_ge(Sm["dv_sevac"].h, Bsev + TES[(qb, h)])
                                    for kt in range(nkt):
                                        if kt >= 3:
                                            t.wait_ge(Sm["dv_sevac"].h, Bsev + TES[(qb, h)] + kt - 2)
                                        bank = pS[kt % 3]
                                        t.matmul(bank[:], q_fm[:, 2 * h, :],
                                                 Kst[:, 2 * h, 512 * kt:512 * (kt + 1)],
                                                 start=True, stop=False)
                                        mi = t.matmul(bank[:], q_fm[:, 2 * h + 1, :],
                                                      Kst[:, 2 * h + 1, 512 * kt:512 * (kt + 1)],
                                                      start=False, stop=True)
                                        Sm["te_s"].inc(mi)
                                if qb + 1 < NTB:
                                    q_mm(qb + 1)
                                ptpv(qb, 0)
                                ptpv(qb, 1)

                        @blk.scalar
                        def _(s):
                            def attn_store(qb):
                                s.wait_ge(Sm["dv_o"].h, Bo + 2 * (qb + 1))
                                Sm["st_attn"].inc(s.dma_start(
                                    out=attn_myr[rnd][P * qb:P * (qb + 1), :],
                                    in_=astage[:, qb % 2, :]), 16)

                            for qb in range(NTB):
                                nkt = NKT[qb]
                                s.wait_ge(Sm["te_q"].h, Bq + qb + 1)
                                s.wait_ge(Sm["dv_q"].h, max(0, Bdvq + qb))
                                Sm["se_q"].inc(s.activation(
                                    out=q_tm[:], in_=pQ[:], func=AF.Identity, scale=c_inv16[:]))
                                if qb >= 1:
                                    attn_store(qb - 1)
                                for h in (0, 1):
                                    s.wait_ge(Sm["dv_negm"].h, Bnegm + 2 * qb + h + 1)
                                    s.wait_ge(Sm["dv_l"].h, max(0, Bl + 2 * qb + h))
                                    if qb >= 1:
                                        prev_done = TES[(qb - 1, h)] + NKT[qb - 1]
                                        s.wait_ge(Sm["te_pt"].h, Bpt + prev_done)
                                    for kt in range(nkt):
                                        ai = s.activation(
                                            out=P_row2[:, h, 512 * kt:512 * (kt + 1)],
                                            in_=S_row[:, 512 * kt:512 * (kt + 1)],
                                            func=AF.Exp, bias=negm2[:, h:h + 1],
                                            accum_out=lp[:, kt:kt + 1])
                                    Sm["se_exp"].inc(ai)
                                for h in (0, 1):
                                    s.wait_ge(Sm["te_pv"].h, Bpv + TES[(qb, 1)] + nkt)
                                    s.wait_ge(Sm["dv_l"].h, Bl + 2 * qb + h + 1)
                                    if h == 0:
                                        s.wait_ge(Sm["st_attn"].h, max(0, Battn + 16 * (qb - 1)))
                                    Sm["dv_o"].inc(s.activation(
                                        out=astage[:, qb % 2, 256 * h:256 * (h + 1)],
                                        in_=pO[h], func=AF.Identity,
                                        scale=linv_h2[h]))
                                if qb == NTB - 1:
                                    attn_store(qb)

                        @blk.vector
                        def _(v):
                            for qb in range(NTB):
                                nkt = NKT[qb]
                                if qb == 0 and rnd == 0:
                                    Sm["ld_aux"].wait(v)
                                if qb == 0:
                                    v.wait_ge(Sm["gp_fl"].h, gp_t)
                                v.wait_ge(Sm["se_q"].h, Bseq + qb + 1)
                                cos = rope_sb[:, qb, 0:64]
                                sin = rope_sb[:, qb, 64:128]
                                for h in (0, 1):
                                    qr = q_tm[:, 256 * h:256 * h + 64]
                                    t1 = q_rsc[:, 0:64]
                                    t2 = q_rsc[:, 64:128]
                                    v.tensor_mul(t1, qr, cos)
                                    Sm["dv_fl"].inc(v.tensor_mul(t2, qr, sin))
                                    Sm["dv_fl"].wait(v)
                                    ev = lambda a: a.rearrange("p (i two) -> p i two", two=2)[:, :, 0]
                                    od = lambda a: a.rearrange("p (i two) -> p i two", two=2)[:, :, 1]
                                    v.tensor_sub(ev(qr), ev(t1), od(t2))
                                    Sm["dv_fl"].inc(v.tensor_add(od(qr), od(t1), ev(t2)))
                                    Sm["dv_fl"].wait(v)
                                v.wait_ge(Sm["te_qtr"].h, max(0, Bqtr + qb))
                                Sm["dv_q"].inc(v.tensor_copy(q_bf[:], q_tm[:]))
                                v.wait_ge(Sm["te_qtr"].h, Bqtr + qb + 1)
                                v.wait_ge(Sm["te_s"].h, Bs + TES[(qb, 0)])
                                Sm["dv_qfm"].inc(v.tensor_copy(
                                    q_fm.rearrange("p u q -> p (u q)"),
                                    p_qtr[:, 0:512]))
                                for h in (0, 1):
                                    v.wait_ge(Sm["se_exp"].h, max(0, Bexp + 2 * qb + h))
                                    for kt in range(nkt):
                                        v.wait_ge(Sm["te_s"].h, Bs + TES[(qb, h)] + kt + 1)
                                        Sm["dv_sevac"].inc(v.tensor_scalar(
                                            out=S_row[:, 512 * kt:512 * (kt + 1)],
                                            in0=pS[kt % 3][:], scalar1=0.0, scalar2=None,
                                            op0=ALU.add, op1=ALU.max,
                                            accum_out=mxp[:, kt:kt + 1]))
                                    Sm["dv_sevac"].wait(v)
                                    Sm["dv_negm"].inc(v.reduce_max(
                                        negm2[:, h:h + 1], mxp[:, 0:nkt], axis=AX.X, negate=True))
                                    v.wait_ge(Sm["se_exp"].h, Bexp + 2 * qb + h + 1)
                                    dtile = P_row2[:, h, 512 * (nkt - 1):512 * nkt]
                                    Sm["dv_fl"].inc(v.scalar_tensor_tensor(
                                        out=dtile, in0=dtile, scalar=1.0,
                                        in1=dmask[:, qb % 4, :], op0=ALU.mult, op1=ALU.mult,
                                        accum_out=lp[:, nkt - 1:nkt]))
                                    Sm["dv_fl"].wait(v)
                                    Sm["dv_fl"].inc(v.reduce_sum(rtmp7[:, 0:1], lp[:, 0:nkt], axis=AX.X))
                                    Sm["dv_fl"].wait(v)
                                    v.wait_ge(Sm["dv_o"].h, max(0, Bo + 2 * qb + h - 1))
                                    Sm["dv_l"].inc(v.reciprocal(linv_h2[h], rtmp7[:, 0:1]))
                                for h in (0, 1):
                                    v.wait_ge(Sm["te_pv"].h, Bpv + TES[(qb, h)])
                                    for kt in range(NKT[qb]):
                                        v.wait_ge(Sm["te_pt"].h, Bpt + TES[(qb, h)] + kt + 1)
                                        if kt >= 2:
                                            v.wait_ge(Sm["te_pv"].h, Bpv + TES[(qb, h)] + kt - 1)
                                        Sm["dv_ptevac"].inc(v.tensor_copy(
                                            Pt_sb[:, kt % 2, :],
                                            p_pt[kt % 2][:, 0:512]))

                    with nc.Block() as blk:
                        @blk.scalar
                        def _(s):
                            Sm["st_attn"].wait(s)

            def emit_gather(rnd):
                with nc.Block() as blk:
                    @blk.gpsimd
                    def _(g):
                        Sm["st_attn"].wait(g)
                        Sm["cc"].inc(g.collective_compute(
                            "AllGather", ALU.bypass,
                            replica_groups=[[0, 1, 2, 3], [4, 5, 6, 7]],
                            ins=[attn_myr[rnd]],
                            outs=[attn_gr[rnd]]), 1)

            emit_kv_phase(0, True)
            if "xn" in taps:
                _emit_dram_tap(nc, Sm, tap_outs["xn"],
                               xn_dram.rearrange("tb p d -> (tb p) d"))
            if "kst" in taps:
                _emit_sbuf_tap(nc, Sm, tap_outs["kst"], Kst.rearrange("p u s -> p (u s)"))
            if "vst" in taps:
                _emit_sbuf_tap(nc, Sm, tap_outs["vst"], Vst.rearrange("p tb n -> p (tb n)"))
            if upto >= 2:
                emit_flash_phase(0)
            if upto >= 5:
                emit_gather(0)
            if upto >= 3:
                emit_kv_phase(1, False)
            if upto >= 4:
                emit_flash_phase(1)
            if upto >= 5:
                emit_gather(1)

        if upto >= 6:
            with ExitStack() as pctx:
                pen = pctx.enter_context
                attnT = pen(nc.sbuf_tensor("attnT", [P, NTBL, D], BF16))
                wo_sb = pen(nc.sbuf_tensor("wo_sb", [P, 3, 16, 512], BF16))
                xr_sb = pen(nc.sbuf_tensor("xr_sb", [P, NTBL, 512], F32))
                x2st = pen(nc.sbuf_tensor("x2st", [P, 2, 512], F32))
                wo_b = Sm["ld_wo"].v
                xr_b = Sm["ld_x2r"].v
                x2_b = Sm["st_x2"].v

                def wo_src(step):
                    o, ph = step // 2, step % 2
                    return wobf.rearrange("(c p) n -> p c n", p=P)[
                        :, 16 * ph:16 * (ph + 1), 512 * o:512 * (o + 1)]

                with ExitStack() as sctx:
                    sen = sctx.enter_context
                    attn_tm = sen(nc.sbuf_tensor("attn_tm", [P, NTBL, D], BF16))
                    p_at = sen(nc.psum_tensor("p_at", [P, 2, 1024], BF16))
                    at_b = Sm["ld_at"].v
                    at2_b = Sm["ld_at2"].v
                    JL = [0, 2, 4, 6, 1, 3, 5, 7]
                    with nc.Block() as blk:
                        Tat = Sm["te_at"].v
                        Dat = Sm["dv_at"].v

                        @blk.gpsimd
                        def _(g):
                            g.wait_ge(Sm["cc"].h, Sm["cc"].v - 1)
                            gm = g.partition_id() % 4
                            big = attn_gr[0].rearrange("(r s) n -> s r n", r=4)
                            for tb_ in range(NTBL):
                                Sm["ld_at"].inc(g.dma_start(
                                    out=attn_tm[:, tb_, :].rearrange("p (r n) -> p r n", r=4)[:, :, 0:512],
                                    in_=big[bass.ds(gm * TSL + P * tb_, P), :, :]), 16)

                        @blk.scalar
                        def _(sc):
                            Sm["cc"].wait(sc)
                            gm2 = sc.partition_id() % 4
                            big2 = attn_gr[1].rearrange("(r s) n -> s r n", r=4)
                            for tb_ in range(NTBL):
                                Sm["ld_at2"].inc(sc.dma_start(
                                    out=attn_tm[:, tb_, :].rearrange("p (r n) -> p r n", r=4)[:, :, 512:1024],
                                    in_=big2[bass.ds(gm2 * TSL + P * tb_, P), :, :]), 16)

                        @blk.sync
                        def _(sync):
                            sync.wait_ge(Sm["cast_w"].h, 112)
                            for st_ in range(3):
                                Sm["ld_wo"].inc(sync.dma_start(
                                    out=wo_sb[:, st_, :, :], in_=wo_src(st_)), 16)
                            Sm["ld_x2r"].inc(sync.dma_start(
                                out=xr_sb[:],
                                in_=xres_d.rearrange("(tb p) n -> p tb n", p=P)[:, :, 0:512]), 16)

                        @blk.tensor
                        def _(t):
                            for tblk in range(NTBL):
                                for idx, j in enumerate(JL):
                                    if idx == 0:
                                        t.wait_ge(Sm["ld_at"].h, at_b + 16 * (tblk + 1))
                                    if idx == 4:
                                        t.wait_ge(Sm["ld_at2"].h, at2_b + 16 * (tblk + 1))
                                    t.wait_ge(Sm["dv_at"].h, max(0, Dat + 8 * tblk + idx - 1))
                                    for u in range(4):
                                        c = 4 * j + u
                                        ti = t.transpose(
                                            p_at[:, idx % 2, 128 * u:128 * (u + 1)],
                                            attn_tm[:, tblk, 128 * c:128 * (c + 1)], ident[:])
                                        if u == 3:
                                            Sm["te_at"].inc(ti)

                        @blk.vector
                        def _(v):
                            for tblk in range(NTBL):
                                for idx, j in enumerate(JL):
                                    v.wait_ge(Sm["te_at"].h, Tat + 8 * tblk + idx + 1)
                                    Sm["dv_at"].inc(v.tensor_copy(
                                        attnT[:, tblk, 512 * j:512 * (j + 1)],
                                        p_at[:, idx % 2, 0:512]))

                with ExitStack() as sctx:
                    sen = sctx.enter_context
                    pF6 = [sen(nc.psum_tensor(f"pF6_{i}", [P, 512], F32)) for i in range(NTBL)]
                    with nc.Block() as blk:
                        To = Sm["te_o"].v
                        Toh = Sm["te_oh"].v
                        Dx2 = Sm["dv_x2"].v

                        @blk.tensor
                        def _(t):
                            for step in range(16):
                                o, ph = step // 2, step % 2
                                t.wait_ge(Sm["ld_wo"].h, wo_b + 16 * (step + 1))
                                for cl in range(16):
                                    c = 16 * ph + cl
                                    for tblk in range(NTBL):
                                        if c == 0:
                                            t.wait_ge(Sm["dv_x2"].h,
                                                      max(0, Dx2 + 8 * (o - 1) + tblk + 1))
                                        mi = t.matmul(
                                            pF6[tblk][:],
                                            attnT[:, tblk, 128 * c:128 * (c + 1)],
                                            wo_sb[:, step % 3, cl, :],
                                            start=(c == 0), stop=(c == NCH - 1))
                                        if c == NCH - 1 and tblk < NTBL - 1:
                                            Sm["te_o"].inc(mi)
                                        if cl == 15 and tblk == NTBL - 1:
                                            Sm["te_oh"].inc(mi)

                        @blk.vector
                        def _(v):
                            for o in range(8):
                                v.wait_ge(Sm["ld_x2r"].h, xr_b + 16 * (o + 1))
                                for tblk in range(NTBL):
                                    if tblk < NTBL - 1:
                                        v.wait_ge(Sm["te_o"].h, To + 7 * o + tblk + 1)
                                    else:
                                        v.wait_ge(Sm["te_oh"].h, Toh + 2 * o + 2)
                                    v.wait_ge(Sm["st_x2"].h, max(0, x2_b + 16 * (8 * o + tblk - 1)))
                                    Sm["dv_x2"].inc(v.tensor_add(
                                        x2st[:, tblk % 2, :], pF6[tblk][:],
                                        xr_sb[:, tblk, :]))

                        @blk.scalar
                        def _(sc):
                            for o in range(8):
                                for tblk in range(NTBL):
                                    sc.wait_ge(Sm["dv_x2"].h, Dx2 + 8 * o + tblk + 1)
                                    Sm["st_x2"].inc(sc.dma_start(
                                        out=x2_dram[:, 512 * o:512 * (o + 1)]
                                        .rearrange("(tb p) n -> p tb n", p=P)[:, tblk, :],
                                        in_=x2st[:, tblk % 2, :]), 16)

                        @blk.sync
                        def _(sync):
                            for st_ in range(3, 16):
                                sync.wait_ge(Sm["te_oh"].h, Toh + st_ - 2)
                                Sm["ld_wo"].inc(sync.dma_start(
                                    out=wo_sb[:, st_ % 3, :, :], in_=wo_src(st_)), 16)
                                if st_ % 2 == 1:
                                    o_next = (st_ - 1) // 2
                                    sync.wait_ge(Sm["dv_x2"].h, Dx2 + 8 * o_next)
                                    Sm["ld_x2r"].inc(sync.dma_start(
                                        out=xr_sb[:],
                                        in_=xres_d.rearrange("(tb p) n -> p tb n", p=P)[:, :, 512 * o_next:512 * (o_next + 1)]), 16)

                    with nc.Block() as blk:
                        @blk.gpsimd
                        def _(g):
                            Sm["st_x2"].wait(g)
            if "x2" in taps:
                _emit_dram_tap(nc, Sm, tap_outs["x2"], x2_dram)

        if upto >= 7:
            with ExitStack() as pctx:
                pen = pctx.enter_context
                bufA = pen(nc.sbuf_tensor("bufA", [P, NTBL, D], BF16))
                bufB = pen(nc.sbuf_tensor("bufB", [P, NTBL, D], BF16))
                rs_prev = pen(nc.sbuf_tensor("rs_prev", [P, NTBL], F32))
                rs_next = pen(nc.sbuf_tensor("rs_next", [P, NTBL], F32))
                bL = pen(nc.sbuf_tensor("bL", [P, NTBL], F32))
                hp = pen(nc.sbuf_tensor("hp", [P, NTBL, 16], F32))
                rtmp = pen(nc.sbuf_tensor("rtmp", [P, 2], F32))

                with ExitStack() as sctx:
                    sen = sctx.enter_context
                    x2row = sen(nc.sbuf_tensor("x2row", [P, 2, D], F32))
                    x2n_tm = sen(nc.sbuf_tensor("x2n_tm", [P, D], BF16))
                    p_tr7 = sen(nc.psum_tensor("p_tr7", [P, 2, 1024], BF16))
                    x2r_b = Sm["ld_x2r"].v
                    var_b = Sm["dv_var"].v
                    rstd_b = Sm["se_rstd"].v
                    xn_b = Sm["se_xn"].v
                    tr_b = Sm["te_tr"].v
                    evac_b = Sm["dv_evac"].v
                    sq_b = Sm["se_sq"].v

                    with nc.Block() as blk:
                        @blk.sync
                        def _(sync):
                            Sm["st_x2"].wait(sync)
                            Sm["ld_x2r"].inc(sync.dma_start(
                                out=x2row[:, 0, :], in_=x2_dram[0:P, :]), 16)

                    with nc.Block() as blk:
                        @blk.sync
                        def _(sync):
                            for tblk in range(1, NTBL):
                                sync.wait_ge(Sm["se_xn"].h, max(0, xn_b + tblk - 1))
                                Sm["ld_x2r"].inc(sync.dma_start(
                                    out=x2row[:, tblk % 2, :],
                                    in_=x2_dram[P * tblk:P * (tblk + 1), :]), 16)

                        def ln2_chain(v, tblk):
                            xt = x2row[:, tblk % 2, :]
                            v.wait_ge(Sm["ld_x2r"].h, x2r_b + 16 * (tblk + 1))
                            if tblk >= 1:
                                v.wait_ge(Sm["te_tr"].h, tr_b + 8 * tblk)
                                v.wait_ge(Sm["se_sq"].h, sq_b + tblk)
                                v.wait_ge(Sm["se_rstd"].h, rstd_b + tblk)
                            Sm["dv_fl"].inc(v.reduce_sum(st_sum[:], xt, axis=AX.X))
                            Sm["dv_fl"].inc(v.scalar_tensor_tensor(
                                out=x2n_tm[:], in0=xt, scalar=1.0, in1=xt,
                                op0=ALU.mult, op1=ALU.mult, accum_out=st_ssq[:]))
                            Sm["dv_fl"].wait(v)
                            Sm["dv_fl"].inc(v.tensor_scalar(
                                out=st_mean[:], in0=st_sum[:],
                                scalar1=-1.0 / D, scalar2=None, op0=ALU.mult))
                            Sm["dv_fl"].inc(v.scalar_tensor_tensor(
                                out=st_m2[:], in0=st_sum[:], scalar=1.0 / (D * D),
                                in1=st_sum[:], op0=ALU.mult, op1=ALU.mult))
                            Sm["dv_fl"].wait(v)
                            Sm["dv_var"].inc(v.scalar_tensor_tensor(
                                out=st_var[:], in0=st_ssq[:], scalar=1.0 / D,
                                in1=st_m2[:], op0=ALU.mult, op1=ALU.subtract))
                            v.wait_ge(Sm["se_rstd"].h, rstd_b + tblk + 1)
                            Sm["se_xn"].inc(v.tensor_scalar(
                                out=x2n_tm[:], in0=xt, scalar1=st_mean[:],
                                scalar2=st_rstd[:], op0=ALU.add, op1=ALU.mult))

                        @blk.vector
                        def _(v):
                            ln2_chain(v, 0)
                            for tblk in range(NTBL):
                                for j in range(8):
                                    v.wait_ge(Sm["te_tr"].h, tr_b + 8 * tblk + j + 1)
                                    Sm["dv_evac"].inc(v.tensor_copy(
                                        bufA[:, tblk, 512 * j:512 * (j + 1)],
                                        p_tr7[:, j % 2, 0:512]))
                                if tblk + 1 < NTBL:
                                    ln2_chain(v, tblk + 1)

                        @blk.scalar
                        def _(s):
                            for tblk in range(NTBL):
                                s.wait_ge(Sm["dv_var"].h, var_b + tblk + 1)
                                if tblk >= 1:
                                    s.wait_ge(Sm["se_xn"].h, xn_b + tblk)
                                Sm["se_fl"].inc(s.activation(
                                    out=st_rstd[:], in_=st_var[:], func=AF.Ln, bias=c_eps[:]))
                                Sm["se_fl"].wait(s)
                                Sm["se_rstd"].inc(s.activation(
                                    out=st_rstd[:], in_=st_rstd[:], func=AF.Exp, scale=c_mhalf[:]))
                                s.wait_ge(Sm["se_xn"].h, xn_b + tblk + 1)
                                Sm["se_sq"].inc(s.activation(
                                    out=x2n_tm[:], in_=x2n_tm[:], func=AF.Identity,
                                    accum_out=rs_prev[:, tblk:tblk + 1]))

                        @blk.tensor
                        def _(t):
                            for tblk in range(NTBL):
                                t.wait_ge(Sm["se_xn"].h, xn_b + tblk + 1)
                                for j in range(8):
                                    t.wait_ge(Sm["dv_evac"].h, max(0, evac_b + 8 * tblk + j - 1))
                                    for u in range(4):
                                        c = 4 * j + u
                                        ti = t.transpose(
                                            p_tr7[:, j % 2, 128 * u:128 * (u + 1)],
                                            x2n_tm[:, 128 * c:128 * (c + 1)], ident[:])
                                        if u == 3:
                                            Sm["te_tr"].inc(ti)

                with ExitStack() as sctx:
                    sen = sctx.enter_context
                    wm_sb = sen(nc.sbuf_tensor("wm_sb", [P, 2, 16, 512], BF16))
                    x2c = sen(nc.sbuf_tensor("x2c", [P, NTBL, 512], F32))
                    ostage = sen(nc.sbuf_tensor("ostage", [P, 2, 512], F32))
                    otmp = sen(nc.sbuf_tensor("otmp", [P, 512], F32))

                    def mask_src(L, step):
                        o, ph = step // 2, step % 2
                        return km_d[L].rearrange("(c p) n -> p c n", p=P)[
                            :, 16 * ph:16 * (ph + 1), 512 * o:512 * (o + 1)]

                    def emit_layer(L, src_buf, dst_buf, pFF):
                        msk_b = Sm["ld_msk"].v
                        xc_b = Sm["ld_xc"].v
                        out_b = Sm["st_out"].v
                        with nc.Block() as blk:
                            Tff = Sm["te_ff"].v
                            Tfh = Sm["te_fh"].v
                            Wfx = Sm["dv_wfix"].v
                            Sff = Sm["se_ff"].v
                            Dff = Sm["dv_ff"].v

                            @blk.gpsimd
                            def _(g):
                                for st_ in range(16):
                                    if st_ >= 2:
                                        g.wait_ge(Sm["te_fh"].h, Tfh + st_ - 1)
                                    Sm["ld_msk"].inc(g.dma_start(
                                        out=wm_sb[:, st_ % 2, :, :], in_=mask_src(L, st_)), 16)

                            @blk.tensor
                            def _(t):
                                for step in range(16):
                                    o, ph = step // 2, step % 2
                                    t.wait_ge(Sm["ld_msk"].h, msk_b + 16 * (step + 1))
                                    if L >= 1 and ph == 1:
                                        t.wait_ge(Sm["dv_wfix"].h, Wfx + (step + 1) // 2)
                                    for cl in range(16):
                                        c = 16 * ph + cl
                                        for tblk in range(NTBL):
                                            if c == 0:
                                                if L < 2:
                                                    t.wait_ge(Sm["se_ff"].h,
                                                              max(0, Sff + 8 * (o - 1) + tblk + 1))
                                                else:
                                                    t.wait_ge(Sm["dv_ff"].h,
                                                              max(0, Dff + 8 * (o - 1) + tblk + 1))
                                            mi = t.matmul(
                                                pFF[tblk][:],
                                                src_buf[:, tblk, 128 * c:128 * (c + 1)],
                                                wm_sb[:, step % 2, cl, :],
                                                start=(c == 0), stop=(c == NCH - 1))
                                            if c == NCH - 1 and tblk < NTBL - 1:
                                                Sm["te_ff"].inc(mi)
                                            if cl == 15 and tblk == NTBL - 1:
                                                Sm["te_fh"].inc(mi)

                            if L >= 1:
                                @blk.vector
                                def _(v):
                                    if L == 2:
                                        Sm["dv_rs"].wait(v)
                                        for tblk in range(NTBL):
                                            v.tensor_mul(bL[:, tblk:tblk + 1],
                                                         rs_next[:, tblk:tblk + 1],
                                                         sderiv[:, 5:6])
                                    for o in range(8):
                                        step = 2 * o + 1
                                        v.wait_ge(Sm["ld_msk"].h, msk_b + 16 * (step + 1))
                                        buf = step % 2
                                        v.tensor_scalar(
                                            out=wm_sb[:, buf, 9, :], in0=wm_sb[:, buf, 9, :],
                                            scalar1=rs_mix[:], scalar2=None, op0=ALU.mult)
                                        Sm["dv_wfix"].inc(v.tensor_scalar(
                                            out=wm_sb[:, buf, 10:16, :].rearrange("p c n -> p (c n)"),
                                            in0=wm_sb[:, buf, 10:16, :].rearrange("p c n -> p (c n)"),
                                            scalar1=-4.0, scalar2=None, op0=ALU.mult))
                                        if L == 2:
                                            v.wait_ge(Sm["ld_xc"].h, xc_b + 16 * (o + 1))
                                            for tblk in range(NTBL):
                                                if tblk < NTBL - 1:
                                                    v.wait_ge(Sm["te_ff"].h, Tff + 7 * o + tblk + 1)
                                                else:
                                                    v.wait_ge(Sm["te_fh"].h, Tfh + 2 * o + 2)
                                                v.wait_ge(Sm["st_out"].h, max(0, out_b + 16 * (8 * o + tblk - 1)))
                                                Sm["dv_fl"].inc(v.tensor_scalar(
                                                    out=otmp[:], in0=x2c[:, tblk, :],
                                                    scalar1=bL[:, tblk:tblk + 1],
                                                    scalar2=None, op0=ALU.add))
                                                Sm["dv_fl"].wait(v)
                                                Sm["dv_ff"].inc(v.scalar_tensor_tensor(
                                                    out=ostage[:, tblk % 2, :],
                                                    in0=pFF[tblk][:], scalar=sderiv[:, 2:3],
                                                    in1=otmp[:], op0=ALU.mult, op1=ALU.add))

                            if L < 2:
                                @blk.scalar
                                def _(s):
                                    if L == 1:
                                        Sm["dv_rs"].wait(s)
                                        s.copy(rs_prev[:], rs_next[:])
                                    for tblk in range(NTBL):
                                        s.activation(out=bL[:, tblk:tblk + 1],
                                                     in_=rs_prev[:, tblk:tblk + 1],
                                                     func=AF.Identity,
                                                     scale=sderiv[:, 3 + L:4 + L])
                                    for o in range(8):
                                        for tblk in range(NTBL):
                                            if tblk < NTBL - 1:
                                                s.wait_ge(Sm["te_ff"].h, Tff + 7 * o + tblk + 1)
                                            else:
                                                s.wait_ge(Sm["te_fh"].h, Tfh + 2 * o + 2)
                                            scale_ap = sderiv[:, L:L + 1]
                                            bias_ap = bL[:, tblk:tblk + 1]
                                            if o == 6:
                                                si = s.activation(
                                                    out=dst_buf[:, tblk, 3072:NE],
                                                    in_=pFF[tblk][:, 0:NE - 3072],
                                                    func=AF.Relu, scale=scale_ap, bias=bias_ap,
                                                    accum_out=hp[:, tblk, 6:7])
                                                si = s.activation(
                                                    out=dst_buf[:, tblk, NE:3584],
                                                    in_=pFF[tblk][:, NE - 3072:512],
                                                    func=AF.Relu, scale=scale_ap, bias=bias_ap,
                                                    accum_out=hp[:, tblk, 7:8])
                                            else:
                                                col = o if o < 6 else 8
                                                si = s.activation(
                                                    out=dst_buf[:, tblk, 512 * o:512 * (o + 1)],
                                                    in_=pFF[tblk][:],
                                                    func=AF.Relu, scale=scale_ap, bias=bias_ap,
                                                    accum_out=hp[:, tblk, col:col + 1])
                                            Sm["se_ff"].inc(si)
                            else:
                                @blk.scalar
                                def _(sc):
                                    for o in range(8):
                                        for tblk in range(NTBL):
                                            sc.wait_ge(Sm["dv_ff"].h, Dff + 8 * o + tblk + 1)
                                            Sm["st_out"].inc(sc.dma_start(
                                                out=out_d[:, 512 * o:512 * (o + 1)]
                                                .rearrange("(tb p) n -> p tb n", p=P)[:, tblk, :],
                                                in_=ostage[:, tblk % 2, :]), 16)

                                @blk.sync
                                def _(sync):
                                    Sm["ld_xc"].inc(sync.dma_start(
                                        out=x2c[:],
                                        in_=x2_dram.rearrange("(tb p) n -> p tb n", p=P)[:, :, 0:512]), 16)
                                    for o in range(7):
                                        sync.wait_ge(Sm["dv_ff"].h, Dff + 8 * (o + 1))
                                        Sm["ld_xc"].inc(sync.dma_start(
                                            out=x2c[:],
                                            in_=x2_dram.rearrange("(tb p) n -> p tb n", p=P)[:, :, 512 * (o + 1):512 * (o + 2)]), 16)

                        if L < 2:
                            with nc.Block() as blk:
                                @blk.vector
                                def _(v):
                                    Sm["se_ff"].wait(v)
                                    for tblk in range(NTBL):
                                        v.reduce_sum(rtmp[:, 0:1], hp[:, tblk, 0:7], axis=AX.X)
                                        Sm["dv_fl"].inc(v.reduce_sum(rtmp[:, 1:2], hp[:, tblk, 7:9], axis=AX.X), 2)
                                        Sm["dv_fl"].wait(v)
                                        Sm["dv_rs"].inc(v.scalar_tensor_tensor(
                                            out=rs_next[:, tblk:tblk + 1], in0=rtmp[:, 1:2],
                                            scalar=-4.0, in1=rtmp[:, 0:1],
                                            op0=ALU.mult, op1=ALU.add))

                    _trn = [0]

                    def emit_transpose(src_buf, dst_buf):
                        _trn[0] += 1
                        with ExitStack() as tctx:
                            p_trx = tctx.enter_context(nc.psum_tensor(f"p_trx{_trn[0]}", [P, 2, 1024], BF16))
                            with nc.Block() as blk:
                                tetr_b = Sm["te_tr"].v
                                dvev_b = Sm["dv_evac"].v

                                @blk.tensor
                                def _(t):
                                    for tblk in range(NTBL):
                                        for j in range(8):
                                            t.wait_ge(Sm["dv_evac"].h,
                                                      max(0, dvev_b + 8 * tblk + j - 1))
                                            for u in range(4):
                                                c = 4 * j + u
                                                ti = t.transpose(
                                                    p_trx[:, j % 2, 128 * u:128 * (u + 1)],
                                                    src_buf[:, tblk, 128 * c:128 * (c + 1)], ident[:])
                                                if u == 3:
                                                    Sm["te_tr"].inc(ti)

                                @blk.vector
                                def _(v):
                                    for tblk in range(NTBL):
                                        for j in range(8):
                                            v.wait_ge(Sm["te_tr"].h, tetr_b + 8 * tblk + j + 1)
                                            Sm["dv_evac"].inc(v.tensor_copy(
                                                dst_buf[:, tblk, 512 * j:512 * (j + 1)],
                                                p_trx[:, j % 2, 0:512]))

                    def layer_scoped(L, src_buf, dst_buf):
                        with ExitStack() as lctx:
                            pFF = [lctx.enter_context(
                                nc.psum_tensor(f"pFF{L}_{i}", [P, 512], F32))
                                for i in range(NTBL)]
                            emit_layer(L, src_buf, dst_buf, pFF)

                    if "x2nT" in taps:
                        _emit_sbuf_tap(nc, Sm, tap_outs["x2nT"], bufA.rearrange("p tb d -> p (tb d)"))
                    layer_scoped(0, bufA, bufB)
                    emit_transpose(bufB, bufA)
                    if "h1T" in taps:
                        _emit_sbuf_tap(nc, Sm, tap_outs["h1T"], bufA.rearrange("p tb d -> p (tb d)"))
                    layer_scoped(1, bufA, bufB)
                    emit_transpose(bufB, bufA)
                    if "h2T" in taps:
                        _emit_sbuf_tap(nc, Sm, tap_outs["h2T"], bufA.rearrange("p tb d -> p (tb d)"))
                    layer_scoped(2, bufA, None)

                    with nc.Block() as blk:
                        @blk.gpsimd
                        def _(g):
                            Sm["st_out"].wait(g)

    return nc


def _emit_dram_tap(nc, Sm, dst_f32, src_ap):
    rows = src_ap.shape[0]
    step = P
    _TAPN[0] += 1
    with nc.sbuf_tensor(f"tapbuf_{_TAPN[0]}", [P, src_ap.shape[1]], F32) as tb_:
        for r0 in range(0, rows, step):
            with nc.Block() as blk:
                @blk.gpsimd
                def _(g):
                    Sm["st_tap"].inc(g.dma_start(out=tb_[0:min(step, rows - r0), :],
                                                 in_=src_ap[r0:r0 + min(step, rows - r0), :]), 16)
                    Sm["st_tap"].wait(g)
                    Sm["st_tap"].inc(g.dma_start(out=dst_f32[r0:r0 + min(step, rows - r0), :],
                                                 in_=tb_[0:min(step, rows - r0), :]), 16)
                    Sm["st_tap"].wait(g)


_TAPN = [0]


def _emit_sbuf_tap(nc, Sm, dst_f32, src_sbuf_ap):
    cols = src_sbuf_ap.shape[1]
    CH = 4096
    _TAPN[0] += 1
    with nc.sbuf_tensor(f"tapbuf2_{_TAPN[0]}", [P, CH], F32) as tb_:
        for c0 in range(0, cols, CH):
            w = min(CH, cols - c0)
            with nc.Block() as blk:
                @blk.vector
                def _(v):
                    v.tensor_copy(tb_[:, 0:w], src_sbuf_ap[:, c0:c0 + w])
            with nc.Block() as blk:
                @blk.gpsimd
                def _(g):
                    Sm["st_tap"].inc(g.dma_start(out=dst_f32[:, c0:c0 + w], in_=tb_[:, 0:w]), 16)
                    Sm["st_tap"].wait(g)


def _host_prep(inputs):
    x = np.asarray(inputs["L0_attn_hidden_states"], np.float32)
    wq = np.asarray(inputs["L0_attn_wq"], np.float32)
    wk = np.asarray(inputs["L0_attn_wk"], np.float32)
    wv = np.asarray(inputs["L0_attn_wv"], np.float32)
    wo = np.asarray(inputs["L0_attn_wo"], np.float32)
    k0 = np.ascontiguousarray(np.asarray(inputs["L0_ffn_k0"]).view(np.uint8))
    k1 = np.ascontiguousarray(np.asarray(inputs["L0_ffn_k1"]).view(np.uint8))
    k2 = np.ascontiguousarray(np.asarray(inputs["L0_ffn_k2"]).view(np.uint8))
    sv = np.ascontiguousarray(np.tile(np.array(
        [np.float32(inputs["L0_ffn_s0"]), np.float32(inputs["L0_ffn_s1"]),
         np.float32(inputs["L0_ffn_s2"])], dtype=np.float32)[None, :], (P, 1)))
    inv = 1.0 / 10000.0 ** (np.arange(0, 64, 2, dtype=np.float32) / 64.0)
    ang = np.arange(S, dtype=np.float32)[:, None] * inv[None, :]
    cosr = np.repeat(np.cos(ang), 2, axis=1).astype(np.float32)
    sinr = np.repeat(np.sin(ang), 2, axis=1).astype(np.float32)
    rope = np.ascontiguousarray(np.concatenate([cosr, sinr], axis=1))
    woT = np.ascontiguousarray(wo.T)

    in_maps = []
    for c in range(8):
        b, g = c // 4, c % 4
        sl = slice(1024 * g, 1024 * (g + 1))
        in_maps.append({
            "x": np.ascontiguousarray(x[b]),
            "x_res": np.ascontiguousarray(x[b, sl, :]),
            "wq_sl": np.ascontiguousarray(wq[sl, :].T),
            "wk_sl": np.ascontiguousarray(wk[sl, :].T),
            "wv_sl": np.ascontiguousarray(wv[sl, :].T),
            "woT": woT,
            "k0": k0, "k1": k1, "k2": k2,
            "rope": rope, "svec": sv,
        })
    return in_maps


def kernel(**inputs):
    in_maps = _host_prep(inputs)
    nc = build_nc(upto=7)
    res = run_bass_kernel_spmd(nc, in_maps, core_ids=list(range(8)))
    out = np.empty((2, S, D), np.float32)
    for c in range(8):
        b, g = c // 4, c % 4
        out[b, 1024 * g:1024 * (g + 1), :] = res.results[c]["out"]
    return out
